# version 08
# baseline (speedup 1.0000x reference)
import contextlib
import numpy as np
import ml_dtypes
import concourse.bass as bass
import concourse.mybir as mybir
from concourse.bass_utils import run_bass_kernel_spmd

F32 = mybir.dt.float32
BF16 = mybir.dt.bfloat16
AF = mybir.ActivationFunctionType
ALU = mybir.AluOpType
AX = mybir.AxisListType
AP = bass.AP

P = 2048
HALO = 656
NT = P + HALO
D = 1024
NEG = -1.0e9
WIN = 512
NKV = (P + 640) // 128
NQ = (P + 128) // 128
KW = NKV * 128
RG = [[0, 1, 2, 3], [4, 5, 6, 7]]

DEBUG_TAPS = ()
H2N_GLU = None
DEV_SKIP = ()
PB_CHUNKS = 16
PB_STAGE = 4



def _make_consts():
    shared = {}
    shared["ident"] = np.eye(128, dtype=ml_dtypes.bfloat16)
    r = np.arange(128)
    up01 = (r[None, :] >= r[:, None])
    shared["tri_up01"] = up01.astype(ml_dtypes.bfloat16)
    lo01 = (r[None, :] <= r[:, None])
    shared["addm_lo"] = np.where(~lo01, np.float32(NEG), 0).astype(ml_dtypes.bfloat16)
    shared["addm_up"] = np.where(~up01, np.float32(NEG), 0).astype(ml_dtypes.bfloat16)
    iota = r.astype(np.float32)[:, None]
    shared["iota_col"] = iota
    shared["iota513"] = (iota + 513.0)

    per_core = []
    for c in range(8):
        b, p = c // 4, c % 4
        s = p * P
        d = {}
        d["halo_gate"] = np.full((128, 1), 0.0 if p == 0 else 1.0, np.float32)
        ms = np.zeros((128, NQ), np.float32)
        for qt in range(-1, 16):
            col = qt + 1
            st = iota[:, 0].copy()
            if p == 0 and 0 <= qt <= 3:
                st = np.maximum(st, 512 - 128 * qt)
            ms[:, col] = st
        d["s_mstart"] = ms
        lo01m = np.where(r[None, :] > r[:, None], np.float32(NEG), 0)
        up01m = np.where(r[None, :] < r[:, None], np.float32(NEG), 0)
        smask = np.zeros((5, 128, 640), np.float32)
        smask[:, :, 0:128] += up01m
        smask[:, :, 512:640] += lo01m
        if p == 0:
            for itt in range(5):
                cols = np.arange(640)
                bad = (cols + itt * 128 >= 128) & (cols + itt * 128 < 640)
                smask[itt][:, bad] = NEG
        d["smask"] = smask.astype(ml_dtypes.bfloat16)
        vc = np.ones((128, NKV), np.float32)
        if p == 0:
            vc[:, :5] = 0.0
        d["kv_valid"] = vc
        j = np.arange(128, dtype=np.float64)
        invf = 10000.0 ** (-(j / 128.0))
        t = np.arange(s, s + P, dtype=np.float64)
        ang = invf[:, None] * t[None, :]
        d["ropeq_cos"] = (np.cos(ang) / 16.0).astype(np.float32)
        d["ropeq_sin"] = (np.sin(ang) / 16.0).astype(np.float32)
        d["ropek_cos"] = np.cos(ang).astype(np.float32)
        d["ropek_sin"] = np.sin(ang).astype(np.float32)
        d["ropek_cos_tm"] = np.ascontiguousarray(np.cos(ang).T.astype(np.float32))
        d["ropek_sin_tm"] = np.ascontiguousarray(np.sin(ang).T.astype(np.float32))
        d["pgate"] = np.broadcast_to((np.arange(4) < p).astype(np.float32)[None, :], (128, 4)).copy()
        hi = np.arange(0, 64, 2, dtype=np.float64)
        invx = 10000.0 ** (-hi / 64.0)
        tt = np.arange(s - HALO, s + P, dtype=np.float64)
        fr = tt[:, None] * invx[None, :]
        base = (hi + 0.4 * 64.0) / (1.4 * 64.0)
        power = (tt - 8192 // 2) / 256.0
        scale = base[None, :] ** power[:, None]
        cos2 = np.concatenate([np.cos(fr)] * 2, 1)
        sin2 = np.concatenate([np.sin(fr)] * 2, 1)
        sc2 = np.concatenate([scale] * 2, 1)
        csq = cos2 * sc2 * 8.0
        ssq = sin2 * sc2 * 8.0
        csk = cos2 / sc2
        ssk = sin2 / sc2
        ssmq = np.concatenate([-ssq[:, 32:64], ssq[:, 0:32]], 1)
        ssmk = np.concatenate([-ssk[:, 32:64], ssk[:, 0:32]], 1)
        d["xpos"] = np.concatenate([csq, ssmq, csk, ssmk], 1).astype(np.float32)
        per_core.append(d)
    return {"shared": shared, "per_core": per_core}


def _stage_core(c, inp, consts):
    b, p = c // 4, c % 4
    s = p * P
    x0 = np.asarray(inp["L0_attn_hidden_states"])[b]
    x0c = np.zeros((NT, D), np.float32)
    lo = s - HALO
    x0c[max(0, -lo):] = x0[max(lo, 0): s + P]
    m = {"x0": np.ascontiguousarray(x0c)}
    for k, v in inp.items():
        if k == "L0_attn_hidden_states":
            continue
        m[k] = np.ascontiguousarray(np.asarray(v, np.float32))
    m["lnvec"] = np.stack([np.stack([m["L0_ffn_ln_g"][l], m["L0_ffn_ln_b"][l]]) for l in range(2)])
    b1 = m["L0_ffn_b1"]
    b1a = np.zeros((2, 22, 128), np.float32); b1g = np.zeros((2, 22, 128), np.float32)
    b1a.reshape(2, -1)[:, :2730] = b1[:, :2730]
    b1g.reshape(2, -1)[:, :2730] = b1[:, 2730:]
    m["b1a_cols"] = np.ascontiguousarray(b1a.transpose(0, 2, 1))
    m["b1g_cols"] = np.ascontiguousarray(b1g.transpose(0, 2, 1))
    m["b2_cols"] = np.ascontiguousarray(m["L0_ffn_b2"].reshape(2, 8, 128).transpose(0, 2, 1))
    m["gn_rep"] = np.ascontiguousarray(np.broadcast_to(m["L0_attn_gnorm_w"][None, :], (128, 256)))
    m["qs_rep"] = np.ascontiguousarray(np.broadcast_to(m["L0_ffn_q_scale"][:, None, :], (2, 128, 64)))
    m["ks_rep"] = np.ascontiguousarray(np.broadcast_to(m["L0_ffn_k_scale"][:, None, :], (2, 128, 64)))
    w1 = m["L0_ffn_w1"]
    w1b = np.zeros((2, 2, 22, 1024, 128), np.float32)
    for l in range(2):
        for half in range(2):
            colbase = 2730 * half
            for j in range(22):
                lo2 = colbase + j * 128
                hi2 = min(colbase + 2730, lo2 + 128)
                w1b[l, half, j, :, 0:hi2 - lo2] = w1[l][:, lo2:hi2]
    m["w1_blk"] = w1b
    w2 = m["L0_ffn_w2"]
    w2b = np.zeros((2, 22, 128, 1024), np.float32)
    for l in range(2):
        for j in range(22):
            lo2 = j * 128
            hi2 = min(2730, lo2 + 128)
            w2b[l, j, 0:hi2 - lo2, :] = w2[l][lo2:hi2, :]
    m["w2_blk"] = w2b
    m["skw2aug"] = np.concatenate([m["L0_attn_sk_w2"], m["L0_attn_sk_b2"][None]], 0)
    m["svw2aug"] = np.concatenate([m["L0_attn_sv_w2"], m["L0_attn_sv_b2"][None]], 0)
    m.update(consts["shared"])
    m.update(consts["per_core"][c])
    return m



class Sched:
    def __init__(self):
        self.n = {}

    def inc(self, ins, sem, k=1):
        ins.then_inc(sem, k)
        self.n[sem.name] = self.n.get(sem.name, 0) + k
        return self.n[sem.name]

    def bump(self, sem, k):
        self.n[sem.name] = self.n.get(sem.name, 0) + k
        return self.n[sem.name]

    def val(self, sem):
        return self.n.get(sem.name, 0)

    def raw(self, e, ins, sem):
        self.inc(ins, sem, 1)
        e.wait_ge(sem, self.val(sem))


class Bld:
    pass


def _barrier(B):
    B.phase += 1
    tgt = B.phase
    pairs = [("sync", B.phS), ("tensor", B.phT), ("vector", B.phV),
             ("scalar", B.phA), ("gpsimd", B.phG)]
    for name, sem in pairs:
        def _mk(sm):
            def _f(e):
                B.sched.inc(e.nop(), sm, 1)
                for _n2, s2 in pairs:
                    e.wait_ge(s2, tgt)
            return _f
        getattr(B.block, name)(_mk(sem))


def build_program(taps=()):
    es = contextlib.ExitStack()
    nc = bass.Bass("TRN2", target_bir_lowering=False)
    B = Bld()
    B.nc, B.es, B.sched = nc, es, Sched()
    B.phase = 0
    B.taps = set(taps)
    B.tapped = {}

    dt_ = {}
    def din(name, shape, dt=F32):
        dt_[name] = nc.dram_tensor(name, list(shape), dt, kind="ExternalInput")
    din("x0", (NT, D))
    for nm, sh in [("L0_attn_Wq", (D, D)), ("L0_attn_Wk", (D, D)), ("L0_attn_Wv", (D, D)),
                   ("L0_attn_Wg", (D, D)), ("L0_attn_Wo", (D, D)),
                   ("L0_attn_sk_w1", (D, 16)), ("L0_attn_sk_w2", (16, 256)), ("L0_attn_sk_b2", (256,)),
                   ("L0_attn_sv_w1", (D, 16)), ("L0_attn_sv_w2", (16, 256)), ("L0_attn_sv_b2", (256,)),
                   ("L0_attn_gnorm_w", (256,)),
                   ("L0_ffn_wqkv", (2, D, 1536)), ("L0_ffn_q_scale", (2, 64)), ("L0_ffn_k_scale", (2, 64)),
                   ("L0_ffn_wo", (2, 512, D)), ("L0_ffn_ln_g", (2, D)), ("L0_ffn_ln_b", (2, D)),
                   ("L0_ffn_w1", (2, D, 5460)), ("L0_ffn_b1", (2, 5460)),
                   ("L0_ffn_w2", (2, 2730, D)), ("L0_ffn_b2", (2, D)),
                   ("w1_blk", (2, 2, 22, 1024, 128)), ("w2_blk", (2, 22, 128, 1024)),
                   ("lnvec", (2, 2, D)), ("b1a_cols", (2, 128, 22)), ("b1g_cols", (2, 128, 22)),
                   ("b2_cols", (2, 128, 8)),
                   ("qs_rep", (2, 128, 64)), ("ks_rep", (2, 128, 64)),
                   ("skw2aug", (17, 256)), ("svw2aug", (17, 256)),
                   ("iota_col", (128, 1)), ("iota513", (128, 1)),
                   ("halo_gate", (128, 1)), ("s_mstart", (128, NQ)), ("kv_valid", (128, NKV)),
                   ("ropeq_cos", (128, P)), ("ropeq_sin", (128, P)),
                   ("ropek_cos", (128, P)), ("ropek_sin", (128, P)),
                   ("ropek_cos_tm", (P, 128)), ("ropek_sin_tm", (P, 128)),
                   ("pgate", (128, 4)), ("gn_rep", (128, 256)), ("xpos", (NT, 256))]:
        din(nm, sh)
    din("ident", (128, 128), BF16)
    din("tri_up01", (128, 128), BF16)
    din("addm_lo", (128, 128), BF16)
    din("addm_up", (128, 128), BF16)
    din("smask", (5, 128, 640), BF16)
    B.inp = dt_
    B.out = nc.dram_tensor("out", [P, D], F32, kind="ExternalOutput")

    B.ag1_in = nc.dram_tensor("ag1_in", [128, 2560], F32)
    B.kv_d = nc.dram_tensor("kv_d", [128, 16 * 2560], BF16)
    B.ag1_out = nc.dram_tensor("ag1_out", [512, 2560], F32)
    B.ag2_in = nc.dram_tensor("ag2_in", [128, 8 * HALO], F32)
    B.ag2_out = nc.dram_tensor("ag2_out", [512, 8 * HALO], F32)
    B.og_d = nc.dram_tensor("og_d", [128, 8 * P], BF16,
                            kind="ExternalOutput" if "og" in taps else "Internal")
    if "og" in taps:
        B.tapped["og"] = True

    def sem(name):
        return es.enter_context(nc.semaphore(name))
    for nm in ["dS", "dG", "dA", "tT", "tV", "tA", "tG", "cc", "vZ", "aZ", "gZ",
               "phS", "phT", "phV", "phA", "phG", "e1", "e2", "e3", "e4", "e5", "e6",
               "xwS", "vF", "vQ", "vY", "vN", "vC", "vR", "vE", "vM",
               "aP", "aK", "aN", "aE", "gK", "aSq", "vSum"]:
        setattr(B, nm, sem(nm))

    sb = nc.alloc_sbuf_tensor
    B.stream = sb("stream", [128, 8 * NT], F32)
    B.ident = sb("ident_sb", [128, 128], BF16)
    B.tri_up = sb("tri_up_sb", [128, 128], BF16)
    B.addm_lo = sb("addm_lo_sb", [128, 128], BF16)
    B.addm_up = sb("addm_up_sb", [128, 128], BF16)
    B.identf = sb("identf_sb", [128, 128], F32)
    B.ones1 = sb("ones1", [1, 512], BF16)
    B.onescol = sb("onescol", [128, 1], BF16)
    B.onescolf = sb("onescolf", [128, 1], F32)
    B.iota = sb("iota_sb", [128, 1], F32)
    B.iota513 = sb("iota513_sb", [128, 1], F32)
    B.hgate = sb("hgate", [128, 1], F32)
    B.smstart = sb("smstart", [128, NQ], F32)
    B.kvvalid = sb("kvvalid", [128, NKV], F32)
    B.kvvalid_bf = sb("kvvalid_bf", [128, NKV], BF16)
    B.cst = sb("cst", [128, 4], F32)

    with nc.Block() as block:
        B.block = block

        @block.sync
        def _(e):
            for dst, src in [(B.ident, dt_["ident"]), (B.tri_up, dt_["tri_up01"]),
                             (B.addm_lo, dt_["addm_lo"]), (B.addm_up, dt_["addm_up"]),
                             (B.iota, dt_["iota_col"]), (B.iota513, dt_["iota513"]),
                             (B.hgate, dt_["halo_gate"]), (B.smstart, dt_["s_mstart"]),
                             (B.kvvalid, dt_["kv_valid"])]:
                B.sched.inc(e.dma_start(out=dst[:, :], in_=src[:, :]), B.dS, 16)


        @block.vector
        def _(e):
            e.wait_ge(B.dS, B.sched.val(B.dS))
            e.memset(B.ones1[:, :], 1.0)
            e.memset(B.onescol[:, :], 1.0)
            e.memset(B.onescolf[:, :], 1.0)
            e.memset(B.cst[:, 0:1], 1e-5)
            e.tensor_copy(B.identf[:, :], B.ident[:, :])
            e.tensor_copy(B.kvvalid_bf[:, :], B.kvvalid[:, :])

        _barrier(B)
        phase_xn(B)
        _barrier(B)
        if "passA" not in DEV_SKIP:
            phase_passA(B)
        _barrier(B)
        for lyr in (0, 1):
            o0stack = contextlib.ExitStack()
            B.o0 = o0stack.enter_context(nc.sbuf_tensor(f"o0_{lyr}", [128, 4 * NQ * 128], BF16, side="right"))
            phase_ffn_subA(B, lyr=lyr)
            _barrier(B)
            if lyr == 0 and "passB" not in DEV_SKIP:
                phase_passB(B)
                _barrier(B)
            phase_ffn_rest(B, lyr=lyr, o0stack=o0stack)
            _barrier(B)
        phase_final(B)

    es.close()
    return B


def _tap_sbuf(B, name, ap, shape, dtype=F32):
    if name not in B.taps:
        return
    t = B.nc.dram_tensor("tap_" + name, list(shape), dtype, kind="ExternalOutput")
    B.tapped[name] = True
    _barrier(B)

    @B.block.sync
    def _(e):
        B.sched.inc(e.dma_start(out=t[:, :], in_=ap), B.dS, 16)
        e.wait_ge(B.dS, B.sched.val(B.dS))
    _barrier(B)



def phase_xn(B):
    nc, S = B.nc, B.sched
    with contextlib.ExitStack() as ph:
        x0t = [ph.enter_context(nc.sbuf_tensor(f"x0t{i}", [128, D], F32)) for i in range(2)]
        xnt = [ph.enter_context(nc.sbuf_tensor(f"xnt{i}", [128, D], F32)) for i in range(2)]
        scr = ph.enter_context(nc.sbuf_tensor("xn_scr", [128, D], BF16))
        st_ = ph.enter_context(nc.sbuf_tensor("xn_stat", [128, 16], F32))
        scr2 = ph.enter_context(nc.sbuf_tensor("xn_scr2", [128, D], F32))
        tp = [ph.enter_context(nc.psum_tensor(f"xn_ps{i}", [128, 512], F32)) for i in range(2)]

        starts = [i * 128 for i in range(NT // 128)] + ([NT - 128] if NT % 128 else [])
        NTI = len(starts)
        b_dS, b_tV, b_tA, b_tT, b_e1, b_e2 = (S.val(x) for x in (B.dS, B.tV, B.tA, B.tT, B.e1, B.e2))

        @B.block.sync
        def _(e):
            for i in range(NTI):
                if i >= 2:
                    e.wait_ge(B.tV, b_tV + (i - 2) + 1)
                S.inc(e.dma_start(out=x0t[i % 2][:, :],
                                  in_=B.inp["x0"][starts[i]:starts[i] + 128, :]), B.dS, 16)

        def stp(i):
            return st_[:, (i % 2) * 8:(i % 2) * 8 + 8]

        def _evac(e, i):
            for j in range(8):
                g = i * 8 + j
                e.wait_ge(B.tT, b_tT + g + 1)
                S.inc(e.copy(B.stream[:, j * NT + starts[i]: j * NT + starts[i] + 128],
                             tp[g % 2][:, 0:128]), B.e2, 1)

        @B.block.scalar
        def _(e):
            for i in range(NTI):
                e.wait_ge(B.dS, b_dS + (i + 1) * 16)
                st = stp(i)
                S.inc(e.activation(scr[:, :], x0t[i % 2][:, :], AF.Square,
                                   accum_out=st[:, 1:2]), B.tA, 1)
                e.wait_ge(B.e1, b_e1 + i + 1)
                S.inc(e.activation(st[:, 5:6], st[:, 4:5], AF.Sqrt, bias=B.cst[:, 0:1]), B.tA, 1)
                if i >= 1:
                    _evac(e, i - 1)
            _evac(e, NTI - 1)

        @B.block.vector
        def _(e):
            for i in range(NTI):
                st = stp(i)
                e.wait_ge(B.dS, b_dS + (i + 1) * 16)
                S.raw(e, e.tensor_reduce(st[:, 0:1], x0t[i % 2][:, :], AX.X, ALU.add), B.vZ)
                e.wait_ge(B.tA, b_tA + 2 * i + 1)
                e.tensor_scalar(st[:, 2:3], st[:, 0:1], 1.0 / D, None, ALU.mult)
                S.raw(e, e.tensor_scalar(st[:, 3:4], st[:, 1:2], 1.0 / D, None, ALU.mult), B.vZ)
                S.raw(e, e.tensor_tensor(st[:, 7:8], st[:, 2:3], st[:, 2:3], ALU.mult), B.vZ)
                S.inc(e.tensor_tensor(st[:, 4:5], st[:, 3:4], st[:, 7:8], ALU.subtract), B.e1, 1)
                e.wait_ge(B.tA, b_tA + 2 * i + 2)
                S.raw(e, e.reciprocal(st[:, 6:7], st[:, 5:6]), B.vZ)
                if i >= 2:
                    e.wait_ge(B.tT, b_tT + (i - 2) * 8 + 8)
                S.raw(e, e.tensor_scalar(scr2[:, :], x0t[i % 2][:, :], st[:, 2:3], None, ALU.subtract), B.vZ)
                S.inc(e.tensor_scalar(xnt[i % 2][:, :], scr2[:, :], st[:, 6:7], None, ALU.mult), B.tV, 1)

        @B.block.tensor
        def _(e):
            for i in range(NTI):
                e.wait_ge(B.tV, b_tV + i + 1)
                for j in range(8):
                    g = i * 8 + j
                    if g >= 2:
                        e.wait_ge(B.e2, b_e2 + g - 1)
                    S.inc(e.transpose(tp[g % 2][:, 0:128], xnt[i % 2][:, j * 128:(j + 1) * 128],
                                      B.identf[:, :]), B.tT, 1)

    _tap_sbuf(B, "st", st_[:, :], (128, 16))
    _tap_sbuf(B, "xn", B.stream[:, 0:8 * NT], (128, 8 * NT))



def phase_passA(B):
    nc, S = B.nc, B.sched
    with contextlib.ExitStack() as ph:
        sbt = lambda n, sh, dt=F32: ph.enter_context(nc.sbuf_tensor(f"{n}_pa", sh, dt))
        wkb = sbt("wkb", [128, 8 * 1024], BF16)
        wvb = sbt("wvb", [128, 8 * 1024], BF16)
        skw1 = sbt("skw1", [128, 256], BF16)
        skw2 = sbt("skw2", [17, 512], BF16)
        xnb = sbt("xnb", [128, 8 * 512], BF16)
        rct = sbt("rct", [128, 4 * 256])
        ka = sbt("ka", [128, 512])
        kb = sbt("kb", [128, 512])
        khat = sbt("khat", [128, 1024], BF16)
        vtm = sbt("vtm", [128, 1024], BF16)
        t1aug = sbt("t1aug", [17, 1024], BF16)
        scf = sbt("scf", [128, 512])
        wkv = sbt("wkv", [128, 512], BF16)
        s1A = [sbt(f"s1A{i}", [128, 1024]) for i in range(2)]
        s2A = [sbt(f"s2A{i}", [64, 1024]) for i in range(2)]
        zA = [sbt(f"zA{i}", [1, 512]) for i in range(2)]
        onesbf = sbt("onesbf", [128, 1], BF16)
        pj = ph.enter_context(nc.psum_tensor("pj_pa", [128, 1024], F32))
        skbank = ph.enter_context(nc.psum_tensor("skbank_pa", [128, 512], F32))
        st1 = ph.enter_context(nc.psum_tensor("st1_pa", [128, 1024], F32))
        st2 = ph.enter_context(nc.psum_tensor("st2_pa", [64, 1024], F32))
        stz = ph.enter_context(nc.psum_tensor("stz_pa", [1, 512], F32))

        b = {k: S.val(getattr(B, k)) for k in ["dS", "dG", "tT", "vF", "vQ", "vN", "aP", "cc"]}
        mm = {}

        @B.block.gpsimd
        def _(e):
            for fc in range(8):
                S.inc(e.dma_start(out=wkb[:, fc * 1024:(fc + 1) * 1024],
                                  in_=B.inp["L0_attn_Wk"][fc * 128:(fc + 1) * 128, :]), B.dG, 16)
                S.inc(e.dma_start(out=wvb[:, fc * 1024:(fc + 1) * 1024],
                                  in_=B.inp["L0_attn_Wv"][fc * 128:(fc + 1) * 128, :]), B.dG, 16)
            S.inc(e.dma_start(out=skw1[:, 0:128],
                              in_=AP(B.inp["L0_attn_sk_w1"], 0, [[16, 128], [2048, 8], [1, 16]])), B.dG, 16)
            S.inc(e.dma_start(out=skw1[:, 128:256],
                              in_=AP(B.inp["L0_attn_sv_w1"], 0, [[16, 128], [2048, 8], [1, 16]])), B.dG, 16)
            S.inc(e.dma_start(out=skw2[:, 0:256], in_=B.inp["skw2aug"][:, :]), B.dG, 16)
            S.inc(e.dma_start(out=skw2[:, 256:512], in_=B.inp["svw2aug"][:, :]), B.dG, 16)

        @B.block.vector
        def _(e):
            e.memset(onesbf[:, :], 1.0)
            e.memset(s1A[1][:, :], 0.0)
            e.memset(s2A[1][0:64, :], 0.0)
            e.memset(zA[1][0:1, :], 0.0)
            mm["init"] = S.inc(e.memset(t1aug[0:17, :], 1.0), B.vF, 1)

        for st in range(4):
            @B.block.sync
            def _(e, st=st):
                if st > 0:
                    e.wait_ge(B.vF, mm["rope", st * 4 - 1])
                for c in range(4):
                    r0 = st * 512 + c * 128
                    S.inc(e.dma_start(out=rct[:, c * 256:c * 256 + 128],
                                      in_=B.inp["ropek_cos_tm"][r0:r0 + 128, :]), B.dS, 16)
                    S.inc(e.dma_start(out=rct[:, c * 256 + 128:c * 256 + 256],
                                      in_=B.inp["ropek_sin_tm"][r0:r0 + 128, :]), B.dS, 16)
                mm["rctd", st] = S.val(B.dS)

            @B.block.vector
            def _(e, st=st):
                if st > 0:
                    e.wait_ge(B.tT, mm["stu", st * 4 - 1])
                for ci in range(8):
                    e.tensor_copy(xnb[:, ci * 512:(ci + 1) * 512],
                                  B.stream[:, ci * NT + HALO + st * 512: ci * NT + HALO + st * 512 + 512])
                mm["xnb", st] = S.inc(e.engine_nop(), B.vQ, 1)

            @B.block.tensor
            def _(e, st=st):
                e.wait_ge(B.vQ, mm["xnb", st])
                if st == 0:
                    e.wait_ge(B.dG, b["dG"] + 20 * 16)
                    e.wait_ge(B.vF, mm["init"])
                else:
                    e.wait_ge(B.vF, mm["clip", st * 4 - 1])
                ins = None
                for fc in range(8):
                    ins = e.matmul(skbank[0:16, 0:512], skw1[:, fc * 16:(fc + 1) * 16],
                                   xnb[:, fc * 512:(fc + 1) * 512], start=(fc == 0), stop=(fc == 7))
                mm["t1", st, 0] = S.inc(ins, B.tT, 1)

            @B.block.vector
            def _(e, st=st):
                e.wait_ge(B.tT, mm["t1", st, 0])
                mm["t1e", st, 0] = S.inc(
                    e.tensor_copy(t1aug[0:16, 0:512], skbank[0:16, 0:512]), B.vF, 1)

            @B.block.tensor
            def _(e, st=st):
                e.wait_ge(B.vF, mm["t1e", st, 0])
                ins = None
                for fc in range(8):
                    ins = e.matmul(skbank[0:16, 0:512], skw1[:, 128 + fc * 16: 128 + (fc + 1) * 16],
                                   xnb[:, fc * 512:(fc + 1) * 512], start=(fc == 0), stop=(fc == 7))
                mm["t1", st, 1] = S.inc(ins, B.tT, 1)

            @B.block.vector
            def _(e, st=st):
                e.wait_ge(B.tT, mm["t1", st, 1])
                mm["t1e", st, 1] = S.inc(
                    e.tensor_copy(t1aug[0:16, 512:1024], skbank[0:16, 0:512]), B.vF, 1)

            for c in range(4):
                g = st * 4 + c

                @B.block.tensor
                def _(e, st=st, c=c, g=g):
                    if g > 0:
                        e.wait_ge(B.vF, mm["vtme", g - 1])
                    ins = None
                    for fc in range(8):
                        for hb in range(2):
                            ins = e.matmul(pj[:, hb * 512:(hb + 1) * 512],
                                           xnb[:, fc * 512 + c * 128: fc * 512 + (c + 1) * 128],
                                           wkb[:, fc * 1024 + hb * 512: fc * 1024 + (hb + 1) * 512],
                                           start=(fc == 0), stop=(fc == 7))
                    mm["kp", g] = S.inc(ins, B.tT, 1)
                    e.wait_ge(B.vF, mm["t1e", st, 1])
                    if g > 0:
                        e.wait_ge(B.vF, mm["clip", g - 1])
                    e.matmul(skbank[:, 0:256], t1aug[:, c * 128:(c + 1) * 128], skw2[:, 0:256],
                             start=True, stop=True)
                    ins = e.matmul(skbank[:, 256:512], t1aug[:, 512 + c * 128: 512 + (c + 1) * 128],
                                   skw2[:, 256:512], start=True, stop=True)
                    mm["sk", g] = S.inc(ins, B.tT, 1)

                @B.block.vector
                def _(e, st=st, c=c, g=g):
                    e.wait_ge(B.tT, mm["kp", g])
                    e.wait_ge(B.dS, mm["rctd", st])
                    cosb = rct[:, c * 256:c * 256 + 128].unsqueeze(1).broadcast_to([128, 4, 128])
                    sinb = rct[:, c * 256 + 128:c * 256 + 256].unsqueeze(1).broadcast_to([128, 4, 128])
                    pjv = pj[:, :].rearrange("p (h two j) -> p h two j", two=2, j=128)
                    khv = khat[:, :].rearrange("p (h two j) -> p h two j", two=2, j=128)
                    kav = ka[:, :].rearrange("p (h j) -> p h j", j=128)
                    kbv = kb[:, :].rearrange("p (h j) -> p h j", j=128)
                    if g > 0:
                        e.wait_ge(B.tT, mm["stu", g - 1])
                        e.wait_ge(B.dS, mm["spill", g - 1])
                    e.tensor_tensor(kav, pjv[:, :, 0, :], cosb, ALU.mult)
                    S.raw(e, e.tensor_tensor(kbv, pjv[:, :, 1, :], sinb, ALU.mult), B.vZ)
                    e.tensor_tensor(khv[:, :, 0, :], kav, kbv, ALU.subtract)
                    e.tensor_tensor(kav, pjv[:, :, 1, :], cosb, ALU.mult)
                    S.raw(e, e.tensor_tensor(kbv, pjv[:, :, 0, :], sinb, ALU.mult), B.vZ)
                    mm["rope", g] = S.inc(e.tensor_add(khv[:, :, 1, :], kav, kbv), B.vF, 1)
                    e.wait_ge(B.tT, mm["sk", g])
                    mm["clip", g] = S.inc(
                        e.tensor_scalar(scf[:, :], skbank[:, :], -32.0, 32.0, ALU.max, ALU.min), B.vF, 1)

                @B.block.scalar
                def _(e, g=g):
                    e.wait_ge(B.vF, mm["clip", g])
                    if g > 0:
                        e.wait_ge(B.tT, mm["stu", g - 1])
                        e.wait_ge(B.dS, mm["spill", g - 1])
                    mm["exp", g] = S.inc(e.activation(wkv[:, :], scf[:, :], AF.Exp), B.aP, 1)

                @B.block.tensor
                def _(e, st=st, c=c, g=g):
                    e.wait_ge(B.vF, mm["rope", g])
                    ins = None
                    for fc in range(8):
                        for hb in range(2):
                            ins = e.matmul(pj[:, hb * 512:(hb + 1) * 512],
                                           xnb[:, fc * 512 + c * 128: fc * 512 + (c + 1) * 128],
                                           wvb[:, fc * 1024 + hb * 512: fc * 1024 + (hb + 1) * 512],
                                           start=(fc == 0), stop=(fc == 7))
                    mm["vp", g] = S.inc(ins, B.tT, 1)

                @B.block.vector
                def _(e, g=g):
                    e.wait_ge(B.tT, mm["vp", g])
                    if g > 0:
                        e.wait_ge(B.dS, mm["spill", g - 1])
                    mm["vtme", g] = S.inc(e.tensor_copy(vtm[:, :], pj[:, :]), B.vF, 1)

                @B.block.tensor
                def _(e, g=g):
                    e.wait_ge(B.vF, mm["vtme", g])
                    e.wait_ge(B.aP, mm["exp", g])
                    if g > 0:
                        e.wait_ge(B.vF, mm["sacc", g - 1])
                    ins = None
                    for h in range(4):
                        for c2 in range(2):
                            e.matmul(st1[:, (2 * h + c2) * 64:(2 * h + c2) * 64 + 64],
                                     khat[:, h * 256 + c2 * 128: h * 256 + (c2 + 1) * 128],
                                     wkv[:, h * 64:(h + 1) * 64], start=True, stop=True)
                        e.matmul(st2[0:64, h * 256:(h + 1) * 256], wkv[:, 256 + h * 64: 256 + (h + 1) * 64],
                                 vtm[:, h * 256:(h + 1) * 256], start=True, stop=True)
                        e.matmul(stz[0:1, h * 64:(h + 1) * 64], onesbf[:, :], wkv[:, h * 64:(h + 1) * 64],
                                 start=True, stop=True)
                        ins = e.matmul(stz[0:1, 256 + h * 64: 256 + (h + 1) * 64], onesbf[:, :],
                                       wkv[:, 256 + h * 64: 256 + (h + 1) * 64],
                                       start=True, stop=True)
                    mm["stu", g] = S.inc(ins, B.tT, 1)

                @B.block.vector
                def _(e, g=g):
                    e.wait_ge(B.tT, mm["stu", g])
                    e.tensor_tensor(s1A[g % 2][:, :], st1[:, :], s1A[(g + 1) % 2][:, :], ALU.add)
                    e.tensor_tensor(s2A[g % 2][0:64, :], st2[0:64, :], s2A[(g + 1) % 2][0:64, :], ALU.add)
                    mm["sacc", g] = S.inc(
                        e.tensor_tensor(zA[g % 2][0:1, :], stz[0:1, :], zA[(g + 1) % 2][0:1, :],
                                        ALU.add), B.vF, 1)

                @B.block.sync
                def _(e, g=g):
                    e.wait_ge(B.vF, mm["vtme", g])
                    e.wait_ge(B.vF, mm["rope", g])
                    e.wait_ge(B.aP, mm["exp", g])
                    S.inc(e.dma_start(out=B.kv_d[:, g * 2560: g * 2560 + 1024], in_=khat[:, :]),
                          B.dS, 16)
                    S.inc(e.dma_start(out=B.kv_d[:, g * 2560 + 1024: g * 2560 + 2048], in_=vtm[:, :]),
                          B.dS, 16)
                    S.inc(e.dma_start(out=B.kv_d[:, g * 2560 + 2048: g * 2560 + 2560], in_=wkv[:, :]),
                          B.dS, 16)
                    mm["spill", g] = S.val(B.dS)

        @B.block.sync
        def _(e):
            e.wait_ge(B.vF, mm["sacc", 15])
            S.inc(e.dma_start(out=B.ag1_in[:, 0:1024], in_=s1A[1][:, :]), B.dS, 16)
            S.inc(e.dma_start(out=B.ag1_in[0:64, 1024:2048], in_=s2A[1][0:64, :]), B.dS, 16)
            S.inc(e.dma_start(out=B.ag1_in[0:1, 2048:2560], in_=zA[1][0:1, :]), B.dS, 16)

        @B.block.gpsimd
        def _(e):
            e.wait_ge(B.dS, S.val(B.dS))
            S.inc(e.collective_compute("AllGather", ALU.bypass, replica_groups=RG,
                                       ins=[B.ag1_in.ap()], outs=[B.ag1_out.ap()]), B.cc, 1)


def phase_passB(B):
    nc, S = B.nc, B.sched
    mm = {}
    phx = contextlib.ExitStack()
    sbx = lambda n, sh, dt=F32: phx.enter_context(nc.sbuf_tensor(f"{n}_pbx", sh, dt))
    s1f = [sbx(f"s1f{i}", [128, 1024]) for i in range(2)]
    s2f = [sbx(f"s2f{i}", [64, 1024]) for i in range(2)]
    zf = [sbx(f"zf{i}", [1, 512]) for i in range(2)]
    s1sb = sbx("s1sb", [128, 1024], BF16)
    s2sb = sbx("s2sb", [64, 1024], BF16)
    zsbb = sbx("zsbb", [1, 512], BF16)
    with contextlib.ExitStack() as ph0:
        acc = [ph0.enter_context(nc.sbuf_tensor(f"pacc{i}", [128, 2560], F32)) for i in range(3)]
        pg = ph0.enter_context(nc.sbuf_tensor("pg_sb", [128, 4], F32))

        @B.block.gpsimd
        def _(e):
            e.wait_ge(B.cc, S.val(B.cc))
            S.inc(e.dma_start(out=pg[:, :], in_=B.inp["pgate"][:, :]), B.dG, 16)
            for r in range(3):
                S.inc(e.dma_start(out=acc[r][:, :], in_=B.ag1_out[r * 128:(r + 1) * 128, :]), B.dG, 16)

        @B.block.vector
        def _(e):
            e.wait_ge(B.dG, S.val(B.dG))
            e.tensor_scalar(s1f[1][:, :], acc[0][:, 0:1024], pg[:, 0:1], None, ALU.mult)
            e.tensor_scalar(s2f[1][0:64, :], acc[0][0:64, 1024:2048], pg[0:64, 0:1], None, ALU.mult)
            S.raw(e, e.tensor_scalar(zf[1][0:1, :], acc[0][0:1, 2048:2560], pg[0:1, 0:1],
                                     None, ALU.mult), B.vZ)
            e.scalar_tensor_tensor(s1f[0][:, :], acc[1][:, 0:1024], pg[:, 1:2], s1f[1][:, :],
                                   ALU.mult, ALU.add)
            e.scalar_tensor_tensor(s2f[0][0:64, :], acc[1][0:64, 1024:2048], pg[0:64, 1:2],
                                   s2f[1][0:64, :], ALU.mult, ALU.add)
            S.raw(e, e.scalar_tensor_tensor(zf[0][0:1, :], acc[1][0:1, 2048:2560], pg[0:1, 1:2],
                                            zf[1][0:1, :], ALU.mult, ALU.add), B.vZ)
            e.scalar_tensor_tensor(s1f[1][:, :], acc[2][:, 0:1024], pg[:, 2:3], s1f[0][:, :],
                                   ALU.mult, ALU.add)
            e.scalar_tensor_tensor(s2f[1][0:64, :], acc[2][0:64, 1024:2048], pg[0:64, 2:3],
                                   s2f[0][0:64, :], ALU.mult, ALU.add)
            S.raw(e, e.scalar_tensor_tensor(zf[1][0:1, :], acc[2][0:1, 2048:2560], pg[0:1, 2:3],
                                            zf[0][0:1, :], ALU.mult, ALU.add), B.vZ)
            e.tensor_copy(s1sb[:, :], s1f[1][:, :])
            e.tensor_copy(s2sb[0:64, :], s2f[1][0:64, :])
            mm["sinit"] = S.inc(e.tensor_copy(zsbb[0:1, :], zf[1][0:1, :]), B.vF, 1)

    _barrier(B)

    with contextlib.ExitStack() as ph:
        sbt = lambda n, sh, dt=F32: ph.enter_context(nc.sbuf_tensor(f"{n}_pb", sh, dt))
        wqb = sbt("wqb", [128, 8 * 1024], BF16)
        wgb = sbt("wgb", [128, 8 * 1024], BF16)
        xnb = sbt("xnb", [128, 8 * 512], BF16)
        rpt = [sbt(f"rpt{i}", [128, 256], F32) for i in range(2)]
        qf = sbt("qf", [128, 8 * 128], BF16)
        kf = sbt("kf", [128, 8 * 128], BF16)
        qfb = qf[:, :].bitcast(F32)
        kfb = kf[:, :].bitcast(F32)
        gtm = sbt("gtm", [128, 1024], BF16)
        ld_ = [sbt(f"ldkv{i}", [128, 2560], BF16) for i in range(2)]
        scf = sbt("scf", [128, 512])
        wvT = sbt("wvT", [64, 512], BF16)
        stm = sbt("stm", [128, 512], BF16)
        ptbi = sbt("ptbi", [128, 256], BF16)
        atm = sbt("atm", [128, 512], BF16)
        rz = sbt("rz", [128, 512])
        lg = sbt("lg", [128, 256])
        ev = sbt("ev", [128, 256])
        nmx = sbt("nmx", [128, 12])
        ptb = sbt("ptb", [64, 512], BF16)
        hv1 = sbt("hv1", [128, 512])
        hv2 = sbt("hv2", [128, 512])
        ssq = sbt("ssq", [128, 16])
        gnr = sbt("gnr", [128, 256], BF16)
        ogsb = sbt("ogsb", [128, 1024], BF16)
        ogfm = sbt("ogfm", [128, 1024], BF16)
        onesbf = sbt("onesbf2", [128, 1], BF16)
        pq2 = ph.enter_context(nc.psum_tensor("pq2_pb", [128, 1024], F32))
        pk2 = ph.enter_context(nc.psum_tensor("pk2_pb", [128, 1024], F32))
        st1 = ph.enter_context(nc.psum_tensor("st1_pb", [128, 1024], F32))
        st2 = ph.enter_context(nc.psum_tensor("st2_pb", [64, 1024], F32))
        pq2b = pq2[:, :].bitcast(BF16)
        pk2b = pk2[:, :].bitcast(BF16)

        b = {k: S.val(getattr(B, k)) for k in ["dS", "dG", "tT", "vF", "vQ", "aP", "cc"]}

        @B.block.gpsimd
        def _(e):
            for fc in range(8):
                for w_, nmW in ((wqb, "L0_attn_Wq"), (wgb, "L0_attn_Wg")):
                    S.inc(e.dma_start(out=w_[:, fc * 1024:(fc + 1) * 1024],
                                      in_=B.inp[nmW][fc * 128:(fc + 1) * 128, :]), B.dG, 16)
            S.inc(e.dma_start(out=gnr[:, :], in_=B.inp["gn_rep"][:, :]), B.dG, 16)
            mm["wload"] = S.val(B.dG)

        @B.block.vector
        def _(e):
            mm["minit"] = S.inc(e.memset(onesbf[:, :], 1.0), B.vF, 1)

        for st4 in range(4):
            if st4 * 4 >= PB_CHUNKS:
                break

            @B.block.scalar
            def _(e, st4=st4):
                if st4 > 0:
                    e.wait_ge(B.tT, mm["stu", st4 * 4 - 1])
                ins = None
                for ci in range(8):
                    ins = e.copy(xnb[:, ci * 512:(ci + 1) * 512],
                                 B.stream[:, ci * NT + HALO + st4 * 512: ci * NT + HALO + st4 * 512 + 512])
                mm["xnb", st4] = S.inc(ins, B.vQ, 1)

            for c in range(4):
                g = st4 * 4 + c
                if g >= PB_CHUNKS:
                    break

                @B.block.sync
                def _(e, st4=st4, c=c, g=g):
                    if g >= 2:
                        e.wait_ge(B.vF, mm["qrope", g - 2, 3])
                    t0 = st4 * 512 + c * 128
                    for i, nmT in enumerate(["ropeq_cos", "ropeq_sin"]):
                        S.inc(e.dma_start(out=rpt[g % 2][:, i * 128:(i + 1) * 128],
                                          in_=B.inp[nmT][:, t0:t0 + 128]), B.dS, 16)
                    mm["rpt", g] = S.val(B.dS)

                @B.block.gpsimd
                def _(e, g=g):
                    if g >= 2:
                        e.wait_ge(B.tT, mm["scend", g - 2])
                    S.inc(e.dma_start(out=ld_[g % 2][:, :],
                                      in_=B.kv_d[:, g * 2560:(g + 1) * 2560]), B.dG, 16)
                    mm["ldkv", g] = S.val(B.dG)

                @B.block.tensor
                def _(e, st4=st4, c=c, g=g):
                    e.wait_ge(B.vQ, mm["xnb", st4])
                    if g == 0:
                        e.wait_ge(B.dG, mm["wload"])
                        e.wait_ge(B.vF, mm["sinit"])
                        e.wait_ge(B.vF, mm["minit"])
                    else:
                        e.wait_ge(B.vF, mm["zup", g - 1])
                        e.wait_ge(B.vF, mm["pte", g - 1, 3])
                    ins = None
                    for dc in range(8):
                        for fc in range(8):
                            ins = e.matmul(pq2[:, dc * 128:(dc + 1) * 128],
                                           wqb[:, fc * 1024 + dc * 128: fc * 1024 + (dc + 1) * 128],
                                           xnb[:, fc * 512 + c * 128: fc * 512 + (c + 1) * 128],
                                           start=(fc == 0), stop=(fc == 7))
                    mm["qp", g] = S.inc(ins, B.tT, 1)

                @B.block.vector
                def _(e, g=g):
                    e.wait_ge(B.tT, mm["qp", g])
                    e.wait_ge(B.dS, mm["rpt", g])
                    cosb = rpt[g % 2][:, 0:128].unsqueeze(1).broadcast_to([128, 4, 128])
                    sinb = rpt[g % 2][:, 128:256].unsqueeze(1).broadcast_to([128, 4, 128])
                    pv = pq2[:, :].rearrange("p (hh two t) -> p hh two t", two=2, t=128)
                    qv = qf[:, :].rearrange("p (hh two t) -> p hh two t", two=2, t=128)
                    h1 = hv1[:, :].rearrange("p (hh t) -> p hh t", t=128)
                    h2 = hv2[:, :].rearrange("p (hh t) -> p hh t", t=128)
                    e.tensor_tensor(h1, pv[:, :, 0, :], cosb, ALU.mult)
                    S.raw(e, e.tensor_tensor(h2, pv[:, :, 1, :], sinb, ALU.mult), B.vZ)
                    e.tensor_tensor(qv[:, :, 0, :], h1, h2, ALU.subtract)
                    e.tensor_tensor(h1, pv[:, :, 1, :], cosb, ALU.mult)
                    S.raw(e, e.tensor_tensor(h2, pv[:, :, 0, :], sinb, ALU.mult), B.vZ)
                    mm["qrope", g, 3] = S.inc(e.tensor_add(qv[:, :, 1, :], h1, h2), B.vF, 1)

                @B.block.tensor
                def _(e, st4=st4, c=c, g=g):
                    e.wait_ge(B.vQ, mm["xnb", st4])
                    if g > 0:
                        e.wait_ge(B.vF, mm["pkr", g - 1, 1])
                        e.wait_ge(B.aP, mm["sig", g - 1, 1])
                        e.wait_ge(B.vF, mm["oge", g - 1])
                    ins = None
                    for fc in range(8):
                        for hb in range(2):
                            ins = e.matmul(pk2[:, hb * 512:(hb + 1) * 512],
                                           xnb[:, fc * 512 + c * 128: fc * 512 + (c + 1) * 128],
                                           wgb[:, fc * 1024 + hb * 512: fc * 1024 + (hb + 1) * 512],
                                           start=(fc == 0), stop=(fc == 7))
                    mm["gp", g] = S.inc(ins, B.tT, 1)

                @B.block.vector
                def _(e, g=g):
                    e.wait_ge(B.tT, mm["gp", g])
                    mm["ge", g] = S.inc(e.tensor_copy(gtm[:, :], pk2[:, :]), B.vF, 1)

                for j in range(12 if PB_STAGE >= 1 else 0):
                    sl = 896 if j % 2 == 0 else 1536

                    @B.block.tensor
                    def _(e, g=g, j=j, sl=sl):
                        if j >= 2:
                            e.wait_ge(B.vF, mm["tre", g, j - 2])
                        else:
                            e.wait_ge(B.dG, mm["ldkv", g])
                            e.wait_ge(B.vF, mm["qrope", g, 3])
                            if g > 0:
                                e.wait_ge(B.vF, mm["zup", g - 1])
                        if j < 4:
                            ins = e.transpose(pq2b[0:64, sl:sl + 128],
                                              ld_[g % 2][:, 2304 + j * 64: 2304 + (j + 1) * 64],
                                              B.ident[:, :])
                        else:
                            ins = e.transpose(pq2b[:, sl:sl + 128],
                                              ld_[g % 2][:, (j - 4) * 128:(j - 3) * 128],
                                              B.ident[:, :])
                        mm["tr", g, j] = S.inc(ins, B.tT, 1)

                    @B.block.vector
                    def _(e, g=g, j=j, sl=sl):
                        e.wait_ge(B.tT, mm["tr", g, j])
                        if j < 4:
                            ins = e.tensor_copy(wvT[0:64, j * 128:(j + 1) * 128], pq2b[0:64, sl:sl + 128])
                        else:
                            ins = e.tensor_copy(kf[:, (j - 4) * 128:(j - 3) * 128], pq2b[:, sl:sl + 128])
                        mm["tre", g, j] = S.inc(ins, B.vF, 1)

                if PB_STAGE < 3:
                    continue

                @B.block.tensor
                def _(e, g=g):
                    e.wait_ge(B.vF, mm["tre", g, 11])
                    if g > 0:
                        e.wait_ge(B.vF, mm["sevac", g - 1])
                        e.wait_ge(B.vF, mm["zup", g - 1])
                    ins = None
                    for h in range(4):
                        for c2 in range(2):
                            dc = 2 * h + c2
                            ins = e.matmul(pq2[:, h * 128:(h + 1) * 128],
                                           kf[:, dc * 128:(dc + 1) * 128],
                                           qf[:, dc * 128:(dc + 1) * 128],
                                           start=(c2 == 0), stop=(c2 == 1))
                    mm["ST4", g] = S.inc(ins, B.tT, 1)

                @B.block.vector
                def _(e, g=g):
                    e.wait_ge(B.tT, mm["ST4", g])
                    mm["stme4", g] = S.inc(
                        e.tensor_tensor(stm[:, :].rearrange("p (h t) -> p h t", t=128),
                                        pq2[:, 0:512].rearrange("p (h t) -> p h t", t=128),
                                        B.tri_up[:, :].unsqueeze(1).broadcast_to([128, 4, 128]),
                                        ALU.mult), B.vF, 1)

                @B.block.tensor
                def _(e, g=g):
                    e.wait_ge(B.vF, mm["stme4", g])
                    for h in range(4):
                        for c2 in range(2):
                            dc = 2 * h + c2
                            e.matmul(pq2[:, 512 + h * 64: 512 + (h + 1) * 64],
                                     qf[:, dc * 128:(dc + 1) * 128],
                                     s1sb[:, dc * 64:(dc + 1) * 64], start=(c2 == 0), stop=False)
                        e.matmul(pq2[:, 512 + h * 64: 512 + (h + 1) * 64],
                                 stm[:, h * 128:(h + 1) * 128],
                                 ld_[g % 2][:, 2048 + h * 64: 2048 + (h + 1) * 64],
                                 start=False, stop=True)
                    for h in range(4):
                        e.matmul(pq2[:, 768 + h * 64: 768 + (h + 1) * 64], B.tri_up[:, :],
                                 ld_[g % 2][:, 2048 + h * 64: 2048 + (h + 1) * 64],
                                 start=True, stop=False)
                        e.matmul(pq2[:, 768 + h * 64: 768 + (h + 1) * 64], B.ones1[0:1, 0:128],
                                 zsbb[0:1, h * 64:(h + 1) * 64], start=False, stop=True)
                    ins = None
                    for h in range(4):
                        e.matmul(pq2[:, h * 64:(h + 1) * 64], B.tri_up[:, :],
                                 ld_[g % 2][:, 2304 + h * 64: 2304 + (h + 1) * 64],
                                 start=True, stop=False)
                        ins = e.matmul(pq2[:, h * 64:(h + 1) * 64], B.ones1[0:1, 0:128],
                                       zsbb[0:1, 256 + h * 64: 256 + (h + 1) * 64], start=False, stop=True)
                    mm["zz4", g] = S.inc(ins, B.tT, 1)

                @B.block.vector
                def _(e, g=g):
                    e.wait_ge(B.tT, mm["zz4", g])
                    e.reciprocal(rz[:, 0:256], pq2[:, 768:1024])
                    S.raw(e, e.reciprocal(rz[:, 256:512], pq2[:, 0:256]), B.vZ)
                    S.raw(e, e.tensor_tensor(lg[:, :], pq2[:, 512:768], rz[:, 0:256], ALU.mult), B.vZ)
                    mm["nmx4", g] = S.inc(
                        e.tensor_reduce(nmx[:, 0:4].unsqueeze(2),
                                        lg[:, :].rearrange("p (h m) -> p h m", m=64),
                                        AX.X, ALU.max, negate=True), B.vF, 1)

                @B.block.scalar
                def _(e, g=g):
                    e.wait_ge(B.vF, mm["nmx4", g])
                    ins = None
                    for h in range(4):
                        ins = e.activation(ev[:, h * 64:(h + 1) * 64], lg[:, h * 64:(h + 1) * 64],
                                           AF.Exp, bias=nmx[:, h:h + 1],
                                           accum_out=nmx[:, 4 + h:5 + h])
                    mm["ev4", g] = S.inc(ins, B.aP, 1)

                @B.block.vector
                def _(e, g=g):
                    e.wait_ge(B.aP, mm["ev4", g])
                    S.raw(e, e.reciprocal(nmx[:, 8:12], nmx[:, 4:8]), B.vZ)
                    S.raw(e, e.tensor_tensor(lg[:, :].rearrange("p (h m) -> p h m", m=64),
                                             ev[:, :].rearrange("p (h m) -> p h m", m=64),
                                             nmx[:, 8:12].unsqueeze(2).broadcast_to([128, 4, 64]),
                                             ALU.mult), B.vZ)
                    mm["pt4", g] = S.inc(
                        e.tensor_tensor(ptbi[:, :], lg[:, :], rz[:, 256:512], ALU.mult), B.vF, 1)

                for h in range(4):
                    sl = 896 if h % 2 == 0 else 1536

                    @B.block.tensor
                    def _(e, g=g, h=h, sl=sl):
                        if h >= 2:
                            e.wait_ge(B.vF, mm["pte", g, h - 2])
                        else:
                            e.wait_ge(B.vF, mm["pt4", g])
                        ins = e.transpose(pq2b[0:64, sl:sl + 128],
                                          ptbi[:, h * 64:(h + 1) * 64], B.ident[:, :])
                        mm["ptt", g, h] = S.inc(ins, B.tT, 1)

                    @B.block.vector
                    def _(e, g=g, h=h, sl=sl):
                        e.wait_ge(B.tT, mm["ptt", g, h])
                        mm["pte", g, h] = S.inc(
                            e.tensor_copy(ptb[0:64, h * 128:(h + 1) * 128], pq2b[0:64, sl:sl + 128]),
                            B.vF, 1)

                @B.block.tensor
                def _(e, g=g):
                    e.wait_ge(B.vF, mm["pte", g, 3])
                    ins = None
                    for h in range(4):
                        ins = e.matmul(pq2[:, h * 128:(h + 1) * 128],
                                       wvT[0:64, h * 128:(h + 1) * 128],
                                       ptb[0:64, h * 128:(h + 1) * 128], start=True, stop=True)
                    mm["at4", g] = S.inc(ins, B.tT, 1)

                @B.block.vector
                def _(e, g=g):
                    e.wait_ge(B.tT, mm["at4", g])
                    mm["atme4", g] = S.inc(
                        e.tensor_tensor(atm[:, :].rearrange("p (h t) -> p h t", t=128),
                                        pq2[:, 0:512].rearrange("p (h t) -> p h t", t=128),
                                        B.tri_up[:, :].unsqueeze(1).broadcast_to([128, 4, 128]),
                                        ALU.mult), B.vF, 1)

                @B.block.tensor
                def _(e, g=g):
                    e.wait_ge(B.vF, mm["atme4", g])
                    e.wait_ge(B.vF, mm["ge", g])
                    if g > 0:
                        e.wait_ge(B.vF, mm["sevac", g - 1])
                    for h in range(4):
                        e.matmul(pk2[:, h * 256:(h + 1) * 256], atm[:, h * 128:(h + 1) * 128],
                                 ld_[g % 2][:, 1024 + h * 256: 1024 + (h + 1) * 256],
                                 start=True, stop=False)
                        e.matmul(pk2[:, h * 256:(h + 1) * 256], ptb[0:64, h * 128:(h + 1) * 128],
                                 s2sb[0:64, h * 256:(h + 1) * 256], start=False, stop=True)
                        for c2 in range(2):
                            e.matmul(st1[:, (2 * h + c2) * 64:(2 * h + c2) * 64 + 64],
                                     ld_[g % 2][:, (2 * h + c2) * 128:(2 * h + c2 + 1) * 128],
                                     ld_[g % 2][:, 2048 + h * 64: 2048 + (h + 1) * 64],
                                     start=True, stop=True)
                        e.matmul(st2[0:64, h * 256:(h + 1) * 256],
                                 ld_[g % 2][:, 2304 + h * 64: 2304 + (h + 1) * 64],
                                 ld_[g % 2][:, 1024 + h * 256: 1024 + (h + 1) * 256],
                                 start=True, stop=True)
                    ins = None
                    for h in range(4):
                        e.matmul(pq2[0:1, h * 64:(h + 1) * 64], onesbf[:, :],
                                 ld_[g % 2][:, 2048 + h * 64: 2048 + (h + 1) * 64], start=True, stop=True)
                        ins = e.matmul(pq2[0:1, 256 + h * 64: 256 + (h + 1) * 64], onesbf[:, :],
                                       ld_[g % 2][:, 2304 + h * 64: 2304 + (h + 1) * 64],
                                       start=True, stop=True)
                    mm["scend", g] = S.inc(ins, B.tT, 1)
                mm["stu", g] = mm["scend", g]

                @B.block.vector
                def _(e, g=g):
                    e.wait_ge(B.tT, mm["scend", g])
                    S.raw(e, e.tensor_tensor(zf[g % 2][0:1, :], pq2[0:1, 0:512], zf[(g + 1) % 2][0:1, :],
                                             ALU.add), B.vZ)
                    mm["zup", g] = S.inc(e.tensor_copy(zsbb[0:1, :], zf[g % 2][0:1, :]), B.vF, 1)

                @B.block.vector
                def _(e, g=g):
                    e.wait_ge(B.tT, mm["scend", g])
                    e.tensor_tensor(s1f[g % 2][:, :], st1[:, :], s1f[(g + 1) % 2][:, :], ALU.add)
                    S.raw(e, e.tensor_tensor(s2f[g % 2][0:64, :], st2[0:64, :], s2f[(g + 1) % 2][0:64, :],
                                             ALU.add), B.vZ)
                    e.tensor_copy(s1sb[:, :], s1f[g % 2][:, :])
                    mm["sevac", g] = S.inc(e.tensor_copy(s2sb[0:64, :], s2f[g % 2][0:64, :]), B.vF, 1)

                if PB_STAGE < 4:
                    continue
                for hf in range(2):
                    @B.block.scalar
                    def _(e, g=g, hf=hf):
                        e.wait_ge(B.tT, mm["scend", g])
                        if (g, hf) != (0, 0):
                            e.wait_ge(B.vF, mm["og", g, hf - 1] if hf == 1 else mm["og", g - 1, 1])
                        mm["sq", g, hf] = S.inc(
                            e.activation(qfb[:, :], pk2[:, hf * 512:(hf + 1) * 512], AF.Square), B.aP, 1)
                        mm["sig", g, hf] = S.inc(
                            e.activation(kfb[:, :], gtm[:, hf * 512:(hf + 1) * 512], AF.Sigmoid), B.aP, 1)

                    @B.block.vector
                    def _(e, g=g, hf=hf):
                        e.wait_ge(B.aP, mm["sig", g, hf])
                        S.raw(e, e.tensor_reduce(ssq[:, 0:2].unsqueeze(2),
                                                 qfb[:, :].rearrange("p (h d) -> p h d", d=256),
                                                 AX.X, ALU.add), B.vZ)
                        mm["ssq", g, hf] = S.inc(
                            e.tensor_scalar(ssq[:, 2:4], ssq[:, 0:2], 1.0 / 256, None, ALU.mult), B.vF, 1)

                    @B.block.scalar
                    def _(e, g=g, hf=hf):
                        e.wait_ge(B.vF, mm["ssq", g, hf])
                        mm["sqr", g, hf] = S.inc(
                            e.activation(ssq[:, 4:6], ssq[:, 2:4], AF.Sqrt, bias=B.cst[:, 0:1]), B.aP, 1)

                    @B.block.vector
                    def _(e, g=g, hf=hf):
                        e.wait_ge(B.aP, mm["sqr", g, hf])
                        S.raw(e, e.reciprocal(ssq[:, 6:8], ssq[:, 4:6]), B.vZ)
                        S.raw(e, e.tensor_tensor(scf[:, :], gtm[:, hf * 512:(hf + 1) * 512], kfb[:, :],
                                                 ALU.mult), B.vZ)
                        S.raw(e, e.tensor_tensor(qfb[:, :].rearrange("p (h d) -> p h d", d=256),
                                                 pk2[:, hf * 512:(hf + 1) * 512].rearrange("p (h d) -> p h d", d=256),
                                                 ssq[:, 6:8].unsqueeze(2).broadcast_to([128, 2, 256]),
                                                 ALU.mult), B.vZ)
                        mm["pkr", g, hf] = S.inc(e.engine_nop(), B.vF, 1)
                        S.raw(e, e.tensor_tensor(kfb[:, :].rearrange("p (h d) -> p h d", d=256),
                                                 qfb[:, :].rearrange("p (h d) -> p h d", d=256),
                                                 gnr[:, :].unsqueeze(1).broadcast_to([128, 2, 256]),
                                                 ALU.mult), B.vZ)
                        mm["og", g, hf] = S.inc(
                            e.tensor_tensor(ogsb[:, hf * 512:(hf + 1) * 512], kfb[:, :], scf[:, :],
                                            ALU.mult), B.vF, 1)
                mm["og", g] = mm["og", g, 1]

                for j in range(8):
                    sl = 512 if j % 2 == 0 else 1536

                    @B.block.tensor
                    def _(e, g=g, j=j, sl=sl):
                        e.wait_ge(B.vF, mm["og", g])
                        if j >= 2:
                            e.wait_ge(B.vF, mm["ogte", g, j - 2])
                        ins = e.transpose(pk2b[:, sl:sl + 128], ogsb[:, j * 128:(j + 1) * 128],
                                          B.ident[:, :])
                        mm["ogt", g, j] = S.inc(ins, B.tT, 1)

                    @B.block.vector
                    def _(e, g=g, j=j, sl=sl):
                        e.wait_ge(B.tT, mm["ogt", g, j])
                        mm["ogte", g, j] = S.inc(
                            e.tensor_copy(ogfm[:, j * 128:(j + 1) * 128], pk2b[:, sl:sl + 128]), B.vF, 1)
                mm["oge", g] = mm["ogte", g, 7]

                @B.block.sync
                def _(e, g=g, st4=st4, c=c):
                    e.wait_ge(B.vF, mm["oge", g])
                    tok = st4 * 512 + c * 128
                    dst = B.og_d.ap().rearrange("p (dc t) -> p dc t", t=P)[:, :, tok:tok + 128]
                    S.inc(e.dma_start(out=dst, in_=ogfm[:, :].rearrange("p (dc t) -> p dc t", t=128)),
                          B.dS, 16)
                    mm["ogd", g] = S.val(B.dS)

        @B.block.sync
        def _(e):
            if PB_STAGE >= 4:
                e.wait_ge(B.dS, mm["ogd", min(15, PB_CHUNKS - 1)])

    phx.close()
    _barrier(B)


def _bc_h(ap64, nh=8):
    return ap64.unsqueeze(1).broadcast_to([128, nh, 64])


def _bc_d(ap8, d=64):
    return ap8.unsqueeze(2).broadcast_to([128, 8, d])


def _hd(ap512, d=64):
    return ap512.rearrange("p (h d) -> p h d", d=d)


def phase_ffn_subA(B, lyr):
    nc, S = B.nc, B.sched
    with contextlib.ExitStack() as ph:
        sbt = lambda n, sh, dt=F32: ph.enter_context(nc.sbuf_tensor(f"{n}_l{lyr}", sh, dt))
        wqkv = sbt("wqkv", [128, 8 * 1536], BF16)
        qsr = sbt("qsr", [128, 64])
        ksr = sbt("ksr", [128, 64])
        xw = [sbt(f"xw{i}", [128, 8 * 128], BF16) for i in range(2)]
        xpt = [sbt(f"xpt{i}", [128, 256]) for i in range(2)]
        kf = sbt("kf", [128, 512])
        bufA = sbt("bufA", [128, 512]); bufB = sbt("bufB", [128, 512]); bufC = sbt("bufC", [128, 512])
        bufD = sbt("bufD", [128, 512]); bufE = sbt("bufE", [128, 512]); bufF = sbt("bufF", [128, 512])
        qhat = [sbt(f"qhat{i}", [128, 512], BF16) for i in range(2)]
        khat = [sbt(f"khat{i}", [128, 512], BF16) for i in range(2)]
        ss = sbt("ss", [128, 48])
        scrb = sbt("scrb", [128, 512], BF16)
        vring = sbt("vring", [128, 8 * 512], BF16)
        kring = [ph.enter_context(nc.sbuf_tensor(f"kring{h}_l{lyr}", [65, 8 * 128], BF16)) for h in range(8)]
        qpp = [ph.enter_context(nc.sbuf_tensor(f"qpp{h}_l{lyr}", [65, 2 * 128], BF16)) for h in range(8)]
        sjunk = sbt("sjunk", [128, 16])
        scp = sbt("scp", [128, 2 * 640])
        mjunk = sbt("mjunk", [128, 640])
        nm8 = sbt("nm8", [128, 8])
        nm3 = sbt("nm3", [128, 4])
        nm8n = sbt("nm8n", [128, 8], BF16)
        sc04 = sbt("sc04", [128, 512])
        pt = [sbt(f"pt{i}", [128, 640], BF16) for i in range(2)]
        rcp = sbt("rcp", [128, 8])
        osb = sbt("osb", [128, 512], BF16)

        qkv_ps = ph.enter_context(nc.psum_tensor(f"qkv_ps_l{lyr}", [128, 1536], F32))
        s_ps = ph.enter_context(nc.psum_tensor(f"s_ps_l{lyr}", [128, 1024], F32))
        st_ps = ph.enter_context(nc.psum_tensor(f"st_ps_l{lyr}", [128, 1024], F32))
        o_ps = ph.enter_context(nc.psum_tensor(f"o_ps_l{lyr}", [128, 512], F32))
        trs = s_ps[:, :].bitcast(BF16)
        rs_c, nmt_c, otr_c = 640, 648, 904
        otr = st_ps[:, :].bitcast(BF16)

        sqq = sbt("sqq", [128, 512])
        sqk = sbt("sqk", [128, 512])
        smask = sbt("smask", [128, 5 * 640], BF16)
        B.smask = smask
        mm = {}
        b = {k: S.val(getattr(B, k)) for k in
             ["dS", "dG", "dA", "tT", "xwS", "vF", "vQ", "vY", "vN", "vC", "vR", "vE", "vM",
              "aP", "aK", "aN", "aE", "gK", "aSq", "vSum", "e1"]}

        @B.block.gpsimd
        def _(e):
            for ci in range(8):
                S.inc(e.dma_start(out=wqkv[:, ci * 1536:(ci + 1) * 1536],
                                  in_=B.inp["L0_ffn_wqkv"][lyr, ci * 128:(ci + 1) * 128, :]), B.dG, 16)
            S.inc(e.dma_start(out=qsr[:, :], in_=B.inp["qs_rep"][lyr]), B.dG, 16)
            S.inc(e.dma_start(out=ksr[:, :], in_=B.inp["ks_rep"][lyr]), B.dG, 16)
            for itt in range(5):
                S.inc(e.dma_start(out=smask[:, itt * 640:(itt + 1) * 640],
                                  in_=B.inp["smask"][itt]), B.dG, 16)

        @B.block.vector
        def _(e):
            for h in range(8):
                e.memset(kring[h][64:65, :], 1.0)
            for ci in range(8):
                off = (HALO - 640) - (1 if ci >= 4 else 0)
                e.tensor_copy(xw[0][:, ci * 128:(ci + 1) * 128],
                              B.stream[:, ci * NT + off: ci * NT + off + 128])
            S.inc(e.engine_nop(), B.xwS, 1)

        @B.block.sync
        def _(e):
            for t in range(NKV):
                if t >= 2:
                    e.wait_ge(B.vQ, b["vQ"] + t - 1)
                    e.wait_ge(B.gK, b["gK"] + t - 1)
                S.inc(e.dma_start(out=xpt[t % 2][:, :],
                                  in_=B.inp["xpos"][HALO - 640 + 128 * t: HALO - 512 + 128 * t, :]), B.dS, 16)

        for t in range(NKV):
            it = t - 4
            prod = True

            @B.block.tensor
            def _(e, t=t, it=it, prod=prod):
                if prod:
                    e.wait_ge(B.xwS, b["xwS"] + t + 1)
                    if t == 0:
                        e.wait_ge(B.dG, b["dG"] + 15 * 16)
                    if t >= 1:
                        e.wait_ge(B.vF, b["vF"] + t)
                        e.wait_ge(B.aSq, b["aSq"] + t)
                    ins = None
                    for ci in range(8):
                        for g in range(3):
                            ins = e.matmul(qkv_ps[:, g * 512:(g + 1) * 512],
                                           xw[t % 2][:, ci * 128:(ci + 1) * 128],
                                           wqkv[:, ci * 1536 + g * 512: ci * 1536 + (g + 1) * 512],
                                           start=(ci == 0), stop=(ci == 7))
                    mm["qkv", t] = S.inc(ins, B.tT, 1)
                    e.wait_ge(B.vQ, b["vQ"] + t + 1)
                    e.wait_ge(B.gK, b["gK"] + t + 1)
                    if t >= 1:
                        e.wait_ge(B.aK, b["aK"] + t)
                        e.wait_ge(B.vY, b["vY"] + t)
                    if it >= 1:
                        e.wait_ge(B.vM, b["vM"] + it * 8)
                    for h in range(8):
                        e.transpose(trs[0:64, h * 128:(h + 1) * 128],
                                    khat[t % 2][:, h * 64:(h + 1) * 64], B.ident[:, :])
                    ins = None
                    for h in range(8):
                        ins = e.transpose(trs[0:64, 1024 + h * 128:1024 + (h + 1) * 128],
                                          qhat[t % 2][:, h * 64:(h + 1) * 64], B.ident[:, :])
                    mm["tr", t] = S.inc(ins, B.tT, 1)
                if it >= 0:
                    e.wait_ge(B.aK, b["aK"] + min(it + 4, NKV - 1) + 1)
                    e.wait_ge(B.vY, b["vY"] + min(it + 4, NKV - 1) + 1)
                    base = it % 8
                    runs = []
                    kk0 = 0
                    while kk0 < 5:
                        sl0 = (it + kk0) % 8
                        n_ring = min(5 - kk0, 8 - sl0)
                        n_bank = (4 - kk0) if kk0 < 4 else (5 - kk0)
                        nn = min(n_ring, n_bank)
                        runs.append((kk0, nn))
                        kk0 += nn
                    for h in range(8):
                        e.wait_ge(B.vM, b["vM"] + it * 8 + h)
                        ins = None
                        for kk0, nn in runs:
                            sl0 = (it + kk0) % 8
                            ins = e.matmul(s_ps[:, kk0 * 128:(kk0 + nn) * 128],
                                           qpp[h][0:64, (it % 2) * 128:(it % 2) * 128 + 128],
                                           kring[h][0:64, sl0 * 128:(sl0 + nn) * 128],
                                           start=True, stop=True)
                        mm["S", it, h] = S.inc(ins, B.tT, 1)
                    e.wait_ge(B.vN, b["vN"] + it + 1)
                    if it >= 1:
                        e.wait_ge(B.aE, b["aE"] + it * 8)
                        e.wait_ge(B.vR, b["vR"] + it)
                        e.wait_ge(B.vE, b["vE"] + it * 4)
                    for h in range(8):
                        if h >= 1:
                            e.wait_ge(B.aN, b["aN"] + it * 8 + h)
                        ins = e.matmul(st_ps[64:65, nmt_c + (h % 2) * 128: nmt_c + (h % 2) * 128 + 128],
                                       nm8n[:, h:h + 1], B.ident[:, :], start=True, stop=True,
                                       tile_position=(0, 64))
                        mm["nmt", it, h] = S.inc(ins, B.tT, 1)
                    for h in range(8):
                        e.wait_ge(B.aN, b["aN"] + it * 8 + h + 1)
                        if h >= 1:
                            e.wait_ge(B.aE, b["aE"] + it * 8 + h)
                        ins = None
                        for kk in range(5):
                            kvt = it + kk
                            ins = e.matmul(st_ps[:, kk * 128:(kk + 1) * 128],
                                           kring[h][:, (kvt % 8) * 128:(kvt % 8) * 128 + 128],
                                           qpp[h][:, (it % 2) * 128:(it % 2) * 128 + 128],
                                           start=True, stop=True)
                        mm["ST", it, h] = S.inc(ins, B.tT, 1)
                        e.wait_ge(B.aE, b["aE"] + it * 8 + h + 1)
                        ins = None
                        for kk in range(5):
                            kvt = it + kk
                            e.matmul(o_ps[:, h * 64:(h + 1) * 64],
                                     pt[h % 2][:, kk * 128:(kk + 1) * 128],
                                     vring[:, (kvt % 8) * 512 + h * 64:(kvt % 8) * 512 + (h + 1) * 64],
                                     start=(kk == 0), stop=(kk == 4))
                            ins = e.matmul(st_ps[:, rs_c + h:rs_c + h + 1],
                                           pt[h % 2][:, kk * 128:(kk + 1) * 128],
                                           B.kvvalid_bf[:, kvt:kvt + 1],
                                           start=(kk == 0), stop=(kk == 4))
                        mm["PV", it, h] = S.inc(ins, B.tT, 1)
                    e.wait_ge(B.vR, b["vR"] + it + 1)
                    for j in range(4):
                        if j >= 1:
                            e.wait_ge(B.vE, b["vE"] + it * 4 + j)
                        ins = e.transpose(otr[:, 1808:1936],
                                          osb[:, j * 128:(j + 1) * 128], B.ident[:, :])
                        mm["otr", it, j] = S.inc(ins, B.tT, 1)

            @B.block.scalar
            def _(e, t=t, it=it, prod=prod):
                if prod:
                    e.wait_ge(B.tT, mm["qkv", t])
                    if t >= 1:
                        e.wait_ge(B.vF, b["vF"] + t)
                        e.wait_ge(B.gK, b["gK"] + t)
                    e.copy(kf[:, 0:256], qkv_ps[:, 512:768])
                    e.activation(sqq[:, :], qkv_ps[:, 0:512], AF.Square)
                    S.raw(e, e.copy(kf[:, 256:512], qkv_ps[:, 768:1024]), B.aZ)
                    S.inc(e.activation(sqk[:, :], kf[:, :], AF.Square), B.aSq, 1)
                    e.wait_ge(B.vSum, b["vSum"] + t + 1)
                    S.inc(e.activation(ss[:, 16:32], ss[:, 32:48], AF.Sqrt), B.aP, 1)
                    if t >= 8:
                        e.wait_ge(B.tT, mm["PV", t - 8, 7])
                    e.wait_ge(B.tT, mm["tr", t])
                    ins = None
                    for h in range(8):
                        ins = e.copy(kring[h][0:64, (t % 8) * 128:(t % 8) * 128 + 128],
                                     trs[0:64, h * 128:(h + 1) * 128])
                    S.inc(ins, B.aK, 1)
                if it >= 0:
                    if it >= 2:
                        e.wait_ge(B.tT, mm["ST", it - 2, 7])
                    for h in range(8):
                        e.wait_ge(B.tT, mm["nmt", it, h])
                        ins = e.copy(qpp[h][64:65, (it % 2) * 128:(it % 2) * 128 + 128],
                                     st_ps[64:65, nmt_c + (h % 2) * 128: nmt_c + (h % 2) * 128 + 128])
                        S.inc(ins, B.aN, 1)
                    for h in range(8):
                        e.wait_ge(B.vC, b["vC"] + it * 8 + h + 1)
                        if h >= 2:
                            e.wait_ge(B.tT, mm["PV", it, h - 2])
                        elif it >= 1:
                            e.wait_ge(B.tT, mm["PV", it - 1, h + 6])
                        e.activation(pt[h % 2][:, 128:512], st_ps[:, 128:512], AF.Exp)
                        e.activation(pt[h % 2][:, 0:128], sc04[:, (h % 2) * 256:(h % 2) * 256 + 128], AF.Exp)
                        ins = e.activation(pt[h % 2][:, 512:640],
                                           sc04[:, (h % 2) * 256 + 128:(h % 2) * 256 + 256], AF.Exp)
                        S.inc(ins, B.aE, 1)

            @B.block.vector
            def _(e, t=t, it=it, prod=prod):
                if prod:
                    e.wait_ge(B.tT, mm["qkv", t])
                    if t >= 8:
                        e.wait_ge(B.tT, mm["PV", t - 8, 7])
                    e.tensor_copy(vring[:, (t % 8) * 512:(t % 8) * 512 + 512], qkv_ps[:, 1024:1536])
                    e.wait_ge(B.aSq, b["aSq"] + t + 1)
                    e.tensor_reduce(ss[:, 0:8].unsqueeze(2), _hd(sqq[:, :]), AX.X, ALU.add)
                    S.raw(e, e.tensor_reduce(ss[:, 8:16].unsqueeze(2), _hd(sqk[:, :]), AX.X, ALU.add), B.vZ)
                    S.inc(e.tensor_scalar(ss[:, 32:48], ss[:, 0:16], 1e-24, None, ALU.max), B.vSum, 1)
                    e.wait_ge(B.aP, b["aP"] + t + 1)
                    S.raw(e, e.reciprocal(ss[:, 32:48], ss[:, 16:32]), B.vZ)
                    e.wait_ge(B.vQ, b["vQ"] + t)
                    S.raw(e, e.tensor_tensor(_hd(bufA[:, :]), _hd(qkv_ps[:, 0:512]),
                                             _bc_d(ss[:, 32:40]), ALU.mult), B.vZ)
                    S.inc(e.engine_nop(), B.vF, 1)
                    if t + 1 < NKV:
                        for ci in range(8):
                            off = (HALO - 640) + 128 * (t + 1) - (1 if ci >= 4 else 0)
                            e.tensor_copy(xw[(t + 1) % 2][:, ci * 128:(ci + 1) * 128],
                                          B.stream[:, ci * NT + off: ci * NT + off + 128])
                        S.inc(e.engine_nop(), B.xwS, 1)
                    e.wait_ge(B.tT, mm["tr", t])
                    if t >= 6:
                        e.wait_ge(B.tT, mm["ST", t - 6, 7])
                    if t >= 4:
                        ins = None
                        for h in range(8):
                            ins = e.tensor_copy(qpp[h][0:64, ((t - 4) % 2) * 128:((t - 4) % 2) * 128 + 128],
                                                trs[0:64, 1024 + h * 128:1024 + (h + 1) * 128])
                        S.inc(ins, B.vY, 1)
                    else:
                        S.inc(e.engine_nop(), B.vY, 1)
                if it >= 0:
                    for h in range(8):
                        e.wait_ge(B.tT, mm["S", it, h])
                        o_ = (h % 2) * 640
                        S.inc(e.tensor_copy(scp[:, o_:o_ + 640], s_ps[:, 0:640]), B.vM, 1)
                        if it <= 4:
                            S.raw(e, e.tensor_tensor(mjunk[:, 0:640], scp[:, o_:o_ + 640],
                                                     B.smask[:, it * 640:(it + 1) * 640], ALU.add), B.vZ)
                            e.tensor_reduce(nm8[:, h:h + 1], mjunk[:, 0:640], AX.X, ALU.max)
                        else:
                            e.tensor_tensor(mjunk[:, 0:128], scp[:, o_:o_ + 128], B.addm_lo[:, :], ALU.add)
                            S.raw(e, e.tensor_tensor(mjunk[:, 512:640], scp[:, o_ + 512:o_ + 640],
                                                     B.addm_up[:, :], ALU.add), B.vZ)
                            e.tensor_reduce(nm3[:, 0:1], mjunk[:, 0:128], AX.X, ALU.max)
                            e.tensor_reduce(nm3[:, 1:2], scp[:, o_ + 128:o_ + 512], AX.X, ALU.max)
                            S.raw(e, e.tensor_reduce(nm3[:, 2:3], mjunk[:, 512:640], AX.X, ALU.max), B.vZ)
                            e.tensor_reduce(nm8[:, h:h + 1], nm3[:, 0:3], AX.X, ALU.max)
                    S.raw(e, e.engine_nop(), B.vZ)
                    S.inc(e.tensor_scalar(nm8n[:, :], nm8[:, :], -1.0, None, ALU.mult), B.vN, 1)
                    for h in range(8):
                        e.wait_ge(B.tT, mm["ST", it, h])
                        e.tensor_tensor(sc04[:, (h % 2) * 256:(h % 2) * 256 + 128],
                                        st_ps[:, 0:128], B.addm_lo[:, :], ALU.add)
                        ins = e.tensor_tensor(sc04[:, (h % 2) * 256 + 128:(h % 2) * 256 + 256],
                                              st_ps[:, 512:640], B.addm_up[:, :], ALU.add)
                        S.inc(ins, B.vC, 1)
                    e.wait_ge(B.tT, mm["PV", it, 7])
                    S.raw(e, e.tensor_scalar(sjunk[:, 0:8], st_ps[:, rs_c:rs_c + 8], 1e-30, None, ALU.max), B.vZ)
                    S.raw(e, e.reciprocal(rcp[:, :], sjunk[:, 0:8]), B.vZ)
                    S.inc(e.tensor_tensor(_hd(osb[:, :]), _hd(o_ps[:, :]), _bc_d(rcp[:, :]),
                                          ALU.mult), B.vR, 1)
                    for j in range(4):
                        e.wait_ge(B.tT, mm["otr", it, j])
                        ins = e.tensor_copy(B.o0[:, j * NQ * 128 + it * 128: j * NQ * 128 + (it + 1) * 128],
                                            otr[:, 1808:1936])
                        S.inc(ins, B.vE, 1)

            @B.block.gpsimd
            def _(e, t=t, prod=prod):
                if not prod:
                    return
                e.wait_ge(B.aP, b["aP"] + t + 1)
                e.wait_ge(B.vF, b["vF"] + t + 1)
                e.tensor_tensor(_hd(bufD[:, :]), _hd(kf[:, :]), _bc_d(ss[:, 40:48]), ALU.mult)
                S.raw(e, e.engine_nop(), B.gZ)
                S.raw(e, e.tensor_tensor(_hd(bufE[:, :]), _hd(bufD[:, :]), _bc_h(ksr[:, :]), ALU.mult), B.gZ)
                e.tensor_tensor(_hd(bufD[:, :]), _hd(bufE[:, :]), _bc_h(xpt[t % 2][:, 128:192]), ALU.mult)
                e.tensor_tensor(_hd(bufF[:, :], 64)[:, :, 0:32], _hd(bufE[:, :])[:, :, 32:64],
                                xpt[t % 2][:, 192:224].unsqueeze(1).broadcast_to([128, 8, 32]), ALU.mult)
                S.raw(e, e.tensor_tensor(_hd(bufF[:, :], 64)[:, :, 32:64], _hd(bufE[:, :])[:, :, 0:32],
                                         xpt[t % 2][:, 224:256].unsqueeze(1).broadcast_to([128, 8, 32]),
                                         ALU.mult), B.gZ)
                if t >= 2:
                    e.wait_ge(B.tT, mm["tr", t - 1])
                S.inc(e.tensor_add(khat[t % 2][:, :], bufD[:, :], bufF[:, :]), B.gK, 1)
                S.raw(e, e.tensor_tensor(_hd(bufB[:, :]), _hd(bufA[:, :]), _bc_h(qsr[:, :]), ALU.mult), B.gZ)
                e.tensor_tensor(_hd(bufA[:, :]), _hd(bufB[:, :]), _bc_h(xpt[t % 2][:, 0:64]), ALU.mult)
                e.tensor_tensor(_hd(bufC[:, :], 64)[:, :, 0:32], _hd(bufB[:, :])[:, :, 32:64],
                                xpt[t % 2][:, 64:96].unsqueeze(1).broadcast_to([128, 8, 32]), ALU.mult)
                S.raw(e, e.tensor_tensor(_hd(bufC[:, :], 64)[:, :, 32:64], _hd(bufB[:, :])[:, :, 0:32],
                                         xpt[t % 2][:, 96:128].unsqueeze(1).broadcast_to([128, 8, 32]),
                                         ALU.mult), B.gZ)
                S.inc(e.tensor_add(qhat[t % 2][:, :], bufA[:, :], bufC[:, :]), B.vQ, 1)


    _barrier(B)
    _tap_sbuf(B, f"o0_{lyr}", B.o0[:, 0:4 * NQ * 128], (128, 4 * NQ * 128), BF16)


def phase_ffn_rest(B, lyr, o0stack=None):
    nc, S = B.nc, B.sched
    with contextlib.ExitStack() as ph:
        sbt = lambda n, sh, dt=F32: ph.enter_context(nc.sbuf_tensor(f"{n}_r{lyr}", sh, dt))
        scr = sbt("scr", [128, 512])
        scr2 = sbt("scr2", [128, 512])
        phA = contextlib.ExitStack()
        wo = phA.enter_context(nc.sbuf_tensor(f"wo_r{lyr}", [128, 4 * D], BF16, side="right"))
        ps1 = ph.enter_context(nc.psum_tensor(f"ps1_r{lyr}", [128, 512], F32))
        ps2 = ph.enter_context(nc.psum_tensor(f"ps2_r{lyr}", [128, 512], F32))

        b = {k: S.val(getattr(B, k)) for k in ["dG", "tT", "vF"]}
        mm = {}

        @B.block.gpsimd
        def _(e):
            for ic in range(4):
                S.inc(e.dma_start(out=wo[:, ic * D:(ic + 1) * D],
                                  in_=B.inp["L0_ffn_wo"][lyr, ic * 128:(ic + 1) * 128, :]), B.dG, 16)

        widths = [512, 512, 512, 512, 128]
        offs = [HALO - 128 + i * 512 for i in range(5)]

        @B.block.tensor
        def _(e):
            e.wait_ge(B.dG, b["dG"] + 4 * 16)
            for wi, (w, off) in enumerate(zip(widths, offs)):
                for oc in range(8):
                    g = wi * 8 + oc
                    pp = [ps1, ps2][g % 2]
                    if g >= 2:
                        e.wait_ge(B.vF, b["vF"] + 2 * (g - 1))
                    ins = None
                    o0c = off - (HALO - 128)
                    for ic in range(4):
                        ins = e.matmul(pp[:, 0:w], wo[:, ic * D + oc * 128: ic * D + (oc + 1) * 128],
                                       B.o0[:, ic * NQ * 128 + o0c: ic * NQ * 128 + o0c + w],
                                       start=(ic == 0), stop=(ic == 3))
                    mm["wo", g] = S.inc(ins, B.tT, 1)

        @B.block.vector
        def _(e):
            for wi, (w, off) in enumerate(zip(widths, offs)):
                for oc in range(8):
                    g = wi * 8 + oc
                    pp = [ps1, ps2][g % 2]
                    e.wait_ge(B.tT, mm["wo", g])
                    S.raw(e, e.tensor_add(scr[:, 0:w], pp[:, 0:w],
                                          B.stream[:, oc * NT + off: oc * NT + off + w]), B.vZ)
                    S.inc(e.tensor_copy(B.stream[:, oc * NT + off: oc * NT + off + w], scr[:, 0:w]),
                          B.vF, 2)

        _barrier(B)
        _tap_sbuf(B, f"xmid{lyr}", B.stream[:, 0:8 * NT], (128, 8 * NT))
        phA.close()
        if o0stack is not None:
            o0stack.close()

        w2 = sbt("w2", [128, 22 * D], BF16)
        w1ab = [sbt(f"w1ab{i}", [128, 8 * 128], BF16) for i in range(2)]
        w1gb = [sbt(f"w1gb{i}", [128, 8 * 128], BF16) for i in range(2)]
        h2n = sbt("h2n", [128, 8 * 512], BF16)
        sqb = [sbt(f"sqb{i}", [128, 512]) for i in range(2)]
        glu = sbt("glu", [128, 22 * 512], BF16)
        gl = sbt("gl", [128, 512])
        stv = sbt("stv", [1, 3 * 512])
        r2b = sbt("r2b", [2, 512])
        rstdb = sbt("rstdb", [1, 512])
        lnr = sbt("lnr", [2, 8 * 128])
        b1ac = sbt("b1ac", [128, 22])
        b1gc = sbt("b1gc", [128, 22])
        b2c = sbt("b2c", [128, 8])
        psa = [ph.enter_context(nc.psum_tensor(f"psa{i}_r{lyr}", [128, 512], F32)) for i in range(2)]
        psg = [ph.enter_context(nc.psum_tensor(f"psg{i}_r{lyr}", [128, 512], F32)) for i in range(2)]
        psc1 = ph.enter_context(nc.psum_tensor(f"psc1_r{lyr}", [128, 512], F32))
        psc2 = ph.enter_context(nc.psum_tensor(f"psc2_r{lyr}", [128, 512], F32))

        @B.block.gpsimd
        def _(e):
            for fc in range(22):
                S.inc(e.dma_start(out=w2[:, fc * D:(fc + 1) * D],
                                  in_=B.inp["w2_blk"][lyr, fc]), B.dG, 16)
            S.inc(e.dma_start(out=b1ac[:, :], in_=B.inp["b1a_cols"][lyr]), B.dG, 16)
            S.inc(e.dma_start(out=b1gc[:, :], in_=B.inp["b1g_cols"][lyr]), B.dG, 16)
            S.inc(e.dma_start(out=b2c[:, :], in_=B.inp["b2_cols"][lyr]), B.dG, 16)
            S.inc(e.dma_start(out=lnr[0:2, :], in_=B.inp["lnvec"][lyr]), B.dG, 16)

        @B.block.tensor
        def _(e):
            e.wait_ge(B.dG, S.val(B.dG))

        global H2N_GLU
        H2N_GLU = (h2n, glu)

        for tt in (3, 2, 1, 0):
            toff = HALO + tt * 512
            b2_ = {k: S.val(getattr(B, k)) for k in ["tT", "vF", "vQ", "vN", "vC", "aP", "aK", "dG"]}

            @B.block.scalar
            def _(e, toff=toff, b2_=b2_):
                e.wait_ge(B.vF, b2_["vF"])
                for ci in range(8):
                    off2 = toff - (1 if ci >= 4 else 0)
                    if ci >= 2:
                        e.wait_ge(B.tT, b2_["tT"] + ci - 1)
                    S.inc(e.activation(sqb[ci % 2][:, :],
                                       B.stream[:, ci * NT + off2: ci * NT + off2 + 512],
                                       AF.Square), B.aP, 1)

            @B.block.tensor
            def _(e, tt=tt, toff=toff, b2_=b2_):
                e.wait_ge(B.vF, b2_["vF"])
                for ci in range(8):
                    off2 = toff - (1 if ci >= 4 else 0)
                    e.wait_ge(B.aP, b2_["aP"] + ci + 1)
                    e.matmul(psc1[0:1, 0:512], B.onescolf[:, :],
                             B.stream[:, ci * NT + off2: ci * NT + off2 + 512],
                             start=(ci == 0), stop=(ci == 7))
                    ins = e.matmul(psc2[0:1, 0:512], B.onescolf[:, :], sqb[ci % 2][:, :],
                                   start=(ci == 0), stop=(ci == 7))
                    S.inc(ins, B.tT, 1)
                mm["lnsum", tt] = S.val(B.tT)

            @B.block.vector
            def _(e, tt=tt, b2_=b2_):
                e.wait_ge(B.tT, mm["lnsum", tt])
                e.tensor_scalar(stv[0:1, 0:512], psc1[0:1, 0:512], 1.0 / D, None, ALU.mult)
                S.raw(e, e.tensor_scalar(stv[0:1, 512:1024], psc2[0:1, 0:512], 1.0 / D, None, ALU.mult), B.vZ)
                S.raw(e, e.tensor_tensor(gl[0:1, 0:512], stv[0:1, 0:512], stv[0:1, 0:512], ALU.mult), B.vZ)
                S.inc(e.tensor_tensor(stv[0:1, 1024:1536], stv[0:1, 512:1024], gl[0:1, 0:512],
                                      ALU.subtract), B.vN, 1)

            @B.block.scalar
            def _(e, b2_=b2_):
                e.wait_ge(B.vN, b2_["vN"] + 1)
                S.inc(e.activation(stv[0:1, 512:1024], stv[0:1, 1024:1536], AF.Sqrt, bias=B.cst[0:1, 0:1]), B.aK, 1)

            @B.block.vector
            def _(e, b2_=b2_):
                e.wait_ge(B.aK, b2_["aK"] + 1)
                S.raw(e, e.reciprocal(stv[0:1, 1024:1536], stv[0:1, 512:1024]), B.vZ)
                e.tensor_copy(rstdb[0:1, :], stv[0:1, 1024:1536])
                S.raw(e, e.tensor_tensor(stv[0:1, 512:1024], stv[0:1, 0:512], stv[0:1, 1024:1536], ALU.mult), B.vZ)
                S.raw(e, e.memset(r2b[0:2, :], -1.0), B.vZ)
                S.inc(e.tensor_copy(r2b[0:1, :], stv[0:1, 512:1024]), B.vC, 1)

            @B.block.tensor
            def _(e, tt=tt, b2_=b2_):
                e.wait_ge(B.vC, b2_["vC"] + 1)
                for ci in range(8):
                    if ci >= 1:
                        e.wait_ge(B.vF, b2_["vF"] + 2 * ci)
                    e.matmul(psc1[:, :], lnr[0:1, ci * 128:(ci + 1) * 128], rstdb[0:1, :],
                             start=True, stop=True)
                    ins = e.matmul(psc2[:, :], lnr[0:2, ci * 128:(ci + 1) * 128], r2b[0:2, :],
                                   start=True, stop=True)
                    mm["c12", tt, ci] = S.inc(ins, B.tT, 1)

            @B.block.vector
            def _(e, tt=tt, toff=toff, b2_=b2_):
                for ci in range(8):
                    off2 = toff - (1 if ci >= 4 else 0)
                    e.wait_ge(B.tT, mm["c12", tt, ci])
                    S.raw(e, e.tensor_tensor(scr[:, 0:512],
                                             B.stream[:, ci * NT + off2: ci * NT + off2 + 512],
                                             psc1[:, :], ALU.mult), B.vZ)
                    S.raw(e, e.tensor_tensor(h2n[:, ci * 512:(ci + 1) * 512], scr[:, 0:512],
                                             psc2[:, :], ALU.subtract), B.vZ)
                    S.inc(e.engine_nop(), B.vF, 2)

            @B.block.tensor
            def _(e, tt=tt, b2_=b2_):
                for ocp in range(22):
                    e.wait_ge(B.dG, b2_["dG"] + (ocp + 1) * 32)
                    if ocp >= 2:
                        e.wait_ge(B.vC, b2_["vC"] + 1 + ocp - 1)
                        e.wait_ge(B.aK, b2_["aK"] + 1 + ocp - 1)
                    else:
                        e.wait_ge(B.vC, b2_["vC"])
                        e.wait_ge(B.aK, b2_["aK"])
                    if ocp == 0:
                        e.wait_ge(B.vF, b2_["vF"] + 16)
                    ins = None
                    for ci in range(8):
                        e.matmul(psa[ocp % 2][:, :], w1ab[ocp % 2][:, ci * 128:(ci + 1) * 128],
                                 h2n[:, ci * 512:(ci + 1) * 512], start=(ci == 0), stop=(ci == 7))
                        ins = e.matmul(psg[ocp % 2][:, :], w1gb[ocp % 2][:, ci * 128:(ci + 1) * 128],
                                       h2n[:, ci * 512:(ci + 1) * 512], start=(ci == 0), stop=(ci == 7))
                    mm["w1", tt, ocp] = S.inc(ins, B.tT, 1)

            @B.block.gpsimd
            def _(e, tt=tt, b2_=b2_):
                for ocp in range(22):
                    if ocp >= 2:
                        e.wait_ge(B.tT, mm["w1", tt, ocp - 2])
                    else:
                        e.wait_ge(B.tT, b2_["tT"])
                    for half, dst in ((0, w1ab), (1, w1gb)):
                        base = ((lyr * 2 + half) * 22 + ocp) * 1024 * 128
                        S.inc(e.dma_start(out=dst[ocp % 2][:, :],
                                          in_=AP(B.inp["w1_blk"], base,
                                                 [[128, 128], [16384, 8], [1, 128]])),
                              B.dG, 16)

            @B.block.scalar
            def _(e, tt=tt, b2_=b2_):
                for ocp in range(22):
                    e.wait_ge(B.tT, mm["w1", tt, ocp])
                    if ocp >= 1:
                        e.wait_ge(B.vC, b2_["vC"] + 1 + ocp)
                    S.inc(e.activation(gl[:, :], psg[ocp % 2][:, :], AF.Gelu, bias=b1gc[:, ocp:ocp + 1]), B.aK, 1)

            @B.block.vector
            def _(e, tt=tt, b2_=b2_):
                for ocp in range(22):
                    e.wait_ge(B.aK, b2_["aK"] + 1 + ocp + 1)
                    ins = e.scalar_tensor_tensor(glu[:, ocp * 512:(ocp + 1) * 512], psa[ocp % 2][:, :],
                                                 b1ac[:, ocp:ocp + 1], gl[:, :], ALU.add, ALU.mult)
                    S.inc(ins, B.vC, 1)

            @B.block.tensor
            def _(e, tt=tt, b2_=b2_):
                for oc in range(8):
                    pp = [ps1, ps2][oc % 2]
                    if oc >= 2:
                        e.wait_ge(B.vF, b2_["vF"] + 16 + 2 * (oc - 1))
                    e.wait_ge(B.vC, b2_["vC"] + 1 + 22)
                    ins = None
                    for fc in range(22):
                        ins = e.matmul(pp[:, :], w2[:, fc * D + oc * 128: fc * D + (oc + 1) * 128],
                                       glu[:, fc * 512:(fc + 1) * 512], start=(fc == 0), stop=(fc == 21))
                    mm["w2", tt, oc] = S.inc(ins, B.tT, 1)

            @B.block.vector
            def _(e, tt=tt, toff=toff, b2_=b2_):
                for oc in range(8):
                    pp = [ps1, ps2][oc % 2]
                    e.wait_ge(B.tT, mm["w2", tt, oc])
                    S.raw(e, e.scalar_tensor_tensor(scr2[:, 0:512], pp[:, :], b2c[:, oc:oc + 1],
                                                    B.stream[:, oc * NT + toff: oc * NT + toff + 512],
                                                    ALU.add, ALU.add), B.vZ)
                    S.inc(e.tensor_copy(B.stream[:, oc * NT + toff: oc * NT + toff + 512],
                                        scr2[:, 0:512]), B.vF, 2)

    _barrier(B)
    _tap_sbuf(B, f"h2n{lyr}", H2N_GLU[0][:, :], (128, 8 * 512), BF16)
    _tap_sbuf(B, f"glu{lyr}", H2N_GLU[1][:, :], (128, 22 * 512), BF16)
    _tap_sbuf(B, f"x{lyr + 1}", B.stream[:, 0:8 * NT], (128, 8 * NT))

    if lyr == 0:
        phase_halo_exchange(B)
        _barrier(B)


def phase_halo_exchange(B):
    nc, S = B.nc, B.sched
    with contextlib.ExitStack() as ph:
        hsc = ph.enter_context(nc.sbuf_tensor("hsc", [128, 8 * HALO], F32))
        pid = ph.enter_context(nc.sbuf_tensor("pid_sb", [1, 2], mybir.dt.uint32))

        @B.block.sync
        def _(e):
            src = B.stream[:, :].rearrange("p (c n) -> p c n", n=NT)[:, :, NT - HALO:NT]
            S.inc(e.dma_start(out=B.ag2_in[:, :], in_=src), B.dS, 16)

        @B.block.gpsimd
        def _(e):
            S.inc(e.dma_start(out=pid[0:1, 0:1], in_=B.nc.partition_id_tensor[0:1, 0:1]), B.dG, 16)
            e.wait_ge(B.dS, S.val(B.dS))
            S.inc(e.collective_compute("AllGather", ALU.bypass, replica_groups=RG,
                                       ins=[B.ag2_in.ap()], outs=[B.ag2_out.ap()]), B.cc, 1)
            e.wait_ge(B.cc, S.val(B.cc))
            e.wait_ge(B.dG, S.val(B.dG))
            with e.register("hx") as gr:
                e.reg_load(gr, pid[0:1, 0:1])
                e.reg_mod(gr, gr, 4)
                e.reg_add(gr, gr, 3)
                e.reg_mod(gr, gr, 4)
                e.reg_mul(gr, gr, 128 * 8 * HALO)
                S.inc(e.dma_start(out=hsc[:, :],
                                  in_=AP(B.ag2_out, gr, [[8 * HALO, 128], [1, 8 * HALO]])), B.dG, 16)

        @B.block.vector
        def _(e):
            e.wait_ge(B.dG, S.val(B.dG))
            for ci in range(8):
                S.inc(e.tensor_scalar(B.stream[:, ci * NT: ci * NT + HALO],
                                      hsc[:, ci * HALO:(ci + 1) * HALO],
                                      B.hgate[:, 0:1], None, ALU.mult), B.vF, 1)


def phase_final(B):
    nc, S = B.nc, B.sched
    with contextlib.ExitStack() as ph:
        sbt = lambda n, sh, dt=F32: ph.enter_context(nc.sbuf_tensor(f"{n}_fin", sh, dt))
        wob = sbt("wob", [128, 8 * 1024], BF16)
        ogs = [sbt(f"ogs{i}", [128, 8 * 512], BF16) for i in range(2)]
        fs = sbt("fs", [128, 8 * 512])
        x0t = [sbt(f"x0t{i}", [128, 1024]) for i in range(2)]
        ot = [sbt(f"ot{i}", [128, 1024]) for i in range(2)]
        pa = [ph.enter_context(nc.psum_tensor(f"pa{i}_fin", [128, 512], F32)) for i in range(2)]
        tp = [ph.enter_context(nc.psum_tensor(f"tp{i}_fin", [128, 512], F32)) for i in range(2)]
        mm = {}
        b = {k: S.val(getattr(B, k)) for k in ["dS", "dG", "dA", "tT", "vF", "vQ", "aP"]}

        @B.block.gpsimd
        def _(e):
            for dc in range(8):
                S.inc(e.dma_start(out=wob[:, dc * 1024:(dc + 1) * 1024],
                                  in_=B.inp["L0_attn_Wo"][dc * 128:(dc + 1) * 128, :]), B.dG, 16)
            mm["wo"] = S.val(B.dG)

        for t4 in range(4):
            @B.block.scalar
            def _(e, t4=t4):
                if t4 >= 2:
                    e.wait_ge(B.tT, mm["att", t4 - 2, 7])
                src = B.og_d.ap().rearrange("p (dc t) -> p dc t", t=P)[:, :, t4 * 512:(t4 + 1) * 512]
                S.inc(e.dma_start(out=ogs[t4 % 2][:, :].rearrange("p (dc t) -> p dc t", t=512),
                                  in_=src), B.dA, 16)
                mm["ogl", t4] = S.val(B.dA)

            for oc in range(8):
                @B.block.tensor
                def _(e, t4=t4, oc=oc):
                    if oc == 0:
                        e.wait_ge(B.dA, mm["ogl", t4])
                    if t4 == 0 and oc == 0:
                        e.wait_ge(B.dG, mm["wo"])
                    pp = pa[oc % 2]
                    if oc >= 2:
                        e.wait_ge(B.vF, mm["fse", t4, oc - 2])
                    elif t4 > 0:
                        e.wait_ge(B.vF, mm["fse", t4 - 1, 6 + oc])
                    ins = None
                    for dc in range(8):
                        ins = e.matmul(pp[:, :], wob[:, dc * 1024 + oc * 128: dc * 1024 + (oc + 1) * 128],
                                       ogs[t4 % 2][:, dc * 512:(dc + 1) * 512],
                                       start=(dc == 0), stop=(dc == 7))
                    mm["att", t4, oc] = S.inc(ins, B.tT, 1)

                @B.block.vector
                def _(e, t4=t4, oc=oc):
                    e.wait_ge(B.tT, mm["att", t4, oc])
                    mm["fse", t4, oc] = S.inc(
                        e.tensor_add(fs[:, oc * 512:(oc + 1) * 512], pa[oc % 2][:, :],
                                     B.stream[:, oc * NT + HALO + t4 * 512: oc * NT + HALO + t4 * 512 + 512]),
                        B.vF, 1)

            for tc in range(4):
                gt = t4 * 4 + tc

                @B.block.sync
                def _(e, t4=t4, tc=tc, gt=gt):
                    if gt >= 2:
                        e.wait_ge(B.vQ, mm["oadd", gt - 2])
                    r0 = HALO + t4 * 512 + tc * 128
                    S.inc(e.dma_start(out=x0t[gt % 2][:, :], in_=B.inp["x0"][r0:r0 + 128, :]), B.dS, 16)
                    mm["x0l", gt] = S.val(B.dS)

                for oc in range(8):
                    gg = gt * 8 + oc

                    @B.block.tensor
                    def _(e, t4=t4, tc=tc, gt=gt, oc=oc, gg=gg):
                        e.wait_ge(B.vF, mm["fse", t4, oc])
                        if gg >= 2:
                            e.wait_ge(B.vQ, mm["oev", gg - 2])
                        ins = e.transpose(tp[gg % 2][:, 0:128],
                                          fs[:, oc * 512 + tc * 128: oc * 512 + (tc + 1) * 128],
                                          B.identf[:, :])
                        mm["ftr", gt, oc] = S.inc(ins, B.tT, 1)

                    @B.block.vector
                    def _(e, t4=t4, tc=tc, gt=gt, oc=oc, gg=gg):
                        if oc == 0:
                            e.wait_ge(B.dS, mm["x0l", gt])
                            if gt >= 2:
                                e.wait_ge(B.dS, mm["osave", gt - 2])
                        e.wait_ge(B.tT, mm["ftr", gt, oc])
                        mm["oev", gg] = S.inc(
                            e.tensor_add(ot[gt % 2][:, oc * 128:(oc + 1) * 128], tp[gg % 2][:, 0:128],
                                         x0t[gt % 2][:, oc * 128:(oc + 1) * 128]), B.vQ, 1)
                        if oc == 7:
                            mm["oadd", gt] = S.val(B.vQ)

                @B.block.sync
                def _(e, gt=gt):
                    e.wait_ge(B.vQ, mm["oadd", gt])
                    S.inc(e.dma_start(out=B.out[gt * 128:(gt + 1) * 128, :], in_=ot[gt % 2][:, :]),
                          B.dS, 16)
                    mm["osave", gt] = S.val(B.dS)

        @B.block.sync
        def _(e):
            e.wait_ge(B.dS, mm["osave", 15])



def kernel(**inputs):
    consts = _make_consts()
    in_maps = [_stage_core(c, inputs, consts) for c in range(8)]
    B = build_program(DEBUG_TAPS)
    res = run_bass_kernel_spmd(B.nc, in_maps, core_ids=list(range(8)))
    out = np.zeros((2, 8192, 1024), np.float32)
    for c in range(8):
        b, p = c // 4, c % 4
        out[b, p * P:(p + 1) * P] = res.results[c]["out"]
    if DEBUG_TAPS:
        kernel.last_taps = [{k.replace("tap_", ""): v for k, v in r.items()
                             if k.startswith("tap_") or k == "og_d"} for r in res.results]
    return out


# version 10
# speedup vs baseline: 1.0224x; 1.0224x over previous
import contextlib
import numpy as np
import ml_dtypes
import concourse.bass as bass
import concourse.mybir as mybir
from concourse.bass_utils import run_bass_kernel_spmd

F32 = mybir.dt.float32
BF16 = mybir.dt.bfloat16
AF = mybir.ActivationFunctionType
ALU = mybir.AluOpType
AX = mybir.AxisListType
AP = bass.AP

P = 2048
HALO = 656
NT = P + HALO
D = 1024
NEG = -1.0e9
WIN = 512
NKV = (P + 640) // 128
NQ = (P + 128) // 128
KW = NKV * 128
RG = [[0, 1, 2, 3], [4, 5, 6, 7]]

DEBUG_TAPS = ()
H2N_GLU = None
DEV_SKIP = ()
PB_CHUNKS = 16
PB_STAGE = 4



def _make_consts():
    shared = {}
    shared["ident"] = np.eye(128, dtype=ml_dtypes.bfloat16)
    r = np.arange(128)
    up01 = (r[None, :] >= r[:, None])
    shared["tri_up01"] = up01.astype(ml_dtypes.bfloat16)
    lo01 = (r[None, :] <= r[:, None])
    shared["addm_lo"] = np.where(~lo01, np.float32(NEG), 0).astype(ml_dtypes.bfloat16)
    shared["addm_up"] = np.where(~up01, np.float32(NEG), 0).astype(ml_dtypes.bfloat16)
    iota = r.astype(np.float32)[:, None]
    shared["iota_col"] = iota
    shared["iota513"] = (iota + 513.0)

    per_core = []
    for c in range(8):
        b, p = c // 4, c % 4
        s = p * P
        d = {}
        d["halo_gate"] = np.full((128, 1), 0.0 if p == 0 else 1.0, np.float32)
        ms = np.zeros((128, NQ), np.float32)
        for qt in range(-1, 16):
            col = qt + 1
            st = iota[:, 0].copy()
            if p == 0 and 0 <= qt <= 3:
                st = np.maximum(st, 512 - 128 * qt)
            ms[:, col] = st
        d["s_mstart"] = ms
        lo01m = np.where(r[None, :] > r[:, None], np.float32(NEG), 0)
        up01m = np.where(r[None, :] < r[:, None], np.float32(NEG), 0)
        smask = np.zeros((5, 128, 640), np.float32)
        smask[:, :, 0:128] += up01m
        smask[:, :, 512:640] += lo01m
        if p == 0:
            for itt in range(5):
                cols = np.arange(640)
                bad = (cols + itt * 128 >= 128) & (cols + itt * 128 < 640)
                smask[itt][:, bad] = NEG
        d["smask"] = smask.astype(ml_dtypes.bfloat16)
        vc = np.ones((128, NKV), np.float32)
        if p == 0:
            vc[:, :5] = 0.0
        d["kv_valid"] = vc
        j = np.arange(128, dtype=np.float64)
        invf = 10000.0 ** (-(j / 128.0))
        t = np.arange(s, s + P, dtype=np.float64)
        ang = invf[:, None] * t[None, :]
        d["ropeq_cos"] = (np.cos(ang) / 16.0).astype(np.float32)
        d["ropeq_sin"] = (np.sin(ang) / 16.0).astype(np.float32)
        d["ropek_cos"] = np.cos(ang).astype(np.float32)
        d["ropek_sin"] = np.sin(ang).astype(np.float32)
        d["ropek_cos_tm"] = np.ascontiguousarray(np.cos(ang).T.astype(np.float32))
        d["ropek_sin_tm"] = np.ascontiguousarray(np.sin(ang).T.astype(np.float32))
        d["pgate"] = np.broadcast_to((np.arange(4) < p).astype(np.float32)[None, :], (128, 4)).copy()
        hi = np.arange(0, 64, 2, dtype=np.float64)
        invx = 10000.0 ** (-hi / 64.0)
        tt = np.arange(s - HALO, s + P, dtype=np.float64)
        fr = tt[:, None] * invx[None, :]
        base = (hi + 0.4 * 64.0) / (1.4 * 64.0)
        power = (tt - 8192 // 2) / 256.0
        scale = base[None, :] ** power[:, None]
        cos2 = np.concatenate([np.cos(fr)] * 2, 1)
        sin2 = np.concatenate([np.sin(fr)] * 2, 1)
        sc2 = np.concatenate([scale] * 2, 1)
        csq = cos2 * sc2 * 8.0
        ssq = sin2 * sc2 * 8.0
        csk = cos2 / sc2
        ssk = sin2 / sc2
        ssmq = np.concatenate([-ssq[:, 32:64], ssq[:, 0:32]], 1)
        ssmk = np.concatenate([-ssk[:, 32:64], ssk[:, 0:32]], 1)
        d["xpos"] = np.concatenate([csq, ssmq, csk, ssmk], 1).astype(np.float32)
        per_core.append(d)
    return {"shared": shared, "per_core": per_core}


def _stage_core(c, inp, consts):
    b, p = c // 4, c % 4
    s = p * P
    x0 = np.asarray(inp["L0_attn_hidden_states"])[b]
    x0c = np.zeros((NT, D), np.float32)
    lo = s - HALO
    x0c[max(0, -lo):] = x0[max(lo, 0): s + P]
    m = {"x0": np.ascontiguousarray(x0c)}
    for k, v in inp.items():
        if k == "L0_attn_hidden_states":
            continue
        m[k] = np.ascontiguousarray(np.asarray(v, np.float32))
    m["lnvec"] = np.stack([np.stack([m["L0_ffn_ln_g"][l], m["L0_ffn_ln_b"][l]]) for l in range(2)])
    b1 = m["L0_ffn_b1"]
    b1a = np.zeros((2, 22, 128), np.float32); b1g = np.zeros((2, 22, 128), np.float32)
    b1a.reshape(2, -1)[:, :2730] = b1[:, :2730]
    b1g.reshape(2, -1)[:, :2730] = b1[:, 2730:]
    m["b1a_cols"] = np.ascontiguousarray(b1a.transpose(0, 2, 1))
    m["b1g_cols"] = np.ascontiguousarray(b1g.transpose(0, 2, 1))
    m["b2_cols"] = np.ascontiguousarray(m["L0_ffn_b2"].reshape(2, 8, 128).transpose(0, 2, 1))
    m["gn_rep"] = np.ascontiguousarray(np.broadcast_to(m["L0_attn_gnorm_w"][None, :], (128, 256)))
    m["qs_rep"] = np.ascontiguousarray(np.broadcast_to(m["L0_ffn_q_scale"][:, None, :], (2, 128, 64)))
    m["ks_rep"] = np.ascontiguousarray(np.broadcast_to(m["L0_ffn_k_scale"][:, None, :], (2, 128, 64)))
    w1 = m["L0_ffn_w1"]
    w1b = np.zeros((2, 2, 22, 1024, 128), np.float32)
    for l in range(2):
        for half in range(2):
            colbase = 2730 * half
            for j in range(22):
                lo2 = colbase + j * 128
                hi2 = min(colbase + 2730, lo2 + 128)
                w1b[l, half, j, :, 0:hi2 - lo2] = w1[l][:, lo2:hi2]
    m["w1_blk"] = w1b
    w2 = m["L0_ffn_w2"]
    w2b = np.zeros((2, 22, 128, 1024), np.float32)
    for l in range(2):
        for j in range(22):
            lo2 = j * 128
            hi2 = min(2730, lo2 + 128)
            w2b[l, j, 0:hi2 - lo2, :] = w2[l][lo2:hi2, :]
    m["w2_blk"] = w2b
    m["skw2aug"] = np.concatenate([m["L0_attn_sk_w2"], m["L0_attn_sk_b2"][None]], 0)
    m["svw2aug"] = np.concatenate([m["L0_attn_sv_w2"], m["L0_attn_sv_b2"][None]], 0)
    m.update(consts["shared"])
    m.update(consts["per_core"][c])
    return m



class Sched:
    def __init__(self):
        self.n = {}

    def inc(self, ins, sem, k=1):
        ins.then_inc(sem, k)
        self.n[sem.name] = self.n.get(sem.name, 0) + k
        return self.n[sem.name]

    def bump(self, sem, k):
        self.n[sem.name] = self.n.get(sem.name, 0) + k
        return self.n[sem.name]

    def val(self, sem):
        return self.n.get(sem.name, 0)

    def raw(self, e, ins, sem):
        self.inc(ins, sem, 1)
        e.wait_ge(sem, self.val(sem))


class Bld:
    pass


def _barrier(B):
    B.phase += 1
    tgt = B.phase
    pairs = [("sync", B.phS), ("tensor", B.phT), ("vector", B.phV),
             ("scalar", B.phA), ("gpsimd", B.phG)]
    for name, sem in pairs:
        def _mk(sm):
            def _f(e):
                B.sched.inc(e.nop(), sm, 1)
                for _n2, s2 in pairs:
                    e.wait_ge(s2, tgt)
            return _f
        getattr(B.block, name)(_mk(sem))


def build_program(taps=()):
    es = contextlib.ExitStack()
    nc = bass.Bass("TRN2", target_bir_lowering=False)
    B = Bld()
    B.nc, B.es, B.sched = nc, es, Sched()
    B.phase = 0
    B.taps = set(taps)
    B.tapped = {}

    dt_ = {}
    def din(name, shape, dt=F32):
        dt_[name] = nc.dram_tensor(name, list(shape), dt, kind="ExternalInput")
    din("x0", (NT, D))
    for nm, sh in [("L0_attn_Wq", (D, D)), ("L0_attn_Wk", (D, D)), ("L0_attn_Wv", (D, D)),
                   ("L0_attn_Wg", (D, D)), ("L0_attn_Wo", (D, D)),
                   ("L0_attn_sk_w1", (D, 16)), ("L0_attn_sk_w2", (16, 256)), ("L0_attn_sk_b2", (256,)),
                   ("L0_attn_sv_w1", (D, 16)), ("L0_attn_sv_w2", (16, 256)), ("L0_attn_sv_b2", (256,)),
                   ("L0_attn_gnorm_w", (256,)),
                   ("L0_ffn_wqkv", (2, D, 1536)), ("L0_ffn_q_scale", (2, 64)), ("L0_ffn_k_scale", (2, 64)),
                   ("L0_ffn_wo", (2, 512, D)), ("L0_ffn_ln_g", (2, D)), ("L0_ffn_ln_b", (2, D)),
                   ("L0_ffn_w1", (2, D, 5460)), ("L0_ffn_b1", (2, 5460)),
                   ("L0_ffn_w2", (2, 2730, D)), ("L0_ffn_b2", (2, D)),
                   ("w1_blk", (2, 2, 22, 1024, 128)), ("w2_blk", (2, 22, 128, 1024)),
                   ("lnvec", (2, 2, D)), ("b1a_cols", (2, 128, 22)), ("b1g_cols", (2, 128, 22)),
                   ("b2_cols", (2, 128, 8)),
                   ("qs_rep", (2, 128, 64)), ("ks_rep", (2, 128, 64)),
                   ("skw2aug", (17, 256)), ("svw2aug", (17, 256)),
                   ("iota_col", (128, 1)), ("iota513", (128, 1)),
                   ("halo_gate", (128, 1)), ("s_mstart", (128, NQ)), ("kv_valid", (128, NKV)),
                   ("ropeq_cos", (128, P)), ("ropeq_sin", (128, P)),
                   ("ropek_cos", (128, P)), ("ropek_sin", (128, P)),
                   ("ropek_cos_tm", (P, 128)), ("ropek_sin_tm", (P, 128)),
                   ("pgate", (128, 4)), ("gn_rep", (128, 256)), ("xpos", (NT, 256))]:
        din(nm, sh)
    din("ident", (128, 128), BF16)
    din("tri_up01", (128, 128), BF16)
    din("addm_lo", (128, 128), BF16)
    din("addm_up", (128, 128), BF16)
    din("smask", (5, 128, 640), BF16)
    B.inp = dt_
    B.out = nc.dram_tensor("out", [P, D], F32, kind="ExternalOutput")

    B.ag1_in = nc.dram_tensor("ag1_in", [128, 2560], F32)
    B.kv_d = nc.dram_tensor("kv_d", [128, 16 * 2560], BF16)
    B.ag1_out = nc.dram_tensor("ag1_out", [512, 2560], F32)
    B.ag2_in = nc.dram_tensor("ag2_in", [128, 8 * HALO], F32)
    B.ag2_out = nc.dram_tensor("ag2_out", [512, 8 * HALO], F32)
    B.og_d = nc.dram_tensor("og_d", [128, 8 * P], BF16,
                            kind="ExternalOutput" if "og" in taps else "Internal")
    if "og" in taps:
        B.tapped["og"] = True

    def sem(name):
        return es.enter_context(nc.semaphore(name))
    for nm in ["dS", "dG", "dA", "tT", "tV", "tA", "tG", "cc", "vZ", "aZ", "gZ",
               "phS", "phT", "phV", "phA", "phG", "e1", "e2", "e3", "e4", "e5", "e6",
               "xwS", "vF", "vQ", "vY", "vN", "vC", "vR", "vE", "vM",
               "aP", "aK", "aN", "aE", "gK", "aSq", "vSum"]:
        setattr(B, nm, sem(nm))

    sb = nc.alloc_sbuf_tensor
    B.stream = sb("stream", [128, 8 * NT], F32)
    B.ident = sb("ident_sb", [128, 128], BF16)
    B.tri_up = sb("tri_up_sb", [128, 128], BF16)
    B.addm_lo = sb("addm_lo_sb", [128, 128], BF16)
    B.addm_up = sb("addm_up_sb", [128, 128], BF16)
    B.identf = sb("identf_sb", [128, 128], F32)
    B.ones1 = sb("ones1", [1, 512], BF16)
    B.onescol = sb("onescol", [128, 1], BF16)
    B.onescolf = sb("onescolf", [128, 1], F32)
    B.iota = sb("iota_sb", [128, 1], F32)
    B.iota513 = sb("iota513_sb", [128, 1], F32)
    B.hgate = sb("hgate", [128, 1], F32)
    B.smstart = sb("smstart", [128, NQ], F32)
    B.kvvalid = sb("kvvalid", [128, NKV], F32)
    B.kvvalid_bf = sb("kvvalid_bf", [128, NKV], BF16)
    B.cst = sb("cst", [128, 4], F32)

    with nc.Block() as block:
        B.block = block

        @block.sync
        def _(e):
            for dst, src in [(B.ident, dt_["ident"]), (B.tri_up, dt_["tri_up01"]),
                             (B.addm_lo, dt_["addm_lo"]), (B.addm_up, dt_["addm_up"]),
                             (B.iota, dt_["iota_col"]), (B.iota513, dt_["iota513"]),
                             (B.hgate, dt_["halo_gate"]), (B.smstart, dt_["s_mstart"]),
                             (B.kvvalid, dt_["kv_valid"])]:
                B.sched.inc(e.dma_start(out=dst[:, :], in_=src[:, :]), B.dS, 16)


        @block.vector
        def _(e):
            e.wait_ge(B.dS, B.sched.val(B.dS))
            e.memset(B.ones1[:, :], 1.0)
            e.memset(B.onescol[:, :], 1.0)
            e.memset(B.onescolf[:, :], 1.0)
            e.memset(B.cst[:, 0:1], 1e-5)
            e.tensor_copy(B.identf[:, :], B.ident[:, :])
            e.tensor_copy(B.kvvalid_bf[:, :], B.kvvalid[:, :])

        _barrier(B)
        phase_xn(B)
        _barrier(B)
        if "passA" not in DEV_SKIP:
            phase_passA(B)
        _barrier(B)
        for lyr in (0, 1):
            o0stack = contextlib.ExitStack()
            B.o0 = o0stack.enter_context(nc.sbuf_tensor(f"o0_{lyr}", [128, 4 * NQ * 128], BF16, side="right"))
            phase_ffn_subA(B, lyr=lyr)
            _barrier(B)
            if lyr == 0 and "passB" not in DEV_SKIP:
                phase_passB(B)
                _barrier(B)
            phase_ffn_rest(B, lyr=lyr, o0stack=o0stack)
            _barrier(B)
        phase_final(B)

    es.close()
    return B


def _tap_sbuf(B, name, ap, shape, dtype=F32):
    if name not in B.taps:
        return
    t = B.nc.dram_tensor("tap_" + name, list(shape), dtype, kind="ExternalOutput")
    B.tapped[name] = True
    _barrier(B)

    @B.block.sync
    def _(e):
        B.sched.inc(e.dma_start(out=t[:, :], in_=ap), B.dS, 16)
        e.wait_ge(B.dS, B.sched.val(B.dS))
    _barrier(B)



def phase_xn(B):
    nc, S = B.nc, B.sched
    with contextlib.ExitStack() as ph:
        x0t = [ph.enter_context(nc.sbuf_tensor(f"x0t{i}", [128, D], F32)) for i in range(2)]
        xnt = [ph.enter_context(nc.sbuf_tensor(f"xnt{i}", [128, D], F32)) for i in range(2)]
        scr = ph.enter_context(nc.sbuf_tensor("xn_scr", [128, D], BF16))
        st_ = ph.enter_context(nc.sbuf_tensor("xn_stat", [128, 16], F32))
        scr2 = ph.enter_context(nc.sbuf_tensor("xn_scr2", [128, D], F32))
        tp = [ph.enter_context(nc.psum_tensor(f"xn_ps{i}", [128, 512], F32)) for i in range(2)]

        starts = [i * 128 for i in range(NT // 128)] + ([NT - 128] if NT % 128 else [])
        NTI = len(starts)
        b_dS, b_tV, b_tA, b_tT, b_e1, b_e2 = (S.val(x) for x in (B.dS, B.tV, B.tA, B.tT, B.e1, B.e2))

        @B.block.sync
        def _(e):
            for i in range(NTI):
                if i >= 2:
                    e.wait_ge(B.tV, b_tV + (i - 2) + 1)
                S.inc(e.dma_start(out=x0t[i % 2][:, :],
                                  in_=B.inp["x0"][starts[i]:starts[i] + 128, :]), B.dS, 16)

        def stp(i):
            return st_[:, (i % 2) * 8:(i % 2) * 8 + 8]

        def _evac(e, i):
            for j in range(8):
                g = i * 8 + j
                e.wait_ge(B.tT, b_tT + g + 1)
                S.inc(e.copy(B.stream[:, j * NT + starts[i]: j * NT + starts[i] + 128],
                             tp[g % 2][:, 0:128]), B.e2, 1)

        @B.block.scalar
        def _(e):
            for i in range(NTI):
                e.wait_ge(B.dS, b_dS + (i + 1) * 16)
                st = stp(i)
                S.inc(e.activation(scr[:, :], x0t[i % 2][:, :], AF.Square,
                                   accum_out=st[:, 1:2]), B.tA, 1)
                e.wait_ge(B.e1, b_e1 + i + 1)
                S.inc(e.activation(st[:, 5:6], st[:, 4:5], AF.Sqrt, bias=B.cst[:, 0:1]), B.tA, 1)
                if i >= 1:
                    _evac(e, i - 1)
            _evac(e, NTI - 1)

        @B.block.vector
        def _(e):
            for i in range(NTI):
                st = stp(i)
                e.wait_ge(B.dS, b_dS + (i + 1) * 16)
                S.raw(e, e.tensor_reduce(st[:, 0:1], x0t[i % 2][:, :], AX.X, ALU.add), B.vZ)
                e.wait_ge(B.tA, b_tA + 2 * i + 1)
                e.tensor_scalar(st[:, 2:3], st[:, 0:1], 1.0 / D, None, ALU.mult)
                S.raw(e, e.tensor_scalar(st[:, 3:4], st[:, 1:2], 1.0 / D, None, ALU.mult), B.vZ)
                S.raw(e, e.tensor_tensor(st[:, 7:8], st[:, 2:3], st[:, 2:3], ALU.mult), B.vZ)
                S.inc(e.tensor_tensor(st[:, 4:5], st[:, 3:4], st[:, 7:8], ALU.subtract), B.e1, 1)
                e.wait_ge(B.tA, b_tA + 2 * i + 2)
                S.raw(e, e.reciprocal(st[:, 6:7], st[:, 5:6]), B.vZ)
                if i >= 2:
                    e.wait_ge(B.tT, b_tT + (i - 2) * 8 + 8)
                S.raw(e, e.tensor_scalar(scr2[:, :], x0t[i % 2][:, :], st[:, 2:3], None, ALU.subtract), B.vZ)
                S.inc(e.tensor_scalar(xnt[i % 2][:, :], scr2[:, :], st[:, 6:7], None, ALU.mult), B.tV, 1)

        @B.block.tensor
        def _(e):
            for i in range(NTI):
                e.wait_ge(B.tV, b_tV + i + 1)
                for j in range(8):
                    g = i * 8 + j
                    if g >= 2:
                        e.wait_ge(B.e2, b_e2 + g - 1)
                    S.inc(e.transpose(tp[g % 2][:, 0:128], xnt[i % 2][:, j * 128:(j + 1) * 128],
                                      B.identf[:, :]), B.tT, 1)

    _tap_sbuf(B, "st", st_[:, :], (128, 16))
    _tap_sbuf(B, "xn", B.stream[:, 0:8 * NT], (128, 8 * NT))



def phase_passA(B):
    nc, S = B.nc, B.sched
    with contextlib.ExitStack() as ph:
        sbt = lambda n, sh, dt=F32: ph.enter_context(nc.sbuf_tensor(f"{n}_pa", sh, dt))
        wkb = sbt("wkb", [128, 8 * 1024], BF16)
        wvb = sbt("wvb", [128, 8 * 1024], BF16)
        skw1 = sbt("skw1", [128, 256], BF16)
        skw2 = sbt("skw2", [17, 512], BF16)
        xnb = sbt("xnb", [128, 8 * 512], BF16)
        rct = sbt("rct", [128, 4 * 256])
        ka = sbt("ka", [128, 512])
        kb = sbt("kb", [128, 512])
        khat = sbt("khat", [128, 1024], BF16)
        vtm = sbt("vtm", [128, 1024], BF16)
        t1aug = sbt("t1aug", [17, 1024], BF16)
        scf = sbt("scf", [128, 512])
        wkv = sbt("wkv", [128, 512], BF16)
        s1A = [sbt(f"s1A{i}", [128, 1024]) for i in range(2)]
        s2A = [sbt(f"s2A{i}", [64, 1024]) for i in range(2)]
        zA = [sbt(f"zA{i}", [1, 512]) for i in range(2)]
        onesbf = sbt("onesbf", [128, 1], BF16)
        pj = ph.enter_context(nc.psum_tensor("pj_pa", [128, 1024], F32))
        skbank = ph.enter_context(nc.psum_tensor("skbank_pa", [128, 512], F32))
        st1 = ph.enter_context(nc.psum_tensor("st1_pa", [128, 1024], F32))
        st2 = ph.enter_context(nc.psum_tensor("st2_pa", [64, 1024], F32))
        stz = ph.enter_context(nc.psum_tensor("stz_pa", [1, 512], F32))

        b = {k: S.val(getattr(B, k)) for k in ["dS", "dG", "tT", "vF", "vQ", "vN", "aP", "cc"]}
        mm = {}

        @B.block.gpsimd
        def _(e):
            for fc in range(8):
                S.inc(e.dma_start(out=wkb[:, fc * 1024:(fc + 1) * 1024],
                                  in_=B.inp["L0_attn_Wk"][fc * 128:(fc + 1) * 128, :]), B.dG, 16)
                S.inc(e.dma_start(out=wvb[:, fc * 1024:(fc + 1) * 1024],
                                  in_=B.inp["L0_attn_Wv"][fc * 128:(fc + 1) * 128, :]), B.dG, 16)
            S.inc(e.dma_start(out=skw1[:, 0:128],
                              in_=AP(B.inp["L0_attn_sk_w1"], 0, [[16, 128], [2048, 8], [1, 16]])), B.dG, 16)
            S.inc(e.dma_start(out=skw1[:, 128:256],
                              in_=AP(B.inp["L0_attn_sv_w1"], 0, [[16, 128], [2048, 8], [1, 16]])), B.dG, 16)
            S.inc(e.dma_start(out=skw2[:, 0:256], in_=B.inp["skw2aug"][:, :]), B.dG, 16)
            S.inc(e.dma_start(out=skw2[:, 256:512], in_=B.inp["svw2aug"][:, :]), B.dG, 16)

        @B.block.vector
        def _(e):
            e.memset(onesbf[:, :], 1.0)
            e.memset(s1A[1][:, :], 0.0)
            e.memset(s2A[1][0:64, :], 0.0)
            e.memset(zA[1][0:1, :], 0.0)
            mm["init"] = S.inc(e.memset(t1aug[0:17, :], 1.0), B.vF, 1)

        for st in range(4):
            @B.block.sync
            def _(e, st=st):
                if st > 0:
                    e.wait_ge(B.vF, mm["rope", st * 4 - 1])
                for c in range(4):
                    r0 = st * 512 + c * 128
                    S.inc(e.dma_start(out=rct[:, c * 256:c * 256 + 128],
                                      in_=B.inp["ropek_cos_tm"][r0:r0 + 128, :]), B.dS, 16)
                    S.inc(e.dma_start(out=rct[:, c * 256 + 128:c * 256 + 256],
                                      in_=B.inp["ropek_sin_tm"][r0:r0 + 128, :]), B.dS, 16)
                mm["rctd", st] = S.val(B.dS)

            @B.block.vector
            def _(e, st=st):
                if st > 0:
                    e.wait_ge(B.tT, mm["stu", st * 4 - 1])
                for ci in range(8):
                    e.tensor_copy(xnb[:, ci * 512:(ci + 1) * 512],
                                  B.stream[:, ci * NT + HALO + st * 512: ci * NT + HALO + st * 512 + 512])
                mm["xnb", st] = S.inc(e.engine_nop(), B.vQ, 1)

            @B.block.tensor
            def _(e, st=st):
                e.wait_ge(B.vQ, mm["xnb", st])
                if st == 0:
                    e.wait_ge(B.dG, b["dG"] + 20 * 16)
                    e.wait_ge(B.vF, mm["init"])
                else:
                    e.wait_ge(B.vF, mm["clip", st * 4 - 1])
                ins = None
                for fc in range(8):
                    ins = e.matmul(skbank[0:16, 0:512], skw1[:, fc * 16:(fc + 1) * 16],
                                   xnb[:, fc * 512:(fc + 1) * 512], start=(fc == 0), stop=(fc == 7))
                mm["t1", st, 0] = S.inc(ins, B.tT, 1)

            @B.block.vector
            def _(e, st=st):
                e.wait_ge(B.tT, mm["t1", st, 0])
                mm["t1e", st, 0] = S.inc(
                    e.tensor_copy(t1aug[0:16, 0:512], skbank[0:16, 0:512]), B.vF, 1)

            @B.block.tensor
            def _(e, st=st):
                e.wait_ge(B.vF, mm["t1e", st, 0])
                ins = None
                for fc in range(8):
                    ins = e.matmul(skbank[0:16, 0:512], skw1[:, 128 + fc * 16: 128 + (fc + 1) * 16],
                                   xnb[:, fc * 512:(fc + 1) * 512], start=(fc == 0), stop=(fc == 7))
                mm["t1", st, 1] = S.inc(ins, B.tT, 1)

            @B.block.vector
            def _(e, st=st):
                e.wait_ge(B.tT, mm["t1", st, 1])
                mm["t1e", st, 1] = S.inc(
                    e.tensor_copy(t1aug[0:16, 512:1024], skbank[0:16, 0:512]), B.vF, 1)

            for c in range(4):
                g = st * 4 + c

                @B.block.tensor
                def _(e, st=st, c=c, g=g):
                    if g > 0:
                        e.wait_ge(B.vF, mm["vtme", g - 1])
                    ins = None
                    for fc in range(8):
                        for hb in range(2):
                            ins = e.matmul(pj[:, hb * 512:(hb + 1) * 512],
                                           xnb[:, fc * 512 + c * 128: fc * 512 + (c + 1) * 128],
                                           wkb[:, fc * 1024 + hb * 512: fc * 1024 + (hb + 1) * 512],
                                           start=(fc == 0), stop=(fc == 7))
                    mm["kp", g] = S.inc(ins, B.tT, 1)
                    e.wait_ge(B.vF, mm["t1e", st, 1])
                    if g > 0:
                        e.wait_ge(B.vF, mm["clip", g - 1])
                    e.matmul(skbank[:, 0:256], t1aug[:, c * 128:(c + 1) * 128], skw2[:, 0:256],
                             start=True, stop=True)
                    ins = e.matmul(skbank[:, 256:512], t1aug[:, 512 + c * 128: 512 + (c + 1) * 128],
                                   skw2[:, 256:512], start=True, stop=True)
                    mm["sk", g] = S.inc(ins, B.tT, 1)

                @B.block.vector
                def _(e, st=st, c=c, g=g):
                    e.wait_ge(B.tT, mm["kp", g])
                    e.wait_ge(B.dS, mm["rctd", st])
                    cosb = rct[:, c * 256:c * 256 + 128].unsqueeze(1).broadcast_to([128, 4, 128])
                    sinb = rct[:, c * 256 + 128:c * 256 + 256].unsqueeze(1).broadcast_to([128, 4, 128])
                    pjv = pj[:, :].rearrange("p (h two j) -> p h two j", two=2, j=128)
                    khv = khat[:, :].rearrange("p (h two j) -> p h two j", two=2, j=128)
                    kav = ka[:, :].rearrange("p (h j) -> p h j", j=128)
                    kbv = kb[:, :].rearrange("p (h j) -> p h j", j=128)
                    if g > 0:
                        e.wait_ge(B.tT, mm["stu", g - 1])
                        e.wait_ge(B.dS, mm["spill", g - 1])
                    e.tensor_tensor(kav, pjv[:, :, 0, :], cosb, ALU.mult)
                    S.raw(e, e.tensor_tensor(kbv, pjv[:, :, 1, :], sinb, ALU.mult), B.vZ)
                    e.tensor_tensor(khv[:, :, 0, :], kav, kbv, ALU.subtract)
                    e.tensor_tensor(kav, pjv[:, :, 1, :], cosb, ALU.mult)
                    S.raw(e, e.tensor_tensor(kbv, pjv[:, :, 0, :], sinb, ALU.mult), B.vZ)
                    mm["rope", g] = S.inc(e.tensor_add(khv[:, :, 1, :], kav, kbv), B.vF, 1)
                    e.wait_ge(B.tT, mm["sk", g])
                    mm["clip", g] = S.inc(
                        e.tensor_scalar(scf[:, :], skbank[:, :], -32.0, 32.0, ALU.max, ALU.min), B.vF, 1)

                @B.block.scalar
                def _(e, g=g):
                    e.wait_ge(B.vF, mm["clip", g])
                    if g > 0:
                        e.wait_ge(B.tT, mm["stu", g - 1])
                        e.wait_ge(B.dS, mm["spill", g - 1])
                    mm["exp", g] = S.inc(e.activation(wkv[:, :], scf[:, :], AF.Exp), B.aP, 1)

                @B.block.tensor
                def _(e, st=st, c=c, g=g):
                    e.wait_ge(B.vF, mm["rope", g])
                    ins = None
                    for fc in range(8):
                        for hb in range(2):
                            ins = e.matmul(pj[:, hb * 512:(hb + 1) * 512],
                                           xnb[:, fc * 512 + c * 128: fc * 512 + (c + 1) * 128],
                                           wvb[:, fc * 1024 + hb * 512: fc * 1024 + (hb + 1) * 512],
                                           start=(fc == 0), stop=(fc == 7))
                    mm["vp", g] = S.inc(ins, B.tT, 1)

                @B.block.vector
                def _(e, g=g):
                    e.wait_ge(B.tT, mm["vp", g])
                    if g > 0:
                        e.wait_ge(B.dS, mm["spill", g - 1])
                    mm["vtme", g] = S.inc(e.tensor_copy(vtm[:, :], pj[:, :]), B.vF, 1)

                @B.block.tensor
                def _(e, g=g):
                    e.wait_ge(B.vF, mm["vtme", g])
                    e.wait_ge(B.aP, mm["exp", g])
                    if g > 0:
                        e.wait_ge(B.vF, mm["sacc", g - 1])
                    ins = None
                    for h in range(4):
                        for c2 in range(2):
                            e.matmul(st1[:, (2 * h + c2) * 64:(2 * h + c2) * 64 + 64],
                                     khat[:, h * 256 + c2 * 128: h * 256 + (c2 + 1) * 128],
                                     wkv[:, h * 64:(h + 1) * 64], start=True, stop=True)
                        e.matmul(st2[0:64, h * 256:(h + 1) * 256], wkv[:, 256 + h * 64: 256 + (h + 1) * 64],
                                 vtm[:, h * 256:(h + 1) * 256], start=True, stop=True)
                        e.matmul(stz[0:1, h * 64:(h + 1) * 64], onesbf[:, :], wkv[:, h * 64:(h + 1) * 64],
                                 start=True, stop=True)
                        ins = e.matmul(stz[0:1, 256 + h * 64: 256 + (h + 1) * 64], onesbf[:, :],
                                       wkv[:, 256 + h * 64: 256 + (h + 1) * 64],
                                       start=True, stop=True)
                    mm["stu", g] = S.inc(ins, B.tT, 1)

                @B.block.vector
                def _(e, g=g):
                    e.wait_ge(B.tT, mm["stu", g])
                    e.tensor_tensor(s1A[g % 2][:, :], st1[:, :], s1A[(g + 1) % 2][:, :], ALU.add)
                    e.tensor_tensor(s2A[g % 2][0:64, :], st2[0:64, :], s2A[(g + 1) % 2][0:64, :], ALU.add)
                    mm["sacc", g] = S.inc(
                        e.tensor_tensor(zA[g % 2][0:1, :], stz[0:1, :], zA[(g + 1) % 2][0:1, :],
                                        ALU.add), B.vF, 1)

                @B.block.sync
                def _(e, g=g):
                    e.wait_ge(B.vF, mm["vtme", g])
                    e.wait_ge(B.vF, mm["rope", g])
                    e.wait_ge(B.aP, mm["exp", g])
                    S.inc(e.dma_start(out=B.kv_d[:, g * 2560: g * 2560 + 1024], in_=khat[:, :]),
                          B.dS, 16)
                    S.inc(e.dma_start(out=B.kv_d[:, g * 2560 + 1024: g * 2560 + 2048], in_=vtm[:, :]),
                          B.dS, 16)
                    S.inc(e.dma_start(out=B.kv_d[:, g * 2560 + 2048: g * 2560 + 2560], in_=wkv[:, :]),
                          B.dS, 16)
                    mm["spill", g] = S.val(B.dS)

        @B.block.sync
        def _(e):
            e.wait_ge(B.vF, mm["sacc", 15])
            S.inc(e.dma_start(out=B.ag1_in[:, 0:1024], in_=s1A[1][:, :]), B.dS, 16)
            S.inc(e.dma_start(out=B.ag1_in[0:64, 1024:2048], in_=s2A[1][0:64, :]), B.dS, 16)
            S.inc(e.dma_start(out=B.ag1_in[0:1, 2048:2560], in_=zA[1][0:1, :]), B.dS, 16)

        @B.block.gpsimd
        def _(e):
            e.wait_ge(B.dS, S.val(B.dS))
            S.inc(e.collective_compute("AllGather", ALU.bypass, replica_groups=RG,
                                       ins=[B.ag1_in.ap()], outs=[B.ag1_out.ap()]), B.cc, 1)


def phase_passB(B):
    nc, S = B.nc, B.sched
    mm = {}
    phx = contextlib.ExitStack()
    sbx = lambda n, sh, dt=F32: phx.enter_context(nc.sbuf_tensor(f"{n}_pbx", sh, dt))
    s1f = [sbx(f"s1f{i}", [128, 1024]) for i in range(2)]
    s2f = [sbx(f"s2f{i}", [64, 1024]) for i in range(2)]
    zf = [sbx(f"zf{i}", [1, 512]) for i in range(2)]
    s1sb = sbx("s1sb", [128, 1024], BF16)
    s2sb = sbx("s2sb", [64, 1024], BF16)
    zsbb = sbx("zsbb", [1, 512], BF16)
    with contextlib.ExitStack() as ph0:
        acc = [ph0.enter_context(nc.sbuf_tensor(f"pacc{i}", [128, 2560], F32)) for i in range(3)]
        pg = ph0.enter_context(nc.sbuf_tensor("pg_sb", [128, 4], F32))

        @B.block.gpsimd
        def _(e):
            e.wait_ge(B.cc, S.val(B.cc))
            S.inc(e.dma_start(out=pg[:, :], in_=B.inp["pgate"][:, :]), B.dG, 16)
            for r in range(3):
                S.inc(e.dma_start(out=acc[r][:, :], in_=B.ag1_out[r * 128:(r + 1) * 128, :]), B.dG, 16)

        @B.block.vector
        def _(e):
            e.wait_ge(B.dG, S.val(B.dG))
            e.tensor_scalar(s1f[1][:, :], acc[0][:, 0:1024], pg[:, 0:1], None, ALU.mult)
            e.tensor_scalar(s2f[1][0:64, :], acc[0][0:64, 1024:2048], pg[0:64, 0:1], None, ALU.mult)
            S.raw(e, e.tensor_scalar(zf[1][0:1, :], acc[0][0:1, 2048:2560], pg[0:1, 0:1],
                                     None, ALU.mult), B.vZ)
            e.scalar_tensor_tensor(s1f[0][:, :], acc[1][:, 0:1024], pg[:, 1:2], s1f[1][:, :],
                                   ALU.mult, ALU.add)
            e.scalar_tensor_tensor(s2f[0][0:64, :], acc[1][0:64, 1024:2048], pg[0:64, 1:2],
                                   s2f[1][0:64, :], ALU.mult, ALU.add)
            S.raw(e, e.scalar_tensor_tensor(zf[0][0:1, :], acc[1][0:1, 2048:2560], pg[0:1, 1:2],
                                            zf[1][0:1, :], ALU.mult, ALU.add), B.vZ)
            e.scalar_tensor_tensor(s1f[1][:, :], acc[2][:, 0:1024], pg[:, 2:3], s1f[0][:, :],
                                   ALU.mult, ALU.add)
            e.scalar_tensor_tensor(s2f[1][0:64, :], acc[2][0:64, 1024:2048], pg[0:64, 2:3],
                                   s2f[0][0:64, :], ALU.mult, ALU.add)
            S.raw(e, e.scalar_tensor_tensor(zf[1][0:1, :], acc[2][0:1, 2048:2560], pg[0:1, 2:3],
                                            zf[0][0:1, :], ALU.mult, ALU.add), B.vZ)
            e.tensor_copy(s1sb[:, :], s1f[1][:, :])
            e.tensor_copy(s2sb[0:64, :], s2f[1][0:64, :])
            mm["sinit"] = S.inc(e.tensor_copy(zsbb[0:1, :], zf[1][0:1, :]), B.vF, 1)

    _barrier(B)

    with contextlib.ExitStack() as ph:
        sbt = lambda n, sh, dt=F32: ph.enter_context(nc.sbuf_tensor(f"{n}_pb", sh, dt))
        wqb = sbt("wqb", [128, 8 * 1024], BF16)
        wgb = sbt("wgb", [128, 8 * 1024], BF16)
        xnb = sbt("xnb", [128, 8 * 512], BF16)
        rpt = sbt("rpt", [128, 256], F32)
        qf = sbt("qf", [128, 8 * 128], BF16)
        kf = sbt("kf", [128, 8 * 128], BF16)
        qfb = qf[:, :].bitcast(F32)
        kfb = kf[:, :].bitcast(F32)
        gtm = sbt("gtm", [128, 1024], BF16)
        ld_ = [sbt(f"ldkv{i}", [128, 2560], BF16) for i in range(3)]
        wvT = sbt("wvT", [64, 512], BF16)
        stm = sbt("stm", [128, 512], BF16)
        ptbi = sbt("ptbi", [128, 256], BF16)
        rz = sbt("rz", [128, 512])
        lg = sbt("lg", [128, 256])
        ev = sbt("ev", [128, 256])
        nmx = sbt("nmx", [128, 12])
        ptb = sbt("ptb", [64, 512], BF16)
        hv1 = sbt("hv1", [128, 512])
        hv2 = sbt("hv2", [128, 512])
        ssq = sbt("ssq", [128, 16])
        gnr = sbt("gnr", [128, 256], BF16)
        ogsb = sbt("ogsb", [128, 1024], BF16)
        ogfm = sbt("ogfm", [128, 1024], BF16)
        onesbf = sbt("onesbf2", [128, 1], BF16)
        pq2 = ph.enter_context(nc.psum_tensor("pq2_pb", [128, 1024], F32))
        pk2 = ph.enter_context(nc.psum_tensor("pk2_pb", [128, 1024], F32))
        st1 = ph.enter_context(nc.psum_tensor("st1_pb", [128, 1024], F32))
        st2 = ph.enter_context(nc.psum_tensor("st2_pb", [64, 1024], F32))
        pq2b = pq2[:, :].bitcast(BF16)
        pk2b = pk2[:, :].bitcast(BF16)

        b = {k: S.val(getattr(B, k)) for k in ["dS", "dG", "tT", "vF", "vQ", "aP", "cc"]}

        @B.block.gpsimd
        def _(e):
            for fc in range(8):
                for w_, nmW in ((wqb, "L0_attn_Wq"), (wgb, "L0_attn_Wg")):
                    S.inc(e.dma_start(out=w_[:, fc * 1024:(fc + 1) * 1024],
                                      in_=B.inp[nmW][fc * 128:(fc + 1) * 128, :]), B.dG, 16)
            S.inc(e.dma_start(out=gnr[:, :], in_=B.inp["gn_rep"][:, :]), B.dG, 16)
            mm["wload"] = S.val(B.dG)

        @B.block.vector
        def _(e):
            mm["minit"] = S.inc(e.memset(onesbf[:, :], 1.0), B.vF, 1)

        for st4 in range(4):
            if st4 * 4 >= PB_CHUNKS:
                break

            @B.block.vector
            def _(e, st4=st4):
                if st4 > 0:
                    e.wait_ge(B.tT, mm["stu", st4 * 4 - 1])
                for ci in range(8):
                    e.tensor_copy(xnb[:, ci * 512:(ci + 1) * 512],
                                  B.stream[:, ci * NT + HALO + st4 * 512: ci * NT + HALO + st4 * 512 + 512])
                mm["xnb", st4] = S.inc(e.engine_nop(), B.vQ, 1)

            for c in range(4):
                g = st4 * 4 + c
                if g >= PB_CHUNKS:
                    break

                @B.block.sync
                def _(e, st4=st4, c=c, g=g):
                    if g >= 1:
                        e.wait_ge(B.vF, mm["qrope", g - 1, 3])
                    t0 = st4 * 512 + c * 128
                    for i, nmT in enumerate(["ropeq_cos", "ropeq_sin"]):
                        S.inc(e.dma_start(out=rpt[:, i * 128:(i + 1) * 128],
                                          in_=B.inp[nmT][:, t0:t0 + 128]), B.dS, 16)
                    mm["rpt", g] = S.val(B.dS)

                @B.block.gpsimd
                def _(e, g=g):
                    if g >= 3:
                        e.wait_ge(B.tT, mm["scend", g - 3])
                    S.inc(e.dma_start(out=ld_[g % 3][:, :],
                                      in_=B.kv_d[:, g * 2560:(g + 1) * 2560]), B.dG, 16)
                    mm["ldkv", g] = S.val(B.dG)

                @B.block.tensor
                def _(e, st4=st4, c=c, g=g):
                    e.wait_ge(B.vQ, mm["xnb", st4])
                    if g == 0:
                        e.wait_ge(B.dG, mm["wload"])
                        e.wait_ge(B.vF, mm["sinit"])
                        e.wait_ge(B.vF, mm["minit"])
                    else:
                        e.wait_ge(B.vF, mm["zup", g - 1])
                        e.wait_ge(B.vF, mm["pte", g - 1, 3])
                    ins = None
                    for dc in range(8):
                        for fc in range(8):
                            ins = e.matmul(pq2[:, dc * 128:(dc + 1) * 128],
                                           wqb[:, fc * 1024 + dc * 128: fc * 1024 + (dc + 1) * 128],
                                           xnb[:, fc * 512 + c * 128: fc * 512 + (c + 1) * 128],
                                           start=(fc == 0), stop=(fc == 7))
                    mm["qp", g] = S.inc(ins, B.tT, 1)

                @B.block.vector
                def _(e, g=g):
                    e.wait_ge(B.tT, mm["qp", g])
                    e.wait_ge(B.dS, mm["rpt", g])
                    cosb = rpt[:, 0:128].unsqueeze(1).broadcast_to([128, 4, 128])
                    sinb = rpt[:, 128:256].unsqueeze(1).broadcast_to([128, 4, 128])
                    pv = pq2[:, :].rearrange("p (hh two t) -> p hh two t", two=2, t=128)
                    qv = qf[:, :].rearrange("p (hh two t) -> p hh two t", two=2, t=128)
                    h1 = hv1[:, :].rearrange("p (hh t) -> p hh t", t=128)
                    h2 = hv2[:, :].rearrange("p (hh t) -> p hh t", t=128)
                    e.tensor_tensor(h1, pv[:, :, 0, :], cosb, ALU.mult)
                    S.raw(e, e.tensor_tensor(h2, pv[:, :, 1, :], sinb, ALU.mult), B.vZ)
                    e.tensor_tensor(qv[:, :, 0, :], h1, h2, ALU.subtract)
                    e.tensor_tensor(h1, pv[:, :, 1, :], cosb, ALU.mult)
                    S.raw(e, e.tensor_tensor(h2, pv[:, :, 0, :], sinb, ALU.mult), B.vZ)
                    mm["qrope", g, 3] = S.inc(e.tensor_add(qv[:, :, 1, :], h1, h2), B.vF, 1)

                @B.block.tensor
                def _(e, st4=st4, c=c, g=g):
                    e.wait_ge(B.vQ, mm["xnb", st4])
                    if g > 0:
                        e.wait_ge(B.vF, mm["pkr", g - 1, 1])
                        e.wait_ge(B.aP, mm["sig", g - 1, 1])
                        e.wait_ge(B.vF, mm["oge", g - 1])
                    ins = None
                    for fc in range(8):
                        for hb in range(2):
                            ins = e.matmul(pk2[:, hb * 512:(hb + 1) * 512],
                                           xnb[:, fc * 512 + c * 128: fc * 512 + (c + 1) * 128],
                                           wgb[:, fc * 1024 + hb * 512: fc * 1024 + (hb + 1) * 512],
                                           start=(fc == 0), stop=(fc == 7))
                    mm["gp", g] = S.inc(ins, B.tT, 1)

                @B.block.vector
                def _(e, g=g):
                    e.wait_ge(B.tT, mm["gp", g])
                    mm["ge", g] = S.inc(e.tensor_copy(gtm[:, :], pk2[:, :]), B.vF, 1)

                for j in range(12 if PB_STAGE >= 1 else 0):
                    sl = 896 if j % 2 == 0 else 1536

                    @B.block.tensor
                    def _(e, g=g, j=j, sl=sl):
                        if j >= 2:
                            e.wait_ge(B.vF, mm["tre", g, j - 2])
                        else:
                            e.wait_ge(B.dG, mm["ldkv", g])
                            e.wait_ge(B.vF, mm["qrope", g, 3])
                            if g > 0:
                                e.wait_ge(B.vF, mm["zup", g - 1])
                        if j < 4:
                            ins = e.transpose(pq2b[0:64, sl:sl + 128],
                                              ld_[g % 3][:, 2304 + j * 64: 2304 + (j + 1) * 64],
                                              B.ident[:, :])
                        else:
                            ins = e.transpose(pq2b[:, sl:sl + 128],
                                              ld_[g % 3][:, (j - 4) * 128:(j - 3) * 128],
                                              B.ident[:, :])
                        mm["tr", g, j] = S.inc(ins, B.tT, 1)

                    @B.block.vector
                    def _(e, g=g, j=j, sl=sl):
                        e.wait_ge(B.tT, mm["tr", g, j])
                        if j < 4:
                            ins = e.tensor_copy(wvT[0:64, j * 128:(j + 1) * 128], pq2b[0:64, sl:sl + 128])
                        else:
                            ins = e.tensor_copy(kf[:, (j - 4) * 128:(j - 3) * 128], pq2b[:, sl:sl + 128])
                        mm["tre", g, j] = S.inc(ins, B.vF, 1)

                if PB_STAGE < 3:
                    continue

                @B.block.tensor
                def _(e, g=g):
                    e.wait_ge(B.vF, mm["tre", g, 11])
                    if g > 0:
                        e.wait_ge(B.vF, mm["sevac", g - 1])
                        e.wait_ge(B.vF, mm["zup", g - 1])
                    ins = None
                    for h in range(4):
                        for c2 in range(2):
                            dc = 2 * h + c2
                            ins = e.matmul(pq2[:, h * 128:(h + 1) * 128],
                                           kf[:, dc * 128:(dc + 1) * 128],
                                           qf[:, dc * 128:(dc + 1) * 128],
                                           start=(c2 == 0), stop=(c2 == 1))
                    mm["ST4", g] = S.inc(ins, B.tT, 1)

                @B.block.vector
                def _(e, g=g):
                    e.wait_ge(B.tT, mm["ST4", g])
                    mm["stme4", g] = S.inc(
                        e.tensor_tensor(stm[:, :].rearrange("p (h t) -> p h t", t=128),
                                        pq2[:, 0:512].rearrange("p (h t) -> p h t", t=128),
                                        B.tri_up[:, :].unsqueeze(1).broadcast_to([128, 4, 128]),
                                        ALU.mult), B.vF, 1)

                @B.block.tensor
                def _(e, g=g):
                    e.wait_ge(B.vF, mm["stme4", g])
                    for h in range(4):
                        for c2 in range(2):
                            dc = 2 * h + c2
                            e.matmul(pq2[:, 512 + h * 64: 512 + (h + 1) * 64],
                                     qf[:, dc * 128:(dc + 1) * 128],
                                     s1sb[:, dc * 64:(dc + 1) * 64], start=(c2 == 0), stop=False)
                        e.matmul(pq2[:, 512 + h * 64: 512 + (h + 1) * 64],
                                 stm[:, h * 128:(h + 1) * 128],
                                 ld_[g % 3][:, 2048 + h * 64: 2048 + (h + 1) * 64],
                                 start=False, stop=True)
                    for h in range(4):
                        e.matmul(pq2[:, 768 + h * 64: 768 + (h + 1) * 64], B.tri_up[:, :],
                                 ld_[g % 3][:, 2048 + h * 64: 2048 + (h + 1) * 64],
                                 start=True, stop=False)
                        e.matmul(pq2[:, 768 + h * 64: 768 + (h + 1) * 64], B.ones1[0:1, 0:128],
                                 zsbb[0:1, h * 64:(h + 1) * 64], start=False, stop=True)
                    ins = None
                    for h in range(4):
                        e.matmul(pq2[:, h * 64:(h + 1) * 64], B.tri_up[:, :],
                                 ld_[g % 3][:, 2304 + h * 64: 2304 + (h + 1) * 64],
                                 start=True, stop=False)
                        ins = e.matmul(pq2[:, h * 64:(h + 1) * 64], B.ones1[0:1, 0:128],
                                       zsbb[0:1, 256 + h * 64: 256 + (h + 1) * 64], start=False, stop=True)
                    mm["zz4", g] = S.inc(ins, B.tT, 1)

                @B.block.vector
                def _(e, g=g):
                    e.wait_ge(B.tT, mm["zz4", g])
                    e.reciprocal(rz[:, 0:256], pq2[:, 768:1024])
                    S.raw(e, e.reciprocal(rz[:, 256:512], pq2[:, 0:256]), B.vZ)
                    S.raw(e, e.tensor_tensor(lg[:, :], pq2[:, 512:768], rz[:, 0:256], ALU.mult), B.vZ)
                    mm["nmx4", g] = S.inc(
                        e.tensor_reduce(nmx[:, 0:4].unsqueeze(2),
                                        lg[:, :].rearrange("p (h m) -> p h m", m=64),
                                        AX.X, ALU.max, negate=True), B.vF, 1)

                @B.block.scalar
                def _(e, g=g):
                    e.wait_ge(B.vF, mm["nmx4", g])
                    ins = None
                    for h in range(4):
                        ins = e.activation(ev[:, h * 64:(h + 1) * 64], lg[:, h * 64:(h + 1) * 64],
                                           AF.Exp, bias=nmx[:, h:h + 1],
                                           accum_out=nmx[:, 4 + h:5 + h])
                    mm["ev4", g] = S.inc(ins, B.aP, 1)

                @B.block.vector
                def _(e, g=g):
                    e.wait_ge(B.aP, mm["ev4", g])
                    S.raw(e, e.reciprocal(nmx[:, 8:12], nmx[:, 4:8]), B.vZ)
                    S.raw(e, e.tensor_tensor(lg[:, :].rearrange("p (h m) -> p h m", m=64),
                                             ev[:, :].rearrange("p (h m) -> p h m", m=64),
                                             nmx[:, 8:12].unsqueeze(2).broadcast_to([128, 4, 64]),
                                             ALU.mult), B.vZ)
                    mm["pt4", g] = S.inc(
                        e.tensor_tensor(ptbi[:, :], lg[:, :], rz[:, 256:512], ALU.mult), B.vF, 1)

                for h in range(4):
                    sl = 896 if h % 2 == 0 else 1536

                    @B.block.tensor
                    def _(e, g=g, h=h, sl=sl):
                        if h >= 2:
                            e.wait_ge(B.vF, mm["pte", g, h - 2])
                        else:
                            e.wait_ge(B.vF, mm["pt4", g])
                        ins = e.transpose(pq2b[0:64, sl:sl + 128],
                                          ptbi[:, h * 64:(h + 1) * 64], B.ident[:, :])
                        mm["ptt", g, h] = S.inc(ins, B.tT, 1)

                    @B.block.vector
                    def _(e, g=g, h=h, sl=sl):
                        e.wait_ge(B.tT, mm["ptt", g, h])
                        mm["pte", g, h] = S.inc(
                            e.tensor_copy(ptb[0:64, h * 128:(h + 1) * 128], pq2b[0:64, sl:sl + 128]),
                            B.vF, 1)

                @B.block.tensor
                def _(e, g=g):
                    e.wait_ge(B.vF, mm["pte", g, 3])
                    ins = None
                    for h in range(4):
                        ins = e.matmul(pq2[:, h * 128:(h + 1) * 128],
                                       wvT[0:64, h * 128:(h + 1) * 128],
                                       ptb[0:64, h * 128:(h + 1) * 128], start=True, stop=True)
                    mm["at4", g] = S.inc(ins, B.tT, 1)

                @B.block.vector
                def _(e, g=g):
                    e.wait_ge(B.tT, mm["at4", g])
                    mm["atme4", g] = S.inc(
                        e.tensor_tensor(stm[:, :].rearrange("p (h t) -> p h t", t=128),
                                        pq2[:, 0:512].rearrange("p (h t) -> p h t", t=128),
                                        B.tri_up[:, :].unsqueeze(1).broadcast_to([128, 4, 128]),
                                        ALU.mult), B.vF, 1)

                @B.block.tensor
                def _(e, g=g):
                    e.wait_ge(B.vF, mm["atme4", g])
                    e.wait_ge(B.vF, mm["ge", g])
                    if g > 0:
                        e.wait_ge(B.vF, mm["sevac", g - 1])
                    for h in range(4):
                        e.matmul(pk2[:, h * 256:(h + 1) * 256], stm[:, h * 128:(h + 1) * 128],
                                 ld_[g % 3][:, 1024 + h * 256: 1024 + (h + 1) * 256],
                                 start=True, stop=False)
                        e.matmul(pk2[:, h * 256:(h + 1) * 256], ptb[0:64, h * 128:(h + 1) * 128],
                                 s2sb[0:64, h * 256:(h + 1) * 256], start=False, stop=True)
                        for c2 in range(2):
                            e.matmul(st1[:, (2 * h + c2) * 64:(2 * h + c2) * 64 + 64],
                                     ld_[g % 3][:, (2 * h + c2) * 128:(2 * h + c2 + 1) * 128],
                                     ld_[g % 3][:, 2048 + h * 64: 2048 + (h + 1) * 64],
                                     start=True, stop=True)
                        e.matmul(st2[0:64, h * 256:(h + 1) * 256],
                                 ld_[g % 3][:, 2304 + h * 64: 2304 + (h + 1) * 64],
                                 ld_[g % 3][:, 1024 + h * 256: 1024 + (h + 1) * 256],
                                 start=True, stop=True)
                    ins = None
                    for h in range(4):
                        e.matmul(pq2[0:1, h * 64:(h + 1) * 64], onesbf[:, :],
                                 ld_[g % 3][:, 2048 + h * 64: 2048 + (h + 1) * 64], start=True, stop=True)
                        ins = e.matmul(pq2[0:1, 256 + h * 64: 256 + (h + 1) * 64], onesbf[:, :],
                                       ld_[g % 3][:, 2304 + h * 64: 2304 + (h + 1) * 64],
                                       start=True, stop=True)
                    mm["scend", g] = S.inc(ins, B.tT, 1)
                mm["stu", g] = mm["scend", g]

                @B.block.vector
                def _(e, g=g):
                    e.wait_ge(B.tT, mm["scend", g])
                    S.raw(e, e.tensor_tensor(zf[g % 2][0:1, :], pq2[0:1, 0:512], zf[(g + 1) % 2][0:1, :],
                                             ALU.add), B.vZ)
                    mm["zup", g] = S.inc(e.tensor_copy(zsbb[0:1, :], zf[g % 2][0:1, :]), B.vF, 1)

                @B.block.vector
                def _(e, g=g):
                    e.wait_ge(B.tT, mm["scend", g])
                    e.tensor_tensor(s1f[g % 2][:, :], st1[:, :], s1f[(g + 1) % 2][:, :], ALU.add)
                    S.raw(e, e.tensor_tensor(s2f[g % 2][0:64, :], st2[0:64, :], s2f[(g + 1) % 2][0:64, :],
                                             ALU.add), B.vZ)
                    e.tensor_copy(s1sb[:, :], s1f[g % 2][:, :])
                    mm["sevac", g] = S.inc(e.tensor_copy(s2sb[0:64, :], s2f[g % 2][0:64, :]), B.vF, 1)

                if PB_STAGE < 4:
                    continue
                for hf in range(2):
                    @B.block.scalar
                    def _(e, g=g, hf=hf):
                        e.wait_ge(B.tT, mm["scend", g])
                        if (g, hf) != (0, 0):
                            e.wait_ge(B.vF, mm["og", g, hf - 1] if hf == 1 else mm["og", g - 1, 1])
                        mm["sq", g, hf] = S.inc(
                            e.activation(qfb[:, :], pk2[:, hf * 512:(hf + 1) * 512], AF.Square), B.aP, 1)
                        mm["sig", g, hf] = S.inc(
                            e.activation(kfb[:, :], gtm[:, hf * 512:(hf + 1) * 512], AF.Sigmoid), B.aP, 1)

                    @B.block.vector
                    def _(e, g=g, hf=hf):
                        e.wait_ge(B.aP, mm["sig", g, hf])
                        S.raw(e, e.tensor_reduce(ssq[:, 0:2].unsqueeze(2),
                                                 qfb[:, :].rearrange("p (h d) -> p h d", d=256),
                                                 AX.X, ALU.add), B.vZ)
                        mm["ssq", g, hf] = S.inc(
                            e.tensor_scalar(ssq[:, 2:4], ssq[:, 0:2], 1.0 / 256, None, ALU.mult), B.vF, 1)

                    @B.block.scalar
                    def _(e, g=g, hf=hf):
                        e.wait_ge(B.vF, mm["ssq", g, hf])
                        mm["sqr", g, hf] = S.inc(
                            e.activation(ssq[:, 4:6], ssq[:, 2:4], AF.Sqrt, bias=B.cst[:, 0:1]), B.aP, 1)

                    @B.block.vector
                    def _(e, g=g, hf=hf):
                        e.wait_ge(B.aP, mm["sqr", g, hf])
                        S.raw(e, e.reciprocal(ssq[:, 6:8], ssq[:, 4:6]), B.vZ)
                        S.raw(e, e.tensor_tensor(rz[:, 0:512], gtm[:, hf * 512:(hf + 1) * 512], kfb[:, :],
                                                 ALU.mult), B.vZ)
                        S.raw(e, e.tensor_tensor(qfb[:, :].rearrange("p (h d) -> p h d", d=256),
                                                 pk2[:, hf * 512:(hf + 1) * 512].rearrange("p (h d) -> p h d", d=256),
                                                 ssq[:, 6:8].unsqueeze(2).broadcast_to([128, 2, 256]),
                                                 ALU.mult), B.vZ)
                        mm["pkr", g, hf] = S.inc(e.engine_nop(), B.vF, 1)
                        S.raw(e, e.tensor_tensor(kfb[:, :].rearrange("p (h d) -> p h d", d=256),
                                                 qfb[:, :].rearrange("p (h d) -> p h d", d=256),
                                                 gnr[:, :].unsqueeze(1).broadcast_to([128, 2, 256]),
                                                 ALU.mult), B.vZ)
                        mm["og", g, hf] = S.inc(
                            e.tensor_tensor(ogsb[:, hf * 512:(hf + 1) * 512], kfb[:, :], rz[:, 0:512],
                                            ALU.mult), B.vF, 1)
                mm["og", g] = mm["og", g, 1]

                for j in range(8):
                    sl = 512 if j % 2 == 0 else 1536

                    @B.block.tensor
                    def _(e, g=g, j=j, sl=sl):
                        e.wait_ge(B.vF, mm["og", g])
                        if j >= 2:
                            e.wait_ge(B.vF, mm["ogte", g, j - 2])
                        ins = e.transpose(pk2b[:, sl:sl + 128], ogsb[:, j * 128:(j + 1) * 128],
                                          B.ident[:, :])
                        mm["ogt", g, j] = S.inc(ins, B.tT, 1)

                    @B.block.vector
                    def _(e, g=g, j=j, sl=sl):
                        e.wait_ge(B.tT, mm["ogt", g, j])
                        mm["ogte", g, j] = S.inc(
                            e.tensor_copy(ogfm[:, j * 128:(j + 1) * 128], pk2b[:, sl:sl + 128]), B.vF, 1)
                mm["oge", g] = mm["ogte", g, 7]

                @B.block.sync
                def _(e, g=g, st4=st4, c=c):
                    e.wait_ge(B.vF, mm["oge", g])
                    tok = st4 * 512 + c * 128
                    dst = B.og_d.ap().rearrange("p (dc t) -> p dc t", t=P)[:, :, tok:tok + 128]
                    S.inc(e.dma_start(out=dst, in_=ogfm[:, :].rearrange("p (dc t) -> p dc t", t=128)),
                          B.dS, 16)
                    mm["ogd", g] = S.val(B.dS)

        @B.block.sync
        def _(e):
            if PB_STAGE >= 4:
                e.wait_ge(B.dS, mm["ogd", min(15, PB_CHUNKS - 1)])

    phx.close()
    _barrier(B)


def _bc_h(ap64, nh=8):
    return ap64.unsqueeze(1).broadcast_to([128, nh, 64])


def _bc_d(ap8, d=64):
    return ap8.unsqueeze(2).broadcast_to([128, 8, d])


def _hd(ap512, d=64):
    return ap512.rearrange("p (h d) -> p h d", d=d)


def phase_ffn_subA(B, lyr):
    nc, S = B.nc, B.sched
    with contextlib.ExitStack() as ph:
        sbt = lambda n, sh, dt=F32: ph.enter_context(nc.sbuf_tensor(f"{n}_l{lyr}", sh, dt))
        wqkv = sbt("wqkv", [128, 8 * 1536], BF16)
        qsr = sbt("qsr", [128, 64])
        ksr = sbt("ksr", [128, 64])
        xw = [sbt(f"xw{i}", [128, 8 * 128], BF16) for i in range(2)]
        xpt = [sbt(f"xpt{i}", [128, 256]) for i in range(2)]
        kf = sbt("kf", [128, 512])
        bufA = sbt("bufA", [128, 512]); bufB = sbt("bufB", [128, 512]); bufC = sbt("bufC", [128, 512])
        bufD = sbt("bufD", [128, 512]); bufE = sbt("bufE", [128, 512]); bufF = sbt("bufF", [128, 512])
        qhat = [sbt(f"qhat{i}", [128, 512], BF16) for i in range(2)]
        khat = [sbt(f"khat{i}", [128, 512], BF16) for i in range(2)]
        ss = sbt("ss", [128, 48])
        scrb = sbt("scrb", [128, 512], BF16)
        vring = sbt("vring", [128, 8 * 512], BF16)
        kring = [ph.enter_context(nc.sbuf_tensor(f"kring{h}_l{lyr}", [65, 8 * 128], BF16)) for h in range(8)]
        qpp = [ph.enter_context(nc.sbuf_tensor(f"qpp{h}_l{lyr}", [65, 2 * 128], BF16)) for h in range(8)]
        sjunk = sbt("sjunk", [128, 16])
        scp = sbt("scp", [128, 2 * 640])
        mjunk = sbt("mjunk", [128, 640])
        nm8 = sbt("nm8", [128, 8])
        nm3 = sbt("nm3", [128, 4])
        nm8n = sbt("nm8n", [128, 8], BF16)
        sc04 = sbt("sc04", [128, 512])
        pt = [sbt(f"pt{i}", [128, 640], BF16) for i in range(2)]
        rcp = sbt("rcp", [128, 8])
        osb = sbt("osb", [128, 512], BF16)

        qkv_ps = ph.enter_context(nc.psum_tensor(f"qkv_ps_l{lyr}", [128, 1536], F32))
        s_ps = ph.enter_context(nc.psum_tensor(f"s_ps_l{lyr}", [128, 1024], F32))
        st_ps = ph.enter_context(nc.psum_tensor(f"st_ps_l{lyr}", [128, 1024], F32))
        o_ps = ph.enter_context(nc.psum_tensor(f"o_ps_l{lyr}", [128, 512], F32))
        trs = s_ps[:, :].bitcast(BF16)
        rs_c, nmt_c, otr_c = 640, 648, 904
        otr = st_ps[:, :].bitcast(BF16)

        sqq = sbt("sqq", [128, 512])
        sqk = sbt("sqk", [128, 512])
        smask = sbt("smask", [128, 5 * 640], BF16)
        B.smask = smask
        mm = {}
        b = {k: S.val(getattr(B, k)) for k in
             ["dS", "dG", "dA", "tT", "xwS", "vF", "vQ", "vY", "vN", "vC", "vR", "vE", "vM",
              "aP", "aK", "aN", "aE", "gK", "aSq", "vSum", "e1"]}

        @B.block.gpsimd
        def _(e):
            for ci in range(8):
                S.inc(e.dma_start(out=wqkv[:, ci * 1536:(ci + 1) * 1536],
                                  in_=B.inp["L0_ffn_wqkv"][lyr, ci * 128:(ci + 1) * 128, :]), B.dG, 16)
            S.inc(e.dma_start(out=qsr[:, :], in_=B.inp["qs_rep"][lyr]), B.dG, 16)
            S.inc(e.dma_start(out=ksr[:, :], in_=B.inp["ks_rep"][lyr]), B.dG, 16)
            for itt in range(5):
                S.inc(e.dma_start(out=smask[:, itt * 640:(itt + 1) * 640],
                                  in_=B.inp["smask"][itt]), B.dG, 16)

        @B.block.vector
        def _(e):
            for h in range(8):
                e.memset(kring[h][64:65, :], 1.0)
            for ci in range(8):
                off = (HALO - 640) - (1 if ci >= 4 else 0)
                e.tensor_copy(xw[0][:, ci * 128:(ci + 1) * 128],
                              B.stream[:, ci * NT + off: ci * NT + off + 128])
            S.inc(e.engine_nop(), B.xwS, 1)

        @B.block.sync
        def _(e):
            for t in range(NKV):
                if t >= 2:
                    e.wait_ge(B.vQ, b["vQ"] + t - 1)
                    e.wait_ge(B.gK, b["gK"] + t - 1)
                S.inc(e.dma_start(out=xpt[t % 2][:, :],
                                  in_=B.inp["xpos"][HALO - 640 + 128 * t: HALO - 512 + 128 * t, :]), B.dS, 16)

        for t in range(NKV):
            it = t - 4
            prod = True

            @B.block.tensor
            def _(e, t=t, it=it, prod=prod):
                if prod:
                    e.wait_ge(B.xwS, b["xwS"] + t + 1)
                    if t == 0:
                        e.wait_ge(B.dG, b["dG"] + 15 * 16)
                    if t >= 1:
                        e.wait_ge(B.vF, b["vF"] + t)
                        e.wait_ge(B.aSq, b["aSq"] + t)
                    ins = None
                    for ci in range(8):
                        for g in range(3):
                            ins = e.matmul(qkv_ps[:, g * 512:(g + 1) * 512],
                                           xw[t % 2][:, ci * 128:(ci + 1) * 128],
                                           wqkv[:, ci * 1536 + g * 512: ci * 1536 + (g + 1) * 512],
                                           start=(ci == 0), stop=(ci == 7))
                    mm["qkv", t] = S.inc(ins, B.tT, 1)
                    e.wait_ge(B.vQ, b["vQ"] + t + 1)
                    e.wait_ge(B.gK, b["gK"] + t + 1)
                    if t >= 1:
                        e.wait_ge(B.aK, b["aK"] + t)
                        e.wait_ge(B.vY, b["vY"] + t)
                    if it >= 1:
                        e.wait_ge(B.vM, b["vM"] + it * 8)
                    for h in range(8):
                        e.transpose(trs[0:64, h * 128:(h + 1) * 128],
                                    khat[t % 2][:, h * 64:(h + 1) * 64], B.ident[:, :])
                    ins = None
                    for h in range(8):
                        ins = e.transpose(trs[0:64, 1024 + h * 128:1024 + (h + 1) * 128],
                                          qhat[t % 2][:, h * 64:(h + 1) * 64], B.ident[:, :])
                    mm["tr", t] = S.inc(ins, B.tT, 1)
                if it >= 0:
                    e.wait_ge(B.aK, b["aK"] + min(it + 4, NKV - 1) + 1)
                    e.wait_ge(B.vY, b["vY"] + min(it + 4, NKV - 1) + 1)
                    base = it % 8
                    runs = []
                    kk0 = 0
                    while kk0 < 5:
                        sl0 = (it + kk0) % 8
                        n_ring = min(5 - kk0, 8 - sl0)
                        n_bank = (4 - kk0) if kk0 < 4 else (5 - kk0)
                        nn = min(n_ring, n_bank)
                        runs.append((kk0, nn))
                        kk0 += nn
                    for h in range(8):
                        e.wait_ge(B.vM, b["vM"] + it * 8 + h)
                        ins = None
                        for kk0, nn in runs:
                            sl0 = (it + kk0) % 8
                            ins = e.matmul(s_ps[:, kk0 * 128:(kk0 + nn) * 128],
                                           qpp[h][0:64, (it % 2) * 128:(it % 2) * 128 + 128],
                                           kring[h][0:64, sl0 * 128:(sl0 + nn) * 128],
                                           start=True, stop=True)
                        mm["S", it, h] = S.inc(ins, B.tT, 1)
                    e.wait_ge(B.vN, b["vN"] + it + 1)
                    if it >= 1:
                        e.wait_ge(B.aE, b["aE"] + it * 8)
                        e.wait_ge(B.vR, b["vR"] + it)
                        e.wait_ge(B.vE, b["vE"] + it * 4)
                    for h in range(8):
                        if h >= 1:
                            e.wait_ge(B.aN, b["aN"] + it * 8 + h)
                        ins = e.matmul(st_ps[64:65, nmt_c + (h % 2) * 128: nmt_c + (h % 2) * 128 + 128],
                                       nm8n[:, h:h + 1], B.ident[:, :], start=True, stop=True,
                                       tile_position=(0, 64))
                        mm["nmt", it, h] = S.inc(ins, B.tT, 1)
                    for h in range(8):
                        e.wait_ge(B.aN, b["aN"] + it * 8 + h + 1)
                        if h >= 1:
                            e.wait_ge(B.aE, b["aE"] + it * 8 + h)
                        ins = None
                        for kk in range(5):
                            kvt = it + kk
                            ins = e.matmul(st_ps[:, kk * 128:(kk + 1) * 128],
                                           kring[h][:, (kvt % 8) * 128:(kvt % 8) * 128 + 128],
                                           qpp[h][:, (it % 2) * 128:(it % 2) * 128 + 128],
                                           start=True, stop=True)
                        mm["ST", it, h] = S.inc(ins, B.tT, 1)
                        e.wait_ge(B.aE, b["aE"] + it * 8 + h + 1)
                        ins = None
                        for kk in range(5):
                            kvt = it + kk
                            e.matmul(o_ps[:, h * 64:(h + 1) * 64],
                                     pt[h % 2][:, kk * 128:(kk + 1) * 128],
                                     vring[:, (kvt % 8) * 512 + h * 64:(kvt % 8) * 512 + (h + 1) * 64],
                                     start=(kk == 0), stop=(kk == 4))
                            ins = e.matmul(st_ps[:, rs_c + h:rs_c + h + 1],
                                           pt[h % 2][:, kk * 128:(kk + 1) * 128],
                                           B.kvvalid_bf[:, kvt:kvt + 1],
                                           start=(kk == 0), stop=(kk == 4))
                        mm["PV", it, h] = S.inc(ins, B.tT, 1)
                    e.wait_ge(B.vR, b["vR"] + it + 1)
                    for j in range(4):
                        if j >= 1:
                            e.wait_ge(B.vE, b["vE"] + it * 4 + j)
                        ins = e.transpose(otr[:, 1808:1936],
                                          osb[:, j * 128:(j + 1) * 128], B.ident[:, :])
                        mm["otr", it, j] = S.inc(ins, B.tT, 1)

            @B.block.scalar
            def _(e, t=t, it=it, prod=prod):
                if prod:
                    e.wait_ge(B.tT, mm["qkv", t])
                    if t >= 1:
                        e.wait_ge(B.vF, b["vF"] + t)
                        e.wait_ge(B.gK, b["gK"] + t)
                    e.copy(kf[:, 0:256], qkv_ps[:, 512:768])
                    e.activation(sqq[:, :], qkv_ps[:, 0:512], AF.Square)
                    S.raw(e, e.copy(kf[:, 256:512], qkv_ps[:, 768:1024]), B.aZ)
                    S.inc(e.activation(sqk[:, :], kf[:, :], AF.Square), B.aSq, 1)
                    e.wait_ge(B.vSum, b["vSum"] + t + 1)
                    S.inc(e.activation(ss[:, 16:32], ss[:, 32:48], AF.Sqrt), B.aP, 1)
                    if t >= 8:
                        e.wait_ge(B.tT, mm["PV", t - 8, 7])
                    e.wait_ge(B.tT, mm["tr", t])
                    ins = None
                    for h in range(8):
                        ins = e.copy(kring[h][0:64, (t % 8) * 128:(t % 8) * 128 + 128],
                                     trs[0:64, h * 128:(h + 1) * 128])
                    S.inc(ins, B.aK, 1)
                if it >= 0:
                    if it >= 2:
                        e.wait_ge(B.tT, mm["ST", it - 2, 7])
                    for h in range(8):
                        e.wait_ge(B.tT, mm["nmt", it, h])
                        ins = e.copy(qpp[h][64:65, (it % 2) * 128:(it % 2) * 128 + 128],
                                     st_ps[64:65, nmt_c + (h % 2) * 128: nmt_c + (h % 2) * 128 + 128])
                        S.inc(ins, B.aN, 1)
                    for h in range(8):
                        e.wait_ge(B.vC, b["vC"] + it * 8 + h + 1)
                        if h >= 2:
                            e.wait_ge(B.tT, mm["PV", it, h - 2])
                        elif it >= 1:
                            e.wait_ge(B.tT, mm["PV", it - 1, h + 6])
                        e.activation(pt[h % 2][:, 128:512], st_ps[:, 128:512], AF.Exp)
                        e.activation(pt[h % 2][:, 0:128], sc04[:, (h % 2) * 256:(h % 2) * 256 + 128], AF.Exp)
                        ins = e.activation(pt[h % 2][:, 512:640],
                                           sc04[:, (h % 2) * 256 + 128:(h % 2) * 256 + 256], AF.Exp)
                        S.inc(ins, B.aE, 1)

            @B.block.vector
            def _(e, t=t, it=it, prod=prod):
                if prod:
                    e.wait_ge(B.tT, mm["qkv", t])
                    if t >= 8:
                        e.wait_ge(B.tT, mm["PV", t - 8, 7])
                    e.tensor_copy(vring[:, (t % 8) * 512:(t % 8) * 512 + 512], qkv_ps[:, 1024:1536])
                    e.wait_ge(B.aSq, b["aSq"] + t + 1)
                    e.tensor_reduce(ss[:, 0:8].unsqueeze(2), _hd(sqq[:, :]), AX.X, ALU.add)
                    S.raw(e, e.tensor_reduce(ss[:, 8:16].unsqueeze(2), _hd(sqk[:, :]), AX.X, ALU.add), B.vZ)
                    S.inc(e.tensor_scalar(ss[:, 32:48], ss[:, 0:16], 1e-24, None, ALU.max), B.vSum, 1)
                    e.wait_ge(B.aP, b["aP"] + t + 1)
                    S.raw(e, e.reciprocal(ss[:, 32:48], ss[:, 16:32]), B.vZ)
                    S.raw(e, e.tensor_tensor(_hd(bufA[:, :]), _hd(qkv_ps[:, 0:512]),
                                             _bc_d(ss[:, 32:40]), ALU.mult), B.vZ)
                    S.inc(e.engine_nop(), B.vF, 1)
                    if t + 1 < NKV:
                        for ci in range(8):
                            off = (HALO - 640) + 128 * (t + 1) - (1 if ci >= 4 else 0)
                            e.tensor_copy(xw[(t + 1) % 2][:, ci * 128:(ci + 1) * 128],
                                          B.stream[:, ci * NT + off: ci * NT + off + 128])
                        S.inc(e.engine_nop(), B.xwS, 1)
                    S.raw(e, e.tensor_tensor(_hd(bufB[:, :]), _hd(bufA[:, :]), _bc_h(qsr[:, :]), ALU.mult), B.vZ)
                    e.tensor_tensor(_hd(bufA[:, :]), _hd(bufB[:, :]), _bc_h(xpt[t % 2][:, 0:64]), ALU.mult)
                    e.tensor_tensor(_hd(bufC[:, :], 64)[:, :, 0:32], _hd(bufB[:, :])[:, :, 32:64],
                                    xpt[t % 2][:, 64:96].unsqueeze(1).broadcast_to([128, 8, 32]), ALU.mult)
                    S.raw(e, e.tensor_tensor(_hd(bufC[:, :], 64)[:, :, 32:64], _hd(bufB[:, :])[:, :, 0:32],
                                             xpt[t % 2][:, 96:128].unsqueeze(1).broadcast_to([128, 8, 32]),
                                             ALU.mult), B.vZ)
                    if t >= 2:
                        e.wait_ge(B.tT, mm["tr", t - 1])
                    S.inc(e.tensor_add(qhat[t % 2][:, :], bufA[:, :], bufC[:, :]), B.vQ, 1)
                    e.wait_ge(B.tT, mm["tr", t])
                    if t >= 6:
                        e.wait_ge(B.tT, mm["ST", t - 6, 7])
                    if t >= 4:
                        ins = None
                        for h in range(8):
                            ins = e.tensor_copy(qpp[h][0:64, ((t - 4) % 2) * 128:((t - 4) % 2) * 128 + 128],
                                                trs[0:64, 1024 + h * 128:1024 + (h + 1) * 128])
                        S.inc(ins, B.vY, 1)
                    else:
                        S.inc(e.engine_nop(), B.vY, 1)
                if it >= 0:
                    for h in range(8):
                        e.wait_ge(B.tT, mm["S", it, h])
                        o_ = (h % 2) * 640
                        S.inc(e.tensor_copy(scp[:, o_:o_ + 640], s_ps[:, 0:640]), B.vM, 1)
                        if it <= 4:
                            S.raw(e, e.tensor_tensor(mjunk[:, 0:640], scp[:, o_:o_ + 640],
                                                     B.smask[:, it * 640:(it + 1) * 640], ALU.add), B.vZ)
                            e.tensor_reduce(nm8[:, h:h + 1], mjunk[:, 0:640], AX.X, ALU.max)
                        else:
                            e.tensor_tensor(mjunk[:, 0:128], scp[:, o_:o_ + 128], B.addm_lo[:, :], ALU.add)
                            S.raw(e, e.tensor_tensor(mjunk[:, 512:640], scp[:, o_ + 512:o_ + 640],
                                                     B.addm_up[:, :], ALU.add), B.vZ)
                            e.tensor_reduce(nm3[:, 0:1], mjunk[:, 0:128], AX.X, ALU.max)
                            e.tensor_reduce(nm3[:, 1:2], scp[:, o_ + 128:o_ + 512], AX.X, ALU.max)
                            S.raw(e, e.tensor_reduce(nm3[:, 2:3], mjunk[:, 512:640], AX.X, ALU.max), B.vZ)
                            e.tensor_reduce(nm8[:, h:h + 1], nm3[:, 0:3], AX.X, ALU.max)
                    S.raw(e, e.engine_nop(), B.vZ)
                    S.inc(e.tensor_scalar(nm8n[:, :], nm8[:, :], -1.0, None, ALU.mult), B.vN, 1)
                    for h in range(8):
                        e.wait_ge(B.tT, mm["ST", it, h])
                        e.tensor_tensor(sc04[:, (h % 2) * 256:(h % 2) * 256 + 128],
                                        st_ps[:, 0:128], B.addm_lo[:, :], ALU.add)
                        ins = e.tensor_tensor(sc04[:, (h % 2) * 256 + 128:(h % 2) * 256 + 256],
                                              st_ps[:, 512:640], B.addm_up[:, :], ALU.add)
                        S.inc(ins, B.vC, 1)
                    e.wait_ge(B.tT, mm["PV", it, 7])
                    S.raw(e, e.tensor_scalar(sjunk[:, 0:8], st_ps[:, rs_c:rs_c + 8], 1e-30, None, ALU.max), B.vZ)
                    S.raw(e, e.reciprocal(rcp[:, :], sjunk[:, 0:8]), B.vZ)
                    S.inc(e.tensor_tensor(_hd(osb[:, :]), _hd(o_ps[:, :]), _bc_d(rcp[:, :]),
                                          ALU.mult), B.vR, 1)
                    for j in range(4):
                        e.wait_ge(B.tT, mm["otr", it, j])
                        ins = e.tensor_copy(B.o0[:, j * NQ * 128 + it * 128: j * NQ * 128 + (it + 1) * 128],
                                            otr[:, 1808:1936])
                        S.inc(ins, B.vE, 1)

            @B.block.gpsimd
            def _(e, t=t, prod=prod):
                if not prod:
                    return
                e.wait_ge(B.aP, b["aP"] + t + 1)
                e.wait_ge(B.vF, b["vF"] + t + 1)
                e.tensor_tensor(_hd(bufD[:, :]), _hd(kf[:, :]), _bc_d(ss[:, 40:48]), ALU.mult)
                S.raw(e, e.engine_nop(), B.gZ)
                S.raw(e, e.tensor_tensor(_hd(bufE[:, :]), _hd(bufD[:, :]), _bc_h(ksr[:, :]), ALU.mult), B.gZ)
                e.tensor_tensor(_hd(bufD[:, :]), _hd(bufE[:, :]), _bc_h(xpt[t % 2][:, 128:192]), ALU.mult)
                e.tensor_tensor(_hd(bufF[:, :], 64)[:, :, 0:32], _hd(bufE[:, :])[:, :, 32:64],
                                xpt[t % 2][:, 192:224].unsqueeze(1).broadcast_to([128, 8, 32]), ALU.mult)
                S.raw(e, e.tensor_tensor(_hd(bufF[:, :], 64)[:, :, 32:64], _hd(bufE[:, :])[:, :, 0:32],
                                         xpt[t % 2][:, 224:256].unsqueeze(1).broadcast_to([128, 8, 32]),
                                         ALU.mult), B.gZ)
                if t >= 2:
                    e.wait_ge(B.tT, mm["tr", t - 1])
                S.inc(e.tensor_add(khat[t % 2][:, :], bufD[:, :], bufF[:, :]), B.gK, 1)


    _barrier(B)
    _tap_sbuf(B, f"o0_{lyr}", B.o0[:, 0:4 * NQ * 128], (128, 4 * NQ * 128), BF16)


def phase_ffn_rest(B, lyr, o0stack=None):
    nc, S = B.nc, B.sched
    with contextlib.ExitStack() as ph:
        sbt = lambda n, sh, dt=F32: ph.enter_context(nc.sbuf_tensor(f"{n}_r{lyr}", sh, dt))
        scr = sbt("scr", [128, 512])
        scr2 = sbt("scr2", [128, 512])
        phA = contextlib.ExitStack()
        wo = phA.enter_context(nc.sbuf_tensor(f"wo_r{lyr}", [128, 4 * D], BF16, side="right"))
        ps1 = ph.enter_context(nc.psum_tensor(f"ps1_r{lyr}", [128, 512], F32))
        ps2 = ph.enter_context(nc.psum_tensor(f"ps2_r{lyr}", [128, 512], F32))

        b = {k: S.val(getattr(B, k)) for k in ["dG", "tT", "vF"]}
        mm = {}

        @B.block.gpsimd
        def _(e):
            for ic in range(4):
                S.inc(e.dma_start(out=wo[:, ic * D:(ic + 1) * D],
                                  in_=B.inp["L0_ffn_wo"][lyr, ic * 128:(ic + 1) * 128, :]), B.dG, 16)

        widths = [512, 512, 512, 512, 128]
        offs = [HALO - 128 + i * 512 for i in range(5)]

        @B.block.tensor
        def _(e):
            e.wait_ge(B.dG, b["dG"] + 4 * 16)
            for wi, (w, off) in enumerate(zip(widths, offs)):
                for oc in range(8):
                    g = wi * 8 + oc
                    pp = [ps1, ps2][g % 2]
                    if g >= 2:
                        e.wait_ge(B.vF, b["vF"] + 2 * (g - 1))
                    ins = None
                    o0c = off - (HALO - 128)
                    for ic in range(4):
                        ins = e.matmul(pp[:, 0:w], wo[:, ic * D + oc * 128: ic * D + (oc + 1) * 128],
                                       B.o0[:, ic * NQ * 128 + o0c: ic * NQ * 128 + o0c + w],
                                       start=(ic == 0), stop=(ic == 3))
                    mm["wo", g] = S.inc(ins, B.tT, 1)

        @B.block.vector
        def _(e):
            for wi, (w, off) in enumerate(zip(widths, offs)):
                for oc in range(8):
                    g = wi * 8 + oc
                    pp = [ps1, ps2][g % 2]
                    e.wait_ge(B.tT, mm["wo", g])
                    S.raw(e, e.tensor_add(scr[:, 0:w], pp[:, 0:w],
                                          B.stream[:, oc * NT + off: oc * NT + off + w]), B.vZ)
                    S.inc(e.tensor_copy(B.stream[:, oc * NT + off: oc * NT + off + w], scr[:, 0:w]),
                          B.vF, 2)

        _barrier(B)
        _tap_sbuf(B, f"xmid{lyr}", B.stream[:, 0:8 * NT], (128, 8 * NT))
        phA.close()
        if o0stack is not None:
            o0stack.close()

        w2 = sbt("w2", [128, 22 * D], BF16)
        w1ab = [sbt(f"w1ab{i}", [128, 8 * 128], BF16) for i in range(2)]
        w1gb = [sbt(f"w1gb{i}", [128, 8 * 128], BF16) for i in range(2)]
        h2n = sbt("h2n", [128, 8 * 512], BF16)
        sqb = [sbt(f"sqb{i}", [128, 512]) for i in range(2)]
        glu = sbt("glu", [128, 22 * 512], BF16)
        gl = sbt("gl", [128, 512])
        stv = sbt("stv", [1, 3 * 512])
        r2b = sbt("r2b", [2, 512])
        rstdb = sbt("rstdb", [1, 512])
        lnr = sbt("lnr", [2, 8 * 128])
        b1ac = sbt("b1ac", [128, 22])
        b1gc = sbt("b1gc", [128, 22])
        b2c = sbt("b2c", [128, 8])
        psa = [ph.enter_context(nc.psum_tensor(f"psa{i}_r{lyr}", [128, 512], F32)) for i in range(2)]
        psg = [ph.enter_context(nc.psum_tensor(f"psg{i}_r{lyr}", [128, 512], F32)) for i in range(2)]
        psc1 = ph.enter_context(nc.psum_tensor(f"psc1_r{lyr}", [128, 512], F32))
        psc2 = ph.enter_context(nc.psum_tensor(f"psc2_r{lyr}", [128, 512], F32))

        @B.block.gpsimd
        def _(e):
            for fc in range(22):
                S.inc(e.dma_start(out=w2[:, fc * D:(fc + 1) * D],
                                  in_=B.inp["w2_blk"][lyr, fc]), B.dG, 16)
            S.inc(e.dma_start(out=b1ac[:, :], in_=B.inp["b1a_cols"][lyr]), B.dG, 16)
            S.inc(e.dma_start(out=b1gc[:, :], in_=B.inp["b1g_cols"][lyr]), B.dG, 16)
            S.inc(e.dma_start(out=b2c[:, :], in_=B.inp["b2_cols"][lyr]), B.dG, 16)
            S.inc(e.dma_start(out=lnr[0:2, :], in_=B.inp["lnvec"][lyr]), B.dG, 16)

        @B.block.tensor
        def _(e):
            e.wait_ge(B.dG, S.val(B.dG))

        global H2N_GLU
        H2N_GLU = (h2n, glu)

        for tt in (3, 2, 1, 0):
            toff = HALO + tt * 512
            b2_ = {k: S.val(getattr(B, k)) for k in ["tT", "vF", "vQ", "vN", "vC", "aP", "aK", "dG"]}

            @B.block.scalar
            def _(e, toff=toff, b2_=b2_):
                e.wait_ge(B.vF, b2_["vF"])
                for ci in range(8):
                    off2 = toff - (1 if ci >= 4 else 0)
                    if ci >= 2:
                        e.wait_ge(B.tT, b2_["tT"] + ci - 1)
                    S.inc(e.activation(sqb[ci % 2][:, :],
                                       B.stream[:, ci * NT + off2: ci * NT + off2 + 512],
                                       AF.Square), B.aP, 1)

            @B.block.tensor
            def _(e, tt=tt, toff=toff, b2_=b2_):
                e.wait_ge(B.vF, b2_["vF"])
                for ci in range(8):
                    off2 = toff - (1 if ci >= 4 else 0)
                    e.wait_ge(B.aP, b2_["aP"] + ci + 1)
                    e.matmul(psc1[0:1, 0:512], B.onescolf[:, :],
                             B.stream[:, ci * NT + off2: ci * NT + off2 + 512],
                             start=(ci == 0), stop=(ci == 7))
                    ins = e.matmul(psc2[0:1, 0:512], B.onescolf[:, :], sqb[ci % 2][:, :],
                                   start=(ci == 0), stop=(ci == 7))
                    S.inc(ins, B.tT, 1)
                mm["lnsum", tt] = S.val(B.tT)

            @B.block.vector
            def _(e, tt=tt, b2_=b2_):
                e.wait_ge(B.tT, mm["lnsum", tt])
                e.tensor_scalar(stv[0:1, 0:512], psc1[0:1, 0:512], 1.0 / D, None, ALU.mult)
                S.raw(e, e.tensor_scalar(stv[0:1, 512:1024], psc2[0:1, 0:512], 1.0 / D, None, ALU.mult), B.vZ)
                S.raw(e, e.tensor_tensor(gl[0:1, 0:512], stv[0:1, 0:512], stv[0:1, 0:512], ALU.mult), B.vZ)
                S.inc(e.tensor_tensor(stv[0:1, 1024:1536], stv[0:1, 512:1024], gl[0:1, 0:512],
                                      ALU.subtract), B.vN, 1)

            @B.block.scalar
            def _(e, b2_=b2_):
                e.wait_ge(B.vN, b2_["vN"] + 1)
                S.inc(e.activation(stv[0:1, 512:1024], stv[0:1, 1024:1536], AF.Sqrt, bias=B.cst[0:1, 0:1]), B.aK, 1)

            @B.block.vector
            def _(e, b2_=b2_):
                e.wait_ge(B.aK, b2_["aK"] + 1)
                S.raw(e, e.reciprocal(stv[0:1, 1024:1536], stv[0:1, 512:1024]), B.vZ)
                e.tensor_copy(rstdb[0:1, :], stv[0:1, 1024:1536])
                S.raw(e, e.tensor_tensor(stv[0:1, 512:1024], stv[0:1, 0:512], stv[0:1, 1024:1536], ALU.mult), B.vZ)
                S.raw(e, e.memset(r2b[0:2, :], -1.0), B.vZ)
                S.inc(e.tensor_copy(r2b[0:1, :], stv[0:1, 512:1024]), B.vC, 1)

            @B.block.tensor
            def _(e, tt=tt, b2_=b2_):
                e.wait_ge(B.vC, b2_["vC"] + 1)
                for ci in range(8):
                    if ci >= 1:
                        e.wait_ge(B.vF, b2_["vF"] + 2 * ci)
                    e.matmul(psc1[:, :], lnr[0:1, ci * 128:(ci + 1) * 128], rstdb[0:1, :],
                             start=True, stop=True)
                    ins = e.matmul(psc2[:, :], lnr[0:2, ci * 128:(ci + 1) * 128], r2b[0:2, :],
                                   start=True, stop=True)
                    mm["c12", tt, ci] = S.inc(ins, B.tT, 1)

            @B.block.vector
            def _(e, tt=tt, toff=toff, b2_=b2_):
                for ci in range(8):
                    off2 = toff - (1 if ci >= 4 else 0)
                    e.wait_ge(B.tT, mm["c12", tt, ci])
                    S.raw(e, e.tensor_tensor(scr[:, 0:512],
                                             B.stream[:, ci * NT + off2: ci * NT + off2 + 512],
                                             psc1[:, :], ALU.mult), B.vZ)
                    S.raw(e, e.tensor_tensor(h2n[:, ci * 512:(ci + 1) * 512], scr[:, 0:512],
                                             psc2[:, :], ALU.subtract), B.vZ)
                    S.inc(e.engine_nop(), B.vF, 2)

            @B.block.tensor
            def _(e, tt=tt, b2_=b2_):
                for ocp in range(22):
                    e.wait_ge(B.dG, b2_["dG"] + (ocp + 1) * 32)
                    if ocp >= 2:
                        e.wait_ge(B.vC, b2_["vC"] + 1 + ocp - 1)
                        e.wait_ge(B.aK, b2_["aK"] + 1 + ocp - 1)
                    else:
                        e.wait_ge(B.vC, b2_["vC"])
                        e.wait_ge(B.aK, b2_["aK"])
                    if ocp == 0:
                        e.wait_ge(B.vF, b2_["vF"] + 16)
                    ins = None
                    for ci in range(8):
                        e.matmul(psa[ocp % 2][:, :], w1ab[ocp % 2][:, ci * 128:(ci + 1) * 128],
                                 h2n[:, ci * 512:(ci + 1) * 512], start=(ci == 0), stop=(ci == 7))
                        ins = e.matmul(psg[ocp % 2][:, :], w1gb[ocp % 2][:, ci * 128:(ci + 1) * 128],
                                       h2n[:, ci * 512:(ci + 1) * 512], start=(ci == 0), stop=(ci == 7))
                    mm["w1", tt, ocp] = S.inc(ins, B.tT, 1)

            @B.block.gpsimd
            def _(e, tt=tt, b2_=b2_):
                for ocp in range(22):
                    if ocp >= 2:
                        e.wait_ge(B.tT, mm["w1", tt, ocp - 2])
                    else:
                        e.wait_ge(B.tT, b2_["tT"])
                    for half, dst in ((0, w1ab), (1, w1gb)):
                        base = ((lyr * 2 + half) * 22 + ocp) * 1024 * 128
                        S.inc(e.dma_start(out=dst[ocp % 2][:, :],
                                          in_=AP(B.inp["w1_blk"], base,
                                                 [[128, 128], [16384, 8], [1, 128]])),
                              B.dG, 16)

            @B.block.scalar
            def _(e, tt=tt, b2_=b2_):
                for ocp in range(22):
                    e.wait_ge(B.tT, mm["w1", tt, ocp])
                    if ocp >= 1:
                        e.wait_ge(B.vC, b2_["vC"] + 1 + ocp)
                    S.inc(e.activation(gl[:, :], psg[ocp % 2][:, :], AF.Gelu, bias=b1gc[:, ocp:ocp + 1]), B.aK, 1)

            @B.block.vector
            def _(e, tt=tt, b2_=b2_):
                for ocp in range(22):
                    e.wait_ge(B.aK, b2_["aK"] + 1 + ocp + 1)
                    ins = e.scalar_tensor_tensor(glu[:, ocp * 512:(ocp + 1) * 512], psa[ocp % 2][:, :],
                                                 b1ac[:, ocp:ocp + 1], gl[:, :], ALU.add, ALU.mult)
                    S.inc(ins, B.vC, 1)

            @B.block.tensor
            def _(e, tt=tt, b2_=b2_):
                for oc in range(8):
                    pp = [ps1, ps2][oc % 2]
                    if oc >= 2:
                        e.wait_ge(B.vF, b2_["vF"] + 16 + 2 * (oc - 1))
                    e.wait_ge(B.vC, b2_["vC"] + 1 + 22)
                    ins = None
                    for fc in range(22):
                        ins = e.matmul(pp[:, :], w2[:, fc * D + oc * 128: fc * D + (oc + 1) * 128],
                                       glu[:, fc * 512:(fc + 1) * 512], start=(fc == 0), stop=(fc == 21))
                    mm["w2", tt, oc] = S.inc(ins, B.tT, 1)

            @B.block.vector
            def _(e, tt=tt, toff=toff, b2_=b2_):
                for oc in range(8):
                    pp = [ps1, ps2][oc % 2]
                    e.wait_ge(B.tT, mm["w2", tt, oc])
                    S.raw(e, e.scalar_tensor_tensor(scr2[:, 0:512], pp[:, :], b2c[:, oc:oc + 1],
                                                    B.stream[:, oc * NT + toff: oc * NT + toff + 512],
                                                    ALU.add, ALU.add), B.vZ)
                    S.inc(e.tensor_copy(B.stream[:, oc * NT + toff: oc * NT + toff + 512],
                                        scr2[:, 0:512]), B.vF, 2)

    _barrier(B)
    _tap_sbuf(B, f"h2n{lyr}", H2N_GLU[0][:, :], (128, 8 * 512), BF16)
    _tap_sbuf(B, f"glu{lyr}", H2N_GLU[1][:, :], (128, 22 * 512), BF16)
    _tap_sbuf(B, f"x{lyr + 1}", B.stream[:, 0:8 * NT], (128, 8 * NT))

    if lyr == 0:
        phase_halo_exchange(B)
        _barrier(B)


def phase_halo_exchange(B):
    nc, S = B.nc, B.sched
    with contextlib.ExitStack() as ph:
        hsc = ph.enter_context(nc.sbuf_tensor("hsc", [128, 8 * HALO], F32))
        pid = ph.enter_context(nc.sbuf_tensor("pid_sb", [1, 2], mybir.dt.uint32))

        @B.block.sync
        def _(e):
            src = B.stream[:, :].rearrange("p (c n) -> p c n", n=NT)[:, :, NT - HALO:NT]
            S.inc(e.dma_start(out=B.ag2_in[:, :], in_=src), B.dS, 16)

        @B.block.gpsimd
        def _(e):
            S.inc(e.dma_start(out=pid[0:1, 0:1], in_=B.nc.partition_id_tensor[0:1, 0:1]), B.dG, 16)
            e.wait_ge(B.dS, S.val(B.dS))
            S.inc(e.collective_compute("AllGather", ALU.bypass, replica_groups=RG,
                                       ins=[B.ag2_in.ap()], outs=[B.ag2_out.ap()]), B.cc, 1)
            e.wait_ge(B.cc, S.val(B.cc))
            e.wait_ge(B.dG, S.val(B.dG))
            with e.register("hx") as gr:
                e.reg_load(gr, pid[0:1, 0:1])
                e.reg_mod(gr, gr, 4)
                e.reg_add(gr, gr, 3)
                e.reg_mod(gr, gr, 4)
                e.reg_mul(gr, gr, 128 * 8 * HALO)
                S.inc(e.dma_start(out=hsc[:, :],
                                  in_=AP(B.ag2_out, gr, [[8 * HALO, 128], [1, 8 * HALO]])), B.dG, 16)

        @B.block.vector
        def _(e):
            e.wait_ge(B.dG, S.val(B.dG))
            for ci in range(8):
                S.inc(e.tensor_scalar(B.stream[:, ci * NT: ci * NT + HALO],
                                      hsc[:, ci * HALO:(ci + 1) * HALO],
                                      B.hgate[:, 0:1], None, ALU.mult), B.vF, 1)


def phase_final(B):
    nc, S = B.nc, B.sched
    with contextlib.ExitStack() as ph:
        sbt = lambda n, sh, dt=F32: ph.enter_context(nc.sbuf_tensor(f"{n}_fin", sh, dt))
        wob = sbt("wob", [128, 8 * 1024], BF16)
        ogs = [sbt(f"ogs{i}", [128, 8 * 512], BF16) for i in range(2)]
        fs = sbt("fs", [128, 8 * 512])
        x0t = [sbt(f"x0t{i}", [128, 1024]) for i in range(2)]
        ot = [sbt(f"ot{i}", [128, 1024]) for i in range(2)]
        pa = [ph.enter_context(nc.psum_tensor(f"pa{i}_fin", [128, 512], F32)) for i in range(2)]
        tp = [ph.enter_context(nc.psum_tensor(f"tp{i}_fin", [128, 512], F32)) for i in range(2)]
        mm = {}
        b = {k: S.val(getattr(B, k)) for k in ["dS", "dG", "dA", "tT", "vF", "vQ", "aP"]}

        @B.block.gpsimd
        def _(e):
            for dc in range(8):
                S.inc(e.dma_start(out=wob[:, dc * 1024:(dc + 1) * 1024],
                                  in_=B.inp["L0_attn_Wo"][dc * 128:(dc + 1) * 128, :]), B.dG, 16)
            mm["wo"] = S.val(B.dG)

        for t4 in range(4):
            @B.block.scalar
            def _(e, t4=t4):
                if t4 >= 2:
                    e.wait_ge(B.tT, mm["att", t4 - 2, 7])
                src = B.og_d.ap().rearrange("p (dc t) -> p dc t", t=P)[:, :, t4 * 512:(t4 + 1) * 512]
                S.inc(e.dma_start(out=ogs[t4 % 2][:, :].rearrange("p (dc t) -> p dc t", t=512),
                                  in_=src), B.dA, 16)
                mm["ogl", t4] = S.val(B.dA)

            for oc in range(8):
                @B.block.tensor
                def _(e, t4=t4, oc=oc):
                    if oc == 0:
                        e.wait_ge(B.dA, mm["ogl", t4])
                    if t4 == 0 and oc == 0:
                        e.wait_ge(B.dG, mm["wo"])
                    pp = pa[oc % 2]
                    if oc >= 2:
                        e.wait_ge(B.vF, mm["fse", t4, oc - 2])
                    elif t4 > 0:
                        e.wait_ge(B.vF, mm["fse", t4 - 1, 6 + oc])
                    ins = None
                    for dc in range(8):
                        ins = e.matmul(pp[:, :], wob[:, dc * 1024 + oc * 128: dc * 1024 + (oc + 1) * 128],
                                       ogs[t4 % 2][:, dc * 512:(dc + 1) * 512],
                                       start=(dc == 0), stop=(dc == 7))
                    mm["att", t4, oc] = S.inc(ins, B.tT, 1)

                @B.block.vector
                def _(e, t4=t4, oc=oc):
                    e.wait_ge(B.tT, mm["att", t4, oc])
                    mm["fse", t4, oc] = S.inc(
                        e.tensor_add(fs[:, oc * 512:(oc + 1) * 512], pa[oc % 2][:, :],
                                     B.stream[:, oc * NT + HALO + t4 * 512: oc * NT + HALO + t4 * 512 + 512]),
                        B.vF, 1)

            for tc in range(4):
                gt = t4 * 4 + tc

                @B.block.sync
                def _(e, t4=t4, tc=tc, gt=gt):
                    if gt >= 2:
                        e.wait_ge(B.vQ, mm["oadd", gt - 2])
                    r0 = HALO + t4 * 512 + tc * 128
                    S.inc(e.dma_start(out=x0t[gt % 2][:, :], in_=B.inp["x0"][r0:r0 + 128, :]), B.dS, 16)
                    mm["x0l", gt] = S.val(B.dS)

                for oc in range(8):
                    gg = gt * 8 + oc

                    @B.block.tensor
                    def _(e, t4=t4, tc=tc, gt=gt, oc=oc, gg=gg):
                        e.wait_ge(B.vF, mm["fse", t4, oc])
                        if gg >= 2:
                            e.wait_ge(B.vQ, mm["oev", gg - 2])
                        ins = e.transpose(tp[gg % 2][:, 0:128],
                                          fs[:, oc * 512 + tc * 128: oc * 512 + (tc + 1) * 128],
                                          B.identf[:, :])
                        mm["ftr", gt, oc] = S.inc(ins, B.tT, 1)

                    @B.block.vector
                    def _(e, t4=t4, tc=tc, gt=gt, oc=oc, gg=gg):
                        if oc == 0:
                            e.wait_ge(B.dS, mm["x0l", gt])
                            if gt >= 2:
                                e.wait_ge(B.dS, mm["osave", gt - 2])
                        e.wait_ge(B.tT, mm["ftr", gt, oc])
                        mm["oev", gg] = S.inc(
                            e.tensor_add(ot[gt % 2][:, oc * 128:(oc + 1) * 128], tp[gg % 2][:, 0:128],
                                         x0t[gt % 2][:, oc * 128:(oc + 1) * 128]), B.vQ, 1)
                        if oc == 7:
                            mm["oadd", gt] = S.val(B.vQ)

                @B.block.sync
                def _(e, gt=gt):
                    e.wait_ge(B.vQ, mm["oadd", gt])
                    S.inc(e.dma_start(out=B.out[gt * 128:(gt + 1) * 128, :], in_=ot[gt % 2][:, :]),
                          B.dS, 16)
                    mm["osave", gt] = S.val(B.dS)

        @B.block.sync
        def _(e):
            e.wait_ge(B.dS, mm["osave", 15])



def kernel(**inputs):
    consts = _make_consts()
    in_maps = [_stage_core(c, inputs, consts) for c in range(8)]
    B = build_program(DEBUG_TAPS)
    res = run_bass_kernel_spmd(B.nc, in_maps, core_ids=list(range(8)))
    out = np.zeros((2, 8192, 1024), np.float32)
    for c in range(8):
        b, p = c // 4, c % 4
        out[b, p * P:(p + 1) * P] = res.results[c]["out"]
    if DEBUG_TAPS:
        kernel.last_taps = [{k.replace("tap_", ""): v for k, v in r.items()
                             if k.startswith("tap_") or k == "og_d"} for r in res.results]
    return out


# version 11
# speedup vs baseline: 1.0557x; 1.0326x over previous
import contextlib
import numpy as np
import ml_dtypes
import concourse.bass as bass
import concourse.mybir as mybir
from concourse.bass_utils import run_bass_kernel_spmd

F32 = mybir.dt.float32
BF16 = mybir.dt.bfloat16
AF = mybir.ActivationFunctionType
ALU = mybir.AluOpType
AX = mybir.AxisListType
AP = bass.AP

P = 2048
HALO = 656
NT = P + HALO
D = 1024
NEG = -1.0e9
WIN = 512
NKV = (P + 640) // 128
NQ = (P + 128) // 128
KW = NKV * 128
RG = [[0, 1, 2, 3], [4, 5, 6, 7]]

DEBUG_TAPS = ()
H2N_GLU = None
DEV_SKIP = ()
PB_CHUNKS = 16
PB_STAGE = 4



def _make_consts():
    shared = {}
    shared["ident"] = np.eye(128, dtype=ml_dtypes.bfloat16)
    r = np.arange(128)
    up01 = (r[None, :] >= r[:, None])
    shared["tri_up01"] = up01.astype(ml_dtypes.bfloat16)
    lo01 = (r[None, :] <= r[:, None])
    shared["addm_lo"] = np.where(~lo01, np.float32(NEG), 0).astype(ml_dtypes.bfloat16)
    shared["addm_up"] = np.where(~up01, np.float32(NEG), 0).astype(ml_dtypes.bfloat16)
    iota = r.astype(np.float32)[:, None]
    shared["iota_col"] = iota
    shared["iota513"] = (iota + 513.0)

    per_core = []
    for c in range(8):
        b, p = c // 4, c % 4
        s = p * P
        d = {}
        d["halo_gate"] = np.full((128, 1), 0.0 if p == 0 else 1.0, np.float32)
        ms = np.zeros((128, NQ), np.float32)
        for qt in range(-1, 16):
            col = qt + 1
            st = iota[:, 0].copy()
            if p == 0 and 0 <= qt <= 3:
                st = np.maximum(st, 512 - 128 * qt)
            ms[:, col] = st
        d["s_mstart"] = ms
        lo01m = np.where(r[None, :] > r[:, None], np.float32(NEG), 0)
        up01m = np.where(r[None, :] < r[:, None], np.float32(NEG), 0)
        smask = np.zeros((5, 128, 640), np.float32)
        smask[:, :, 0:128] += up01m
        smask[:, :, 512:640] += lo01m
        if p == 0:
            for itt in range(5):
                cols = np.arange(640)
                bad = (cols + itt * 128 >= 128) & (cols + itt * 128 < 640)
                smask[itt][:, bad] = NEG
        d["smask"] = smask.astype(ml_dtypes.bfloat16)
        vc = np.ones((128, NKV), np.float32)
        if p == 0:
            vc[:, :5] = 0.0
        d["kv_valid"] = vc
        j = np.arange(128, dtype=np.float64)
        invf = 10000.0 ** (-(j / 128.0))
        t = np.arange(s, s + P, dtype=np.float64)
        ang = invf[:, None] * t[None, :]
        d["ropeq_cos"] = (np.cos(ang) / 16.0).astype(np.float32)
        d["ropeq_sin"] = (np.sin(ang) / 16.0).astype(np.float32)
        d["ropek_cos"] = np.cos(ang).astype(np.float32)
        d["ropek_sin"] = np.sin(ang).astype(np.float32)
        d["ropek_cos_tm"] = np.ascontiguousarray(np.cos(ang).T.astype(np.float32))
        d["ropek_sin_tm"] = np.ascontiguousarray(np.sin(ang).T.astype(np.float32))
        d["pgate"] = np.broadcast_to((np.arange(4) < p).astype(np.float32)[None, :], (128, 4)).copy()
        hi = np.arange(0, 64, 2, dtype=np.float64)
        invx = 10000.0 ** (-hi / 64.0)
        tt = np.arange(s - HALO, s + P, dtype=np.float64)
        fr = tt[:, None] * invx[None, :]
        base = (hi + 0.4 * 64.0) / (1.4 * 64.0)
        power = (tt - 8192 // 2) / 256.0
        scale = base[None, :] ** power[:, None]
        cos2 = np.concatenate([np.cos(fr)] * 2, 1)
        sin2 = np.concatenate([np.sin(fr)] * 2, 1)
        sc2 = np.concatenate([scale] * 2, 1)
        csq = cos2 * sc2 * 8.0
        ssq = sin2 * sc2 * 8.0
        csk = cos2 / sc2
        ssk = sin2 / sc2
        ssmq = np.concatenate([-ssq[:, 32:64], ssq[:, 0:32]], 1)
        ssmk = np.concatenate([-ssk[:, 32:64], ssk[:, 0:32]], 1)
        d["xpos"] = np.concatenate([csq, ssmq, csk, ssmk], 1).astype(np.float32)
        per_core.append(d)
    return {"shared": shared, "per_core": per_core}


def _stage_core(c, inp, consts):
    b, p = c // 4, c % 4
    s = p * P
    x0 = np.asarray(inp["L0_attn_hidden_states"])[b]
    x0c = np.zeros((NT, D), np.float32)
    lo = s - HALO
    x0c[max(0, -lo):] = x0[max(lo, 0): s + P]
    m = {"x0": np.ascontiguousarray(x0c)}
    for k, v in inp.items():
        if k == "L0_attn_hidden_states":
            continue
        m[k] = np.ascontiguousarray(np.asarray(v, np.float32))
    m["lnvec"] = np.stack([np.stack([m["L0_ffn_ln_g"][l], m["L0_ffn_ln_b"][l]]) for l in range(2)])
    b1 = m["L0_ffn_b1"]
    b1a = np.zeros((2, 22, 128), np.float32); b1g = np.zeros((2, 22, 128), np.float32)
    b1a.reshape(2, -1)[:, :2730] = b1[:, :2730]
    b1g.reshape(2, -1)[:, :2730] = b1[:, 2730:]
    m["b1a_cols"] = np.ascontiguousarray(b1a.transpose(0, 2, 1))
    m["b1g_cols"] = np.ascontiguousarray(b1g.transpose(0, 2, 1))
    m["b2_cols"] = np.ascontiguousarray(m["L0_ffn_b2"].reshape(2, 8, 128).transpose(0, 2, 1))
    m["gn_rep"] = np.ascontiguousarray(np.broadcast_to(m["L0_attn_gnorm_w"][None, :], (128, 256)))
    m["qs_rep"] = np.ascontiguousarray(np.broadcast_to(m["L0_ffn_q_scale"][:, None, :], (2, 128, 64)))
    m["ks_rep"] = np.ascontiguousarray(np.broadcast_to(m["L0_ffn_k_scale"][:, None, :], (2, 128, 64)))
    w1 = m["L0_ffn_w1"]
    w1b = np.zeros((2, 2, 22, 1024, 128), np.float32)
    for l in range(2):
        for half in range(2):
            colbase = 2730 * half
            for j in range(22):
                lo2 = colbase + j * 128
                hi2 = min(colbase + 2730, lo2 + 128)
                w1b[l, half, j, :, 0:hi2 - lo2] = w1[l][:, lo2:hi2]
    m["w1_blk"] = w1b
    w2 = m["L0_ffn_w2"]
    w2b = np.zeros((2, 22, 128, 1024), np.float32)
    for l in range(2):
        for j in range(22):
            lo2 = j * 128
            hi2 = min(2730, lo2 + 128)
            w2b[l, j, 0:hi2 - lo2, :] = w2[l][lo2:hi2, :]
    m["w2_blk"] = w2b
    m["skw2aug"] = np.concatenate([m["L0_attn_sk_w2"], m["L0_attn_sk_b2"][None]], 0)
    m["svw2aug"] = np.concatenate([m["L0_attn_sv_w2"], m["L0_attn_sv_b2"][None]], 0)
    m.update(consts["shared"])
    m.update(consts["per_core"][c])
    return m



class Sched:
    def __init__(self):
        self.n = {}

    def inc(self, ins, sem, k=1):
        ins.then_inc(sem, k)
        self.n[sem.name] = self.n.get(sem.name, 0) + k
        return self.n[sem.name]

    def bump(self, sem, k):
        self.n[sem.name] = self.n.get(sem.name, 0) + k
        return self.n[sem.name]

    def val(self, sem):
        return self.n.get(sem.name, 0)

    def raw(self, e, ins, sem):
        self.inc(ins, sem, 1)
        e.wait_ge(sem, self.val(sem))


class Bld:
    pass


def _barrier(B):
    B.phase += 1
    tgt = B.phase
    pairs = [("sync", B.phS), ("tensor", B.phT), ("vector", B.phV),
             ("scalar", B.phA), ("gpsimd", B.phG)]
    for name, sem in pairs:
        def _mk(sm):
            def _f(e):
                B.sched.inc(e.nop(), sm, 1)
                for _n2, s2 in pairs:
                    e.wait_ge(s2, tgt)
            return _f
        getattr(B.block, name)(_mk(sem))


def build_program(taps=()):
    es = contextlib.ExitStack()
    nc = bass.Bass("TRN2", target_bir_lowering=False)
    B = Bld()
    B.nc, B.es, B.sched = nc, es, Sched()
    B.phase = 0
    B.taps = set(taps)
    B.tapped = {}

    dt_ = {}
    def din(name, shape, dt=F32):
        dt_[name] = nc.dram_tensor(name, list(shape), dt, kind="ExternalInput")
    din("x0", (NT, D))
    for nm, sh in [("L0_attn_Wq", (D, D)), ("L0_attn_Wk", (D, D)), ("L0_attn_Wv", (D, D)),
                   ("L0_attn_Wg", (D, D)), ("L0_attn_Wo", (D, D)),
                   ("L0_attn_sk_w1", (D, 16)), ("L0_attn_sk_w2", (16, 256)), ("L0_attn_sk_b2", (256,)),
                   ("L0_attn_sv_w1", (D, 16)), ("L0_attn_sv_w2", (16, 256)), ("L0_attn_sv_b2", (256,)),
                   ("L0_attn_gnorm_w", (256,)),
                   ("L0_ffn_wqkv", (2, D, 1536)), ("L0_ffn_q_scale", (2, 64)), ("L0_ffn_k_scale", (2, 64)),
                   ("L0_ffn_wo", (2, 512, D)), ("L0_ffn_ln_g", (2, D)), ("L0_ffn_ln_b", (2, D)),
                   ("L0_ffn_w1", (2, D, 5460)), ("L0_ffn_b1", (2, 5460)),
                   ("L0_ffn_w2", (2, 2730, D)), ("L0_ffn_b2", (2, D)),
                   ("w1_blk", (2, 2, 22, 1024, 128)), ("w2_blk", (2, 22, 128, 1024)),
                   ("lnvec", (2, 2, D)), ("b1a_cols", (2, 128, 22)), ("b1g_cols", (2, 128, 22)),
                   ("b2_cols", (2, 128, 8)),
                   ("qs_rep", (2, 128, 64)), ("ks_rep", (2, 128, 64)),
                   ("skw2aug", (17, 256)), ("svw2aug", (17, 256)),
                   ("iota_col", (128, 1)), ("iota513", (128, 1)),
                   ("halo_gate", (128, 1)), ("s_mstart", (128, NQ)), ("kv_valid", (128, NKV)),
                   ("ropeq_cos", (128, P)), ("ropeq_sin", (128, P)),
                   ("ropek_cos", (128, P)), ("ropek_sin", (128, P)),
                   ("ropek_cos_tm", (P, 128)), ("ropek_sin_tm", (P, 128)),
                   ("pgate", (128, 4)), ("gn_rep", (128, 256)), ("xpos", (NT, 256))]:
        din(nm, sh)
    din("ident", (128, 128), BF16)
    din("tri_up01", (128, 128), BF16)
    din("addm_lo", (128, 128), BF16)
    din("addm_up", (128, 128), BF16)
    din("smask", (5, 128, 640), BF16)
    B.inp = dt_
    B.out = nc.dram_tensor("out", [P, D], F32, kind="ExternalOutput")

    B.ag1_in = nc.dram_tensor("ag1_in", [128, 2560], F32)
    B.kv_d = nc.dram_tensor("kv_d", [128, 16 * 2560], BF16)
    B.ag1_out = nc.dram_tensor("ag1_out", [512, 2560], F32)
    B.ag2_in = nc.dram_tensor("ag2_in", [128, 8 * HALO], F32)
    B.ag2_out = nc.dram_tensor("ag2_out", [512, 8 * HALO], F32)
    B.og_d = nc.dram_tensor("og_d", [128, 8 * P], BF16,
                            kind="ExternalOutput" if "og" in taps else "Internal")
    if "og" in taps:
        B.tapped["og"] = True

    def sem(name):
        return es.enter_context(nc.semaphore(name))
    for nm in ["dS", "dG", "dA", "tT", "tV", "tA", "tG", "cc", "vZ", "aZ", "gZ",
               "phS", "phT", "phV", "phA", "phG", "e1", "e2", "e3", "e4", "e5", "e6",
               "xwS", "vF", "vQ", "vY", "vN", "vC", "vR", "vE", "vM",
               "aP", "aK", "aN", "aE", "gK", "aSq", "vSum"]:
        setattr(B, nm, sem(nm))

    sb = nc.alloc_sbuf_tensor
    B.stream = sb("stream", [128, 8 * NT], F32)
    B.ident = sb("ident_sb", [128, 128], BF16)
    B.tri_up = sb("tri_up_sb", [128, 128], BF16)
    B.addm_lo = sb("addm_lo_sb", [128, 128], BF16)
    B.addm_up = sb("addm_up_sb", [128, 128], BF16)
    B.identf = sb("identf_sb", [128, 128], F32)
    B.ones1 = sb("ones1", [1, 512], BF16)
    B.onescol = sb("onescol", [128, 1], BF16)
    B.onescolf = sb("onescolf", [128, 1], F32)
    B.iota = sb("iota_sb", [128, 1], F32)
    B.iota513 = sb("iota513_sb", [128, 1], F32)
    B.hgate = sb("hgate", [128, 1], F32)
    B.smstart = sb("smstart", [128, NQ], F32)
    B.kvvalid = sb("kvvalid", [128, NKV], F32)
    B.kvvalid_bf = sb("kvvalid_bf", [128, NKV], BF16)
    B.cst = sb("cst", [128, 4], F32)

    with nc.Block() as block:
        B.block = block

        @block.sync
        def _(e):
            for dst, src in [(B.ident, dt_["ident"]), (B.tri_up, dt_["tri_up01"]),
                             (B.addm_lo, dt_["addm_lo"]), (B.addm_up, dt_["addm_up"]),
                             (B.iota, dt_["iota_col"]), (B.iota513, dt_["iota513"]),
                             (B.hgate, dt_["halo_gate"]), (B.smstart, dt_["s_mstart"]),
                             (B.kvvalid, dt_["kv_valid"])]:
                B.sched.inc(e.dma_start(out=dst[:, :], in_=src[:, :]), B.dS, 16)


        @block.vector
        def _(e):
            e.wait_ge(B.dS, B.sched.val(B.dS))
            e.memset(B.ones1[:, :], 1.0)
            e.memset(B.onescol[:, :], 1.0)
            e.memset(B.onescolf[:, :], 1.0)
            e.memset(B.cst[:, 0:1], 1e-5)
            e.tensor_copy(B.identf[:, :], B.ident[:, :])
            e.tensor_copy(B.kvvalid_bf[:, :], B.kvvalid[:, :])

        _barrier(B)
        phase_xn(B)
        _barrier(B)
        if "passA" not in DEV_SKIP:
            phase_passA(B)
        _barrier(B)
        for lyr in (0, 1):
            o0stack = contextlib.ExitStack()
            B.o0 = o0stack.enter_context(nc.sbuf_tensor(f"o0_{lyr}", [128, 4 * NQ * 128], BF16, side="right"))
            phase_ffn_subA(B, lyr=lyr)
            _barrier(B)
            if lyr == 0 and "passB" not in DEV_SKIP:
                phase_passB(B)
                _barrier(B)
            phase_ffn_rest(B, lyr=lyr, o0stack=o0stack)
            _barrier(B)
        phase_final(B)

    es.close()
    return B


def _tap_sbuf(B, name, ap, shape, dtype=F32):
    if name not in B.taps:
        return
    t = B.nc.dram_tensor("tap_" + name, list(shape), dtype, kind="ExternalOutput")
    B.tapped[name] = True
    _barrier(B)

    @B.block.sync
    def _(e):
        B.sched.inc(e.dma_start(out=t[:, :], in_=ap), B.dS, 16)
        e.wait_ge(B.dS, B.sched.val(B.dS))
    _barrier(B)



def phase_xn(B):
    nc, S = B.nc, B.sched
    with contextlib.ExitStack() as ph:
        x0t = [ph.enter_context(nc.sbuf_tensor(f"x0t{i}", [128, D], F32)) for i in range(2)]
        xnt = [ph.enter_context(nc.sbuf_tensor(f"xnt{i}", [128, D], F32)) for i in range(2)]
        scr = ph.enter_context(nc.sbuf_tensor("xn_scr", [128, D], BF16))
        st_ = ph.enter_context(nc.sbuf_tensor("xn_stat", [128, 16], F32))
        scr2 = ph.enter_context(nc.sbuf_tensor("xn_scr2", [128, D], F32))
        tp = [ph.enter_context(nc.psum_tensor(f"xn_ps{i}", [128, 512], F32)) for i in range(2)]

        starts = [i * 128 for i in range(NT // 128)] + ([NT - 128] if NT % 128 else [])
        NTI = len(starts)
        b_dS, b_tV, b_tA, b_tT, b_e1, b_e2 = (S.val(x) for x in (B.dS, B.tV, B.tA, B.tT, B.e1, B.e2))

        @B.block.sync
        def _(e):
            for i in range(NTI):
                if i >= 2:
                    e.wait_ge(B.tV, b_tV + (i - 2) + 1)
                S.inc(e.dma_start(out=x0t[i % 2][:, :],
                                  in_=B.inp["x0"][starts[i]:starts[i] + 128, :]), B.dS, 16)

        def stp(i):
            return st_[:, (i % 2) * 8:(i % 2) * 8 + 8]

        def _evac(e, i):
            for j in range(8):
                g = i * 8 + j
                e.wait_ge(B.tT, b_tT + g + 1)
                S.inc(e.copy(B.stream[:, j * NT + starts[i]: j * NT + starts[i] + 128],
                             tp[g % 2][:, 0:128]), B.e2, 1)

        @B.block.scalar
        def _(e):
            for i in range(NTI):
                e.wait_ge(B.dS, b_dS + (i + 1) * 16)
                st = stp(i)
                S.inc(e.activation(scr[:, :], x0t[i % 2][:, :], AF.Square,
                                   accum_out=st[:, 1:2]), B.tA, 1)
                e.wait_ge(B.e1, b_e1 + i + 1)
                S.inc(e.activation(st[:, 5:6], st[:, 4:5], AF.Sqrt, bias=B.cst[:, 0:1]), B.tA, 1)
                if i >= 1:
                    _evac(e, i - 1)
            _evac(e, NTI - 1)

        @B.block.vector
        def _(e):
            for i in range(NTI):
                st = stp(i)
                e.wait_ge(B.dS, b_dS + (i + 1) * 16)
                S.raw(e, e.tensor_reduce(st[:, 0:1], x0t[i % 2][:, :], AX.X, ALU.add), B.vZ)
                e.wait_ge(B.tA, b_tA + 2 * i + 1)
                e.tensor_scalar(st[:, 2:3], st[:, 0:1], 1.0 / D, None, ALU.mult)
                S.raw(e, e.tensor_scalar(st[:, 3:4], st[:, 1:2], 1.0 / D, None, ALU.mult), B.vZ)
                S.raw(e, e.tensor_tensor(st[:, 7:8], st[:, 2:3], st[:, 2:3], ALU.mult), B.vZ)
                S.inc(e.tensor_tensor(st[:, 4:5], st[:, 3:4], st[:, 7:8], ALU.subtract), B.e1, 1)
                e.wait_ge(B.tA, b_tA + 2 * i + 2)
                S.raw(e, e.reciprocal(st[:, 6:7], st[:, 5:6]), B.vZ)
                if i >= 2:
                    e.wait_ge(B.tT, b_tT + (i - 2) * 8 + 8)
                S.raw(e, e.tensor_scalar(scr2[:, :], x0t[i % 2][:, :], st[:, 2:3], None, ALU.subtract), B.vZ)
                S.inc(e.tensor_scalar(xnt[i % 2][:, :], scr2[:, :], st[:, 6:7], None, ALU.mult), B.tV, 1)

        @B.block.tensor
        def _(e):
            for i in range(NTI):
                e.wait_ge(B.tV, b_tV + i + 1)
                for j in range(8):
                    g = i * 8 + j
                    if g >= 2:
                        e.wait_ge(B.e2, b_e2 + g - 1)
                    S.inc(e.transpose(tp[g % 2][:, 0:128], xnt[i % 2][:, j * 128:(j + 1) * 128],
                                      B.identf[:, :]), B.tT, 1)

    _tap_sbuf(B, "st", st_[:, :], (128, 16))
    _tap_sbuf(B, "xn", B.stream[:, 0:8 * NT], (128, 8 * NT))



def phase_passA(B):
    nc, S = B.nc, B.sched
    with contextlib.ExitStack() as ph:
        sbt = lambda n, sh, dt=F32: ph.enter_context(nc.sbuf_tensor(f"{n}_pa", sh, dt))
        wkb = sbt("wkb", [128, 8 * 1024], BF16)
        wvb = sbt("wvb", [128, 8 * 1024], BF16)
        skw1 = sbt("skw1", [128, 256], BF16)
        skw2 = sbt("skw2", [17, 512], BF16)
        xnb = sbt("xnb", [128, 8 * 512], BF16)
        rct = sbt("rct", [128, 4 * 256])
        ka = sbt("ka", [128, 512])
        kb = sbt("kb", [128, 512])
        khat = sbt("khat", [128, 1024], BF16)
        vtm = sbt("vtm", [128, 1024], BF16)
        t1aug = sbt("t1aug", [17, 1024], BF16)
        scf = sbt("scf", [128, 512])
        wkv = sbt("wkv", [128, 512], BF16)
        s1A = [sbt(f"s1A{i}", [128, 1024]) for i in range(2)]
        s2A = [sbt(f"s2A{i}", [64, 1024]) for i in range(2)]
        zA = [sbt(f"zA{i}", [1, 512]) for i in range(2)]
        onesbf = sbt("onesbf", [128, 1], BF16)
        pj = ph.enter_context(nc.psum_tensor("pj_pa", [128, 1024], F32))
        skbank = ph.enter_context(nc.psum_tensor("skbank_pa", [128, 512], F32))
        st1 = ph.enter_context(nc.psum_tensor("st1_pa", [128, 1024], F32))
        st2 = ph.enter_context(nc.psum_tensor("st2_pa", [64, 1024], F32))
        stz = ph.enter_context(nc.psum_tensor("stz_pa", [1, 512], F32))

        b = {k: S.val(getattr(B, k)) for k in ["dS", "dG", "tT", "vF", "vQ", "vN", "aP", "cc"]}
        mm = {}

        @B.block.gpsimd
        def _(e):
            for fc in range(8):
                S.inc(e.dma_start(out=wkb[:, fc * 1024:(fc + 1) * 1024],
                                  in_=B.inp["L0_attn_Wk"][fc * 128:(fc + 1) * 128, :]), B.dG, 16)
                S.inc(e.dma_start(out=wvb[:, fc * 1024:(fc + 1) * 1024],
                                  in_=B.inp["L0_attn_Wv"][fc * 128:(fc + 1) * 128, :]), B.dG, 16)
            S.inc(e.dma_start(out=skw1[:, 0:128],
                              in_=AP(B.inp["L0_attn_sk_w1"], 0, [[16, 128], [2048, 8], [1, 16]])), B.dG, 16)
            S.inc(e.dma_start(out=skw1[:, 128:256],
                              in_=AP(B.inp["L0_attn_sv_w1"], 0, [[16, 128], [2048, 8], [1, 16]])), B.dG, 16)
            S.inc(e.dma_start(out=skw2[:, 0:256], in_=B.inp["skw2aug"][:, :]), B.dG, 16)
            S.inc(e.dma_start(out=skw2[:, 256:512], in_=B.inp["svw2aug"][:, :]), B.dG, 16)

        @B.block.vector
        def _(e):
            e.memset(onesbf[:, :], 1.0)
            e.memset(s1A[1][:, :], 0.0)
            e.memset(s2A[1][0:64, :], 0.0)
            e.memset(zA[1][0:1, :], 0.0)
            mm["init"] = S.inc(e.memset(t1aug[0:17, :], 1.0), B.vF, 1)

        for st in range(4):
            @B.block.sync
            def _(e, st=st):
                if st > 0:
                    e.wait_ge(B.vF, mm["rope", st * 4 - 1])
                for c in range(4):
                    r0 = st * 512 + c * 128
                    S.inc(e.dma_start(out=rct[:, c * 256:c * 256 + 128],
                                      in_=B.inp["ropek_cos_tm"][r0:r0 + 128, :]), B.dS, 16)
                    S.inc(e.dma_start(out=rct[:, c * 256 + 128:c * 256 + 256],
                                      in_=B.inp["ropek_sin_tm"][r0:r0 + 128, :]), B.dS, 16)
                mm["rctd", st] = S.val(B.dS)

            @B.block.vector
            def _(e, st=st):
                if st > 0:
                    e.wait_ge(B.tT, mm["stu", st * 4 - 1])
                for ci in range(8):
                    e.tensor_copy(xnb[:, ci * 512:(ci + 1) * 512],
                                  B.stream[:, ci * NT + HALO + st * 512: ci * NT + HALO + st * 512 + 512])
                mm["xnb", st] = S.inc(e.engine_nop(), B.vQ, 1)

            @B.block.tensor
            def _(e, st=st):
                e.wait_ge(B.vQ, mm["xnb", st])
                if st == 0:
                    e.wait_ge(B.dG, b["dG"] + 20 * 16)
                    e.wait_ge(B.vF, mm["init"])
                else:
                    e.wait_ge(B.vF, mm["clip", st * 4 - 1])
                ins = None
                for fc in range(8):
                    ins = e.matmul(skbank[0:16, 0:512], skw1[:, fc * 16:(fc + 1) * 16],
                                   xnb[:, fc * 512:(fc + 1) * 512], start=(fc == 0), stop=(fc == 7))
                mm["t1", st, 0] = S.inc(ins, B.tT, 1)

            @B.block.vector
            def _(e, st=st):
                e.wait_ge(B.tT, mm["t1", st, 0])
                mm["t1e", st, 0] = S.inc(
                    e.tensor_copy(t1aug[0:16, 0:512], skbank[0:16, 0:512]), B.vF, 1)

            @B.block.tensor
            def _(e, st=st):
                e.wait_ge(B.vF, mm["t1e", st, 0])
                ins = None
                for fc in range(8):
                    ins = e.matmul(skbank[0:16, 0:512], skw1[:, 128 + fc * 16: 128 + (fc + 1) * 16],
                                   xnb[:, fc * 512:(fc + 1) * 512], start=(fc == 0), stop=(fc == 7))
                mm["t1", st, 1] = S.inc(ins, B.tT, 1)

            @B.block.vector
            def _(e, st=st):
                e.wait_ge(B.tT, mm["t1", st, 1])
                mm["t1e", st, 1] = S.inc(
                    e.tensor_copy(t1aug[0:16, 512:1024], skbank[0:16, 0:512]), B.vF, 1)

            for c in range(4):
                g = st * 4 + c

                @B.block.tensor
                def _(e, st=st, c=c, g=g):
                    if g > 0:
                        e.wait_ge(B.vF, mm["vtme", g - 1])
                    ins = None
                    for fc in range(8):
                        for hb in range(2):
                            ins = e.matmul(pj[:, hb * 512:(hb + 1) * 512],
                                           xnb[:, fc * 512 + c * 128: fc * 512 + (c + 1) * 128],
                                           wkb[:, fc * 1024 + hb * 512: fc * 1024 + (hb + 1) * 512],
                                           start=(fc == 0), stop=(fc == 7))
                    mm["kp", g] = S.inc(ins, B.tT, 1)
                    e.wait_ge(B.vF, mm["t1e", st, 1])
                    if g > 0:
                        e.wait_ge(B.vF, mm["clip", g - 1])
                    e.matmul(skbank[:, 0:256], t1aug[:, c * 128:(c + 1) * 128], skw2[:, 0:256],
                             start=True, stop=True)
                    ins = e.matmul(skbank[:, 256:512], t1aug[:, 512 + c * 128: 512 + (c + 1) * 128],
                                   skw2[:, 256:512], start=True, stop=True)
                    mm["sk", g] = S.inc(ins, B.tT, 1)

                @B.block.vector
                def _(e, st=st, c=c, g=g):
                    e.wait_ge(B.tT, mm["kp", g])
                    e.wait_ge(B.dS, mm["rctd", st])
                    cosb = rct[:, c * 256:c * 256 + 128].unsqueeze(1).broadcast_to([128, 4, 128])
                    sinb = rct[:, c * 256 + 128:c * 256 + 256].unsqueeze(1).broadcast_to([128, 4, 128])
                    pjv = pj[:, :].rearrange("p (h two j) -> p h two j", two=2, j=128)
                    khv = khat[:, :].rearrange("p (h two j) -> p h two j", two=2, j=128)
                    kav = ka[:, :].rearrange("p (h j) -> p h j", j=128)
                    kbv = kb[:, :].rearrange("p (h j) -> p h j", j=128)
                    if g > 0:
                        e.wait_ge(B.tT, mm["stu", g - 1])
                        e.wait_ge(B.dS, mm["spill", g - 1])
                    e.tensor_tensor(kav, pjv[:, :, 0, :], cosb, ALU.mult)
                    S.raw(e, e.tensor_tensor(kbv, pjv[:, :, 1, :], sinb, ALU.mult), B.vZ)
                    e.tensor_tensor(khv[:, :, 0, :], kav, kbv, ALU.subtract)
                    e.tensor_tensor(kav, pjv[:, :, 1, :], cosb, ALU.mult)
                    S.raw(e, e.tensor_tensor(kbv, pjv[:, :, 0, :], sinb, ALU.mult), B.vZ)
                    mm["rope", g] = S.inc(e.tensor_add(khv[:, :, 1, :], kav, kbv), B.vF, 1)
                    e.wait_ge(B.tT, mm["sk", g])
                    mm["clip", g] = S.inc(
                        e.tensor_scalar(scf[:, :], skbank[:, :], -32.0, 32.0, ALU.max, ALU.min), B.vF, 1)

                @B.block.scalar
                def _(e, g=g):
                    e.wait_ge(B.vF, mm["clip", g])
                    if g > 0:
                        e.wait_ge(B.tT, mm["stu", g - 1])
                        e.wait_ge(B.dS, mm["spill", g - 1])
                    mm["exp", g] = S.inc(e.activation(wkv[:, :], scf[:, :], AF.Exp), B.aP, 1)

                @B.block.tensor
                def _(e, st=st, c=c, g=g):
                    e.wait_ge(B.vF, mm["rope", g])
                    ins = None
                    for fc in range(8):
                        for hb in range(2):
                            ins = e.matmul(pj[:, hb * 512:(hb + 1) * 512],
                                           xnb[:, fc * 512 + c * 128: fc * 512 + (c + 1) * 128],
                                           wvb[:, fc * 1024 + hb * 512: fc * 1024 + (hb + 1) * 512],
                                           start=(fc == 0), stop=(fc == 7))
                    mm["vp", g] = S.inc(ins, B.tT, 1)

                @B.block.vector
                def _(e, g=g):
                    e.wait_ge(B.tT, mm["vp", g])
                    if g > 0:
                        e.wait_ge(B.dS, mm["spill", g - 1])
                    mm["vtme", g] = S.inc(e.tensor_copy(vtm[:, :], pj[:, :]), B.vF, 1)

                @B.block.tensor
                def _(e, g=g):
                    e.wait_ge(B.vF, mm["vtme", g])
                    e.wait_ge(B.aP, mm["exp", g])
                    if g > 0:
                        e.wait_ge(B.vF, mm["sacc", g - 1])
                    ins = None
                    for h in range(4):
                        for c2 in range(2):
                            e.matmul(st1[:, (2 * h + c2) * 64:(2 * h + c2) * 64 + 64],
                                     khat[:, h * 256 + c2 * 128: h * 256 + (c2 + 1) * 128],
                                     wkv[:, h * 64:(h + 1) * 64], start=True, stop=True)
                        e.matmul(st2[0:64, h * 256:(h + 1) * 256], wkv[:, 256 + h * 64: 256 + (h + 1) * 64],
                                 vtm[:, h * 256:(h + 1) * 256], start=True, stop=True)
                        e.matmul(stz[0:1, h * 64:(h + 1) * 64], onesbf[:, :], wkv[:, h * 64:(h + 1) * 64],
                                 start=True, stop=True)
                        ins = e.matmul(stz[0:1, 256 + h * 64: 256 + (h + 1) * 64], onesbf[:, :],
                                       wkv[:, 256 + h * 64: 256 + (h + 1) * 64],
                                       start=True, stop=True)
                    mm["stu", g] = S.inc(ins, B.tT, 1)

                @B.block.vector
                def _(e, g=g):
                    e.wait_ge(B.tT, mm["stu", g])
                    e.tensor_tensor(s1A[g % 2][:, :], st1[:, :], s1A[(g + 1) % 2][:, :], ALU.add)
                    e.tensor_tensor(s2A[g % 2][0:64, :], st2[0:64, :], s2A[(g + 1) % 2][0:64, :], ALU.add)
                    mm["sacc", g] = S.inc(
                        e.tensor_tensor(zA[g % 2][0:1, :], stz[0:1, :], zA[(g + 1) % 2][0:1, :],
                                        ALU.add), B.vF, 1)

                @B.block.sync
                def _(e, g=g):
                    e.wait_ge(B.vF, mm["vtme", g])
                    e.wait_ge(B.vF, mm["rope", g])
                    e.wait_ge(B.aP, mm["exp", g])
                    S.inc(e.dma_start(out=B.kv_d[:, g * 2560: g * 2560 + 1024], in_=khat[:, :]),
                          B.dS, 16)
                    S.inc(e.dma_start(out=B.kv_d[:, g * 2560 + 1024: g * 2560 + 2048], in_=vtm[:, :]),
                          B.dS, 16)
                    S.inc(e.dma_start(out=B.kv_d[:, g * 2560 + 2048: g * 2560 + 2560], in_=wkv[:, :]),
                          B.dS, 16)
                    mm["spill", g] = S.val(B.dS)

        @B.block.sync
        def _(e):
            e.wait_ge(B.vF, mm["sacc", 15])
            S.inc(e.dma_start(out=B.ag1_in[:, 0:1024], in_=s1A[1][:, :]), B.dS, 16)
            S.inc(e.dma_start(out=B.ag1_in[0:64, 1024:2048], in_=s2A[1][0:64, :]), B.dS, 16)
            S.inc(e.dma_start(out=B.ag1_in[0:1, 2048:2560], in_=zA[1][0:1, :]), B.dS, 16)

        @B.block.gpsimd
        def _(e):
            e.wait_ge(B.dS, S.val(B.dS))
            S.inc(e.collective_compute("AllGather", ALU.bypass, replica_groups=RG,
                                       ins=[B.ag1_in.ap()], outs=[B.ag1_out.ap()]), B.cc, 1)


def phase_passB(B):
    nc, S = B.nc, B.sched
    mm = {}
    phx = contextlib.ExitStack()
    sbx = lambda n, sh, dt=F32: phx.enter_context(nc.sbuf_tensor(f"{n}_pbx", sh, dt))
    s1f = [sbx(f"s1f{i}", [128, 1024]) for i in range(2)]
    s2f = [sbx(f"s2f{i}", [64, 1024]) for i in range(2)]
    zf = [sbx(f"zf{i}", [1, 512]) for i in range(2)]
    s1sb = sbx("s1sb", [128, 1024], BF16)
    s2sb = sbx("s2sb", [64, 1024], BF16)
    zsbb = sbx("zsbb", [1, 512], BF16)
    with contextlib.ExitStack() as ph0:
        acc = [ph0.enter_context(nc.sbuf_tensor(f"pacc{i}", [128, 2560], F32)) for i in range(3)]
        pg = ph0.enter_context(nc.sbuf_tensor("pg_sb", [128, 4], F32))

        @B.block.gpsimd
        def _(e):
            e.wait_ge(B.cc, S.val(B.cc))
            S.inc(e.dma_start(out=pg[:, :], in_=B.inp["pgate"][:, :]), B.dG, 16)
            for r in range(3):
                S.inc(e.dma_start(out=acc[r][:, :], in_=B.ag1_out[r * 128:(r + 1) * 128, :]), B.dG, 16)

        @B.block.vector
        def _(e):
            e.wait_ge(B.dG, S.val(B.dG))
            e.tensor_scalar(s1f[1][:, :], acc[0][:, 0:1024], pg[:, 0:1], None, ALU.mult)
            e.tensor_scalar(s2f[1][0:64, :], acc[0][0:64, 1024:2048], pg[0:64, 0:1], None, ALU.mult)
            S.raw(e, e.tensor_scalar(zf[1][0:1, :], acc[0][0:1, 2048:2560], pg[0:1, 0:1],
                                     None, ALU.mult), B.vZ)
            e.scalar_tensor_tensor(s1f[0][:, :], acc[1][:, 0:1024], pg[:, 1:2], s1f[1][:, :],
                                   ALU.mult, ALU.add)
            e.scalar_tensor_tensor(s2f[0][0:64, :], acc[1][0:64, 1024:2048], pg[0:64, 1:2],
                                   s2f[1][0:64, :], ALU.mult, ALU.add)
            S.raw(e, e.scalar_tensor_tensor(zf[0][0:1, :], acc[1][0:1, 2048:2560], pg[0:1, 1:2],
                                            zf[1][0:1, :], ALU.mult, ALU.add), B.vZ)
            e.scalar_tensor_tensor(s1f[1][:, :], acc[2][:, 0:1024], pg[:, 2:3], s1f[0][:, :],
                                   ALU.mult, ALU.add)
            e.scalar_tensor_tensor(s2f[1][0:64, :], acc[2][0:64, 1024:2048], pg[0:64, 2:3],
                                   s2f[0][0:64, :], ALU.mult, ALU.add)
            S.raw(e, e.scalar_tensor_tensor(zf[1][0:1, :], acc[2][0:1, 2048:2560], pg[0:1, 2:3],
                                            zf[0][0:1, :], ALU.mult, ALU.add), B.vZ)
            e.tensor_copy(s1sb[:, :], s1f[1][:, :])
            e.tensor_copy(s2sb[0:64, :], s2f[1][0:64, :])
            mm["sinit"] = S.inc(e.tensor_copy(zsbb[0:1, :], zf[1][0:1, :]), B.vF, 1)

    _barrier(B)

    with contextlib.ExitStack() as ph:
        sbt = lambda n, sh, dt=F32: ph.enter_context(nc.sbuf_tensor(f"{n}_pb", sh, dt))
        wqb = sbt("wqb", [128, 8 * 1024], BF16)
        wgb = sbt("wgb", [128, 8 * 1024], BF16)
        xnb = sbt("xnb", [128, 8 * 512], BF16)
        rpt = [sbt(f"rpt{i}", [128, 256], F32) for i in range(2)]
        qf = sbt("qf", [128, 8 * 128], BF16)
        kf = sbt("kf", [128, 8 * 128], BF16)
        qfb = qf[:, :].bitcast(F32)
        kfb = kf[:, :].bitcast(F32)
        gtm = sbt("gtm", [128, 1024], BF16)
        ld_ = [sbt(f"ldkv{i}", [128, 2560], BF16) for i in range(2)]
        scf = sbt("scf", [128, 512])
        wvT = sbt("wvT", [64, 512], BF16)
        stm = sbt("stm", [128, 512], BF16)
        ptbi = sbt("ptbi", [128, 256], BF16)
        atm = sbt("atm", [128, 512], BF16)
        rz = sbt("rz", [128, 512])
        lg = sbt("lg", [128, 256])
        ev = sbt("ev", [128, 256])
        nmx = sbt("nmx", [128, 12])
        ptb = sbt("ptb", [64, 512], BF16)
        hv1 = sbt("hv1", [128, 512])
        hv2 = sbt("hv2", [128, 512])
        ssq = sbt("ssq", [128, 16])
        gnr = sbt("gnr", [128, 256], BF16)
        ogsb = sbt("ogsb", [128, 1024], BF16)
        ogfm = sbt("ogfm", [128, 1024], BF16)
        onesbf = sbt("onesbf2", [128, 1], BF16)
        pq2 = ph.enter_context(nc.psum_tensor("pq2_pb", [128, 1024], F32))
        pk2 = ph.enter_context(nc.psum_tensor("pk2_pb", [128, 1024], F32))
        st1 = ph.enter_context(nc.psum_tensor("st1_pb", [128, 1024], F32))
        st2 = ph.enter_context(nc.psum_tensor("st2_pb", [64, 1024], F32))
        pq2b = pq2[:, :].bitcast(BF16)
        pk2b = pk2[:, :].bitcast(BF16)

        b = {k: S.val(getattr(B, k)) for k in ["dS", "dG", "tT", "vF", "vQ", "aP", "cc"]}

        @B.block.gpsimd
        def _(e):
            for fc in range(8):
                for w_, nmW in ((wqb, "L0_attn_Wq"), (wgb, "L0_attn_Wg")):
                    S.inc(e.dma_start(out=w_[:, fc * 1024:(fc + 1) * 1024],
                                      in_=B.inp[nmW][fc * 128:(fc + 1) * 128, :]), B.dG, 16)
            S.inc(e.dma_start(out=gnr[:, :], in_=B.inp["gn_rep"][:, :]), B.dG, 16)
            mm["wload"] = S.val(B.dG)

        @B.block.vector
        def _(e):
            mm["minit"] = S.inc(e.memset(onesbf[:, :], 1.0), B.vF, 1)

        for st4 in range(4):
            if st4 * 4 >= PB_CHUNKS:
                break

            @B.block.vector
            def _(e, st4=st4):
                if st4 > 0:
                    e.wait_ge(B.tT, mm["stu", st4 * 4 - 1])
                for ci in range(8):
                    e.tensor_copy(xnb[:, ci * 512:(ci + 1) * 512],
                                  B.stream[:, ci * NT + HALO + st4 * 512: ci * NT + HALO + st4 * 512 + 512])
                mm["xnb", st4] = S.inc(e.engine_nop(), B.vQ, 1)

            for c in range(4):
                g = st4 * 4 + c
                if g >= PB_CHUNKS:
                    break

                @B.block.sync
                def _(e, st4=st4, c=c, g=g):
                    if g >= 2:
                        e.wait_ge(B.vF, mm["qrope", g - 2, 3])
                    t0 = st4 * 512 + c * 128
                    for i, nmT in enumerate(["ropeq_cos", "ropeq_sin"]):
                        S.inc(e.dma_start(out=rpt[g % 2][:, i * 128:(i + 1) * 128],
                                          in_=B.inp[nmT][:, t0:t0 + 128]), B.dS, 16)
                    mm["rpt", g] = S.val(B.dS)

                @B.block.gpsimd
                def _(e, g=g):
                    if g >= 2:
                        e.wait_ge(B.tT, mm["scend", g - 2])
                    S.inc(e.dma_start(out=ld_[g % 2][:, :],
                                      in_=B.kv_d[:, g * 2560:(g + 1) * 2560]), B.dG, 16)
                    mm["ldkv", g] = S.val(B.dG)

                @B.block.tensor
                def _(e, st4=st4, c=c, g=g):
                    e.wait_ge(B.vQ, mm["xnb", st4])
                    if g == 0:
                        e.wait_ge(B.dG, mm["wload"])
                        e.wait_ge(B.vF, mm["sinit"])
                        e.wait_ge(B.vF, mm["minit"])
                    else:
                        e.wait_ge(B.vF, mm["zup", g - 1])
                        e.wait_ge(B.vF, mm["pte", g - 1, 3])
                    ins = None
                    for dc in range(8):
                        for fc in range(8):
                            ins = e.matmul(pq2[:, dc * 128:(dc + 1) * 128],
                                           wqb[:, fc * 1024 + dc * 128: fc * 1024 + (dc + 1) * 128],
                                           xnb[:, fc * 512 + c * 128: fc * 512 + (c + 1) * 128],
                                           start=(fc == 0), stop=(fc == 7))
                    mm["qp", g] = S.inc(ins, B.tT, 1)

                @B.block.vector
                def _(e, g=g):
                    e.wait_ge(B.tT, mm["qp", g])
                    e.wait_ge(B.dS, mm["rpt", g])
                    cosb = rpt[g % 2][:, 0:128].unsqueeze(1).broadcast_to([128, 4, 128])
                    sinb = rpt[g % 2][:, 128:256].unsqueeze(1).broadcast_to([128, 4, 128])
                    pv = pq2[:, :].rearrange("p (hh two t) -> p hh two t", two=2, t=128)
                    qv = qf[:, :].rearrange("p (hh two t) -> p hh two t", two=2, t=128)
                    h1 = hv1[:, :].rearrange("p (hh t) -> p hh t", t=128)
                    h2 = hv2[:, :].rearrange("p (hh t) -> p hh t", t=128)
                    e.tensor_tensor(h1, pv[:, :, 0, :], cosb, ALU.mult)
                    S.raw(e, e.tensor_tensor(h2, pv[:, :, 1, :], sinb, ALU.mult), B.vZ)
                    e.tensor_tensor(qv[:, :, 0, :], h1, h2, ALU.subtract)
                    e.tensor_tensor(h1, pv[:, :, 1, :], cosb, ALU.mult)
                    S.raw(e, e.tensor_tensor(h2, pv[:, :, 0, :], sinb, ALU.mult), B.vZ)
                    mm["qrope", g, 3] = S.inc(e.tensor_add(qv[:, :, 1, :], h1, h2), B.vF, 1)

                @B.block.tensor
                def _(e, st4=st4, c=c, g=g):
                    e.wait_ge(B.vQ, mm["xnb", st4])
                    if g > 0:
                        e.wait_ge(B.vF, mm["pkr", g - 1, 1])
                        e.wait_ge(B.aP, mm["sig", g - 1, 1])
                        e.wait_ge(B.vF, mm["oge", g - 1])
                    ins = None
                    for fc in range(8):
                        for hb in range(2):
                            ins = e.matmul(pk2[:, hb * 512:(hb + 1) * 512],
                                           xnb[:, fc * 512 + c * 128: fc * 512 + (c + 1) * 128],
                                           wgb[:, fc * 1024 + hb * 512: fc * 1024 + (hb + 1) * 512],
                                           start=(fc == 0), stop=(fc == 7))
                    mm["gp", g] = S.inc(ins, B.tT, 1)

                @B.block.vector
                def _(e, g=g):
                    e.wait_ge(B.tT, mm["gp", g])
                    mm["ge", g] = S.inc(e.tensor_copy(gtm[:, :], pk2[:, :]), B.vF, 1)

                for j in range(12 if PB_STAGE >= 1 else 0):
                    sl = 896 if j % 2 == 0 else 1536

                    @B.block.tensor
                    def _(e, g=g, j=j, sl=sl):
                        if j >= 2:
                            e.wait_ge(B.vF, mm["tre", g, j - 2])
                        else:
                            e.wait_ge(B.dG, mm["ldkv", g])
                            e.wait_ge(B.vF, mm["qrope", g, 3])
                            if g > 0:
                                e.wait_ge(B.vF, mm["zup", g - 1])
                        if j < 4:
                            ins = e.transpose(pq2b[0:64, sl:sl + 128],
                                              ld_[g % 2][:, 2304 + j * 64: 2304 + (j + 1) * 64],
                                              B.ident[:, :])
                        else:
                            ins = e.transpose(pq2b[:, sl:sl + 128],
                                              ld_[g % 2][:, (j - 4) * 128:(j - 3) * 128],
                                              B.ident[:, :])
                        mm["tr", g, j] = S.inc(ins, B.tT, 1)

                    @B.block.vector
                    def _(e, g=g, j=j, sl=sl):
                        e.wait_ge(B.tT, mm["tr", g, j])
                        if j < 4:
                            ins = e.tensor_copy(wvT[0:64, j * 128:(j + 1) * 128], pq2b[0:64, sl:sl + 128])
                        else:
                            ins = e.tensor_copy(kf[:, (j - 4) * 128:(j - 3) * 128], pq2b[:, sl:sl + 128])
                        mm["tre", g, j] = S.inc(ins, B.vF, 1)

                if PB_STAGE < 3:
                    continue

                @B.block.tensor
                def _(e, g=g):
                    e.wait_ge(B.vF, mm["tre", g, 11])
                    if g > 0:
                        e.wait_ge(B.vF, mm["sevac", g - 1])
                        e.wait_ge(B.vF, mm["zup", g - 1])
                    ins = None
                    for h in range(4):
                        for c2 in range(2):
                            dc = 2 * h + c2
                            ins = e.matmul(pq2[:, h * 128:(h + 1) * 128],
                                           kf[:, dc * 128:(dc + 1) * 128],
                                           qf[:, dc * 128:(dc + 1) * 128],
                                           start=(c2 == 0), stop=(c2 == 1))
                    mm["ST4", g] = S.inc(ins, B.tT, 1)

                @B.block.vector
                def _(e, g=g):
                    e.wait_ge(B.tT, mm["ST4", g])
                    mm["stme4", g] = S.inc(
                        e.tensor_tensor(stm[:, :].rearrange("p (h t) -> p h t", t=128),
                                        pq2[:, 0:512].rearrange("p (h t) -> p h t", t=128),
                                        B.tri_up[:, :].unsqueeze(1).broadcast_to([128, 4, 128]),
                                        ALU.mult), B.vF, 1)

                @B.block.tensor
                def _(e, g=g):
                    e.wait_ge(B.vF, mm["stme4", g])
                    for h in range(4):
                        for c2 in range(2):
                            dc = 2 * h + c2
                            e.matmul(pq2[:, 512 + h * 64: 512 + (h + 1) * 64],
                                     qf[:, dc * 128:(dc + 1) * 128],
                                     s1sb[:, dc * 64:(dc + 1) * 64], start=(c2 == 0), stop=False)
                        e.matmul(pq2[:, 512 + h * 64: 512 + (h + 1) * 64],
                                 stm[:, h * 128:(h + 1) * 128],
                                 ld_[g % 2][:, 2048 + h * 64: 2048 + (h + 1) * 64],
                                 start=False, stop=True)
                    for h in range(4):
                        e.matmul(pq2[:, 768 + h * 64: 768 + (h + 1) * 64], B.tri_up[:, :],
                                 ld_[g % 2][:, 2048 + h * 64: 2048 + (h + 1) * 64],
                                 start=True, stop=False)
                        e.matmul(pq2[:, 768 + h * 64: 768 + (h + 1) * 64], B.ones1[0:1, 0:128],
                                 zsbb[0:1, h * 64:(h + 1) * 64], start=False, stop=True)
                    ins = None
                    for h in range(4):
                        e.matmul(pq2[:, h * 64:(h + 1) * 64], B.tri_up[:, :],
                                 ld_[g % 2][:, 2304 + h * 64: 2304 + (h + 1) * 64],
                                 start=True, stop=False)
                        ins = e.matmul(pq2[:, h * 64:(h + 1) * 64], B.ones1[0:1, 0:128],
                                       zsbb[0:1, 256 + h * 64: 256 + (h + 1) * 64], start=False, stop=True)
                    mm["zz4", g] = S.inc(ins, B.tT, 1)

                @B.block.vector
                def _(e, g=g):
                    e.wait_ge(B.tT, mm["zz4", g])
                    e.reciprocal(rz[:, 0:256], pq2[:, 768:1024])
                    S.raw(e, e.reciprocal(rz[:, 256:512], pq2[:, 0:256]), B.vZ)
                    S.raw(e, e.tensor_tensor(lg[:, :], pq2[:, 512:768], rz[:, 0:256], ALU.mult), B.vZ)
                    mm["nmx4", g] = S.inc(
                        e.tensor_reduce(nmx[:, 0:4].unsqueeze(2),
                                        lg[:, :].rearrange("p (h m) -> p h m", m=64),
                                        AX.X, ALU.max, negate=True), B.vF, 1)

                @B.block.scalar
                def _(e, g=g):
                    e.wait_ge(B.vF, mm["nmx4", g])
                    ins = None
                    for h in range(4):
                        ins = e.activation(ev[:, h * 64:(h + 1) * 64], lg[:, h * 64:(h + 1) * 64],
                                           AF.Exp, bias=nmx[:, h:h + 1],
                                           accum_out=nmx[:, 4 + h:5 + h])
                    mm["ev4", g] = S.inc(ins, B.aP, 1)

                @B.block.vector
                def _(e, g=g):
                    e.wait_ge(B.aP, mm["ev4", g])
                    S.raw(e, e.reciprocal(nmx[:, 8:12], nmx[:, 4:8]), B.vZ)
                    S.raw(e, e.tensor_tensor(lg[:, :].rearrange("p (h m) -> p h m", m=64),
                                             ev[:, :].rearrange("p (h m) -> p h m", m=64),
                                             nmx[:, 8:12].unsqueeze(2).broadcast_to([128, 4, 64]),
                                             ALU.mult), B.vZ)
                    mm["pt4", g] = S.inc(
                        e.tensor_tensor(ptbi[:, :], lg[:, :], rz[:, 256:512], ALU.mult), B.vF, 1)

                for h in range(4):
                    sl = 896 if h % 2 == 0 else 1536

                    @B.block.tensor
                    def _(e, g=g, h=h, sl=sl):
                        if h >= 2:
                            e.wait_ge(B.vF, mm["pte", g, h - 2])
                        else:
                            e.wait_ge(B.vF, mm["pt4", g])
                        ins = e.transpose(pq2b[0:64, sl:sl + 128],
                                          ptbi[:, h * 64:(h + 1) * 64], B.ident[:, :])
                        mm["ptt", g, h] = S.inc(ins, B.tT, 1)

                    @B.block.vector
                    def _(e, g=g, h=h, sl=sl):
                        e.wait_ge(B.tT, mm["ptt", g, h])
                        mm["pte", g, h] = S.inc(
                            e.tensor_copy(ptb[0:64, h * 128:(h + 1) * 128], pq2b[0:64, sl:sl + 128]),
                            B.vF, 1)

                @B.block.tensor
                def _(e, g=g):
                    e.wait_ge(B.vF, mm["pte", g, 3])
                    ins = None
                    for h in range(4):
                        ins = e.matmul(pq2[:, h * 128:(h + 1) * 128],
                                       wvT[0:64, h * 128:(h + 1) * 128],
                                       ptb[0:64, h * 128:(h + 1) * 128], start=True, stop=True)
                    mm["at4", g] = S.inc(ins, B.tT, 1)

                @B.block.vector
                def _(e, g=g):
                    e.wait_ge(B.tT, mm["at4", g])
                    mm["atme4", g] = S.inc(
                        e.tensor_tensor(atm[:, :].rearrange("p (h t) -> p h t", t=128),
                                        pq2[:, 0:512].rearrange("p (h t) -> p h t", t=128),
                                        B.tri_up[:, :].unsqueeze(1).broadcast_to([128, 4, 128]),
                                        ALU.mult), B.vF, 1)

                @B.block.tensor
                def _(e, g=g):
                    e.wait_ge(B.vF, mm["atme4", g])
                    e.wait_ge(B.vF, mm["ge", g])
                    if g > 0:
                        e.wait_ge(B.vF, mm["sevac", g - 1])
                    for h in range(4):
                        e.matmul(pk2[:, h * 256:(h + 1) * 256], atm[:, h * 128:(h + 1) * 128],
                                 ld_[g % 2][:, 1024 + h * 256: 1024 + (h + 1) * 256],
                                 start=True, stop=False)
                        e.matmul(pk2[:, h * 256:(h + 1) * 256], ptb[0:64, h * 128:(h + 1) * 128],
                                 s2sb[0:64, h * 256:(h + 1) * 256], start=False, stop=True)
                        for c2 in range(2):
                            e.matmul(st1[:, (2 * h + c2) * 64:(2 * h + c2) * 64 + 64],
                                     ld_[g % 2][:, (2 * h + c2) * 128:(2 * h + c2 + 1) * 128],
                                     ld_[g % 2][:, 2048 + h * 64: 2048 + (h + 1) * 64],
                                     start=True, stop=True)
                        e.matmul(st2[0:64, h * 256:(h + 1) * 256],
                                 ld_[g % 2][:, 2304 + h * 64: 2304 + (h + 1) * 64],
                                 ld_[g % 2][:, 1024 + h * 256: 1024 + (h + 1) * 256],
                                 start=True, stop=True)
                    ins = None
                    for h in range(4):
                        e.matmul(pq2[0:1, h * 64:(h + 1) * 64], onesbf[:, :],
                                 ld_[g % 2][:, 2048 + h * 64: 2048 + (h + 1) * 64], start=True, stop=True)
                        ins = e.matmul(pq2[0:1, 256 + h * 64: 256 + (h + 1) * 64], onesbf[:, :],
                                       ld_[g % 2][:, 2304 + h * 64: 2304 + (h + 1) * 64],
                                       start=True, stop=True)
                    mm["scend", g] = S.inc(ins, B.tT, 1)
                mm["stu", g] = mm["scend", g]

                @B.block.vector
                def _(e, g=g):
                    e.wait_ge(B.tT, mm["scend", g])
                    S.raw(e, e.tensor_tensor(zf[g % 2][0:1, :], pq2[0:1, 0:512], zf[(g + 1) % 2][0:1, :],
                                             ALU.add), B.vZ)
                    mm["zup", g] = S.inc(e.tensor_copy(zsbb[0:1, :], zf[g % 2][0:1, :]), B.vF, 1)

                @B.block.vector
                def _(e, g=g):
                    e.wait_ge(B.tT, mm["scend", g])
                    e.tensor_tensor(s1f[g % 2][:, :], st1[:, :], s1f[(g + 1) % 2][:, :], ALU.add)
                    S.raw(e, e.tensor_tensor(s2f[g % 2][0:64, :], st2[0:64, :], s2f[(g + 1) % 2][0:64, :],
                                             ALU.add), B.vZ)
                    e.tensor_copy(s1sb[:, :], s1f[g % 2][:, :])
                    mm["sevac", g] = S.inc(e.tensor_copy(s2sb[0:64, :], s2f[g % 2][0:64, :]), B.vF, 1)

                if PB_STAGE < 4:
                    continue
                for hf in range(2):
                    @B.block.scalar
                    def _(e, g=g, hf=hf):
                        e.wait_ge(B.tT, mm["scend", g])
                        if (g, hf) != (0, 0):
                            e.wait_ge(B.vF, mm["og", g, hf - 1] if hf == 1 else mm["og", g - 1, 1])
                        mm["sq", g, hf] = S.inc(
                            e.activation(qfb[:, :], pk2[:, hf * 512:(hf + 1) * 512], AF.Square), B.aP, 1)
                        mm["sig", g, hf] = S.inc(
                            e.activation(kfb[:, :], gtm[:, hf * 512:(hf + 1) * 512], AF.Sigmoid), B.aP, 1)

                    @B.block.vector
                    def _(e, g=g, hf=hf):
                        e.wait_ge(B.aP, mm["sig", g, hf])
                        S.raw(e, e.tensor_reduce(ssq[:, 0:2].unsqueeze(2),
                                                 qfb[:, :].rearrange("p (h d) -> p h d", d=256),
                                                 AX.X, ALU.add), B.vZ)
                        mm["ssq", g, hf] = S.inc(
                            e.tensor_scalar(ssq[:, 2:4], ssq[:, 0:2], 1.0 / 256, None, ALU.mult), B.vF, 1)

                    @B.block.scalar
                    def _(e, g=g, hf=hf):
                        e.wait_ge(B.vF, mm["ssq", g, hf])
                        mm["sqr", g, hf] = S.inc(
                            e.activation(ssq[:, 4:6], ssq[:, 2:4], AF.Sqrt, bias=B.cst[:, 0:1]), B.aP, 1)

                    @B.block.vector
                    def _(e, g=g, hf=hf):
                        e.wait_ge(B.aP, mm["sqr", g, hf])
                        S.raw(e, e.reciprocal(ssq[:, 6:8], ssq[:, 4:6]), B.vZ)
                        S.raw(e, e.tensor_tensor(scf[:, :], gtm[:, hf * 512:(hf + 1) * 512], kfb[:, :],
                                                 ALU.mult), B.vZ)
                        S.raw(e, e.tensor_tensor(qfb[:, :].rearrange("p (h d) -> p h d", d=256),
                                                 pk2[:, hf * 512:(hf + 1) * 512].rearrange("p (h d) -> p h d", d=256),
                                                 ssq[:, 6:8].unsqueeze(2).broadcast_to([128, 2, 256]),
                                                 ALU.mult), B.vZ)
                        mm["pkr", g, hf] = S.inc(e.engine_nop(), B.vF, 1)
                        S.raw(e, e.tensor_tensor(kfb[:, :].rearrange("p (h d) -> p h d", d=256),
                                                 qfb[:, :].rearrange("p (h d) -> p h d", d=256),
                                                 gnr[:, :].unsqueeze(1).broadcast_to([128, 2, 256]),
                                                 ALU.mult), B.vZ)
                        mm["og", g, hf] = S.inc(
                            e.tensor_tensor(ogsb[:, hf * 512:(hf + 1) * 512], kfb[:, :], scf[:, :],
                                            ALU.mult), B.vF, 1)
                mm["og", g] = mm["og", g, 1]

                for j in range(8):
                    sl = 512 if j % 2 == 0 else 1536

                    @B.block.tensor
                    def _(e, g=g, j=j, sl=sl):
                        e.wait_ge(B.vF, mm["og", g])
                        if j >= 2:
                            e.wait_ge(B.vF, mm["ogte", g, j - 2])
                        ins = e.transpose(pk2b[:, sl:sl + 128], ogsb[:, j * 128:(j + 1) * 128],
                                          B.ident[:, :])
                        mm["ogt", g, j] = S.inc(ins, B.tT, 1)

                    @B.block.vector
                    def _(e, g=g, j=j, sl=sl):
                        e.wait_ge(B.tT, mm["ogt", g, j])
                        mm["ogte", g, j] = S.inc(
                            e.tensor_copy(ogfm[:, j * 128:(j + 1) * 128], pk2b[:, sl:sl + 128]), B.vF, 1)
                mm["oge", g] = mm["ogte", g, 7]

                @B.block.sync
                def _(e, g=g, st4=st4, c=c):
                    e.wait_ge(B.vF, mm["oge", g])
                    tok = st4 * 512 + c * 128
                    dst = B.og_d.ap().rearrange("p (dc t) -> p dc t", t=P)[:, :, tok:tok + 128]
                    S.inc(e.dma_start(out=dst, in_=ogfm[:, :].rearrange("p (dc t) -> p dc t", t=128)),
                          B.dS, 16)
                    mm["ogd", g] = S.val(B.dS)

        @B.block.sync
        def _(e):
            if PB_STAGE >= 4:
                e.wait_ge(B.dS, mm["ogd", min(15, PB_CHUNKS - 1)])

    phx.close()
    _barrier(B)


def _bc_h(ap64, nh=8):
    return ap64.unsqueeze(1).broadcast_to([128, nh, 64])


def _bc_d(ap8, d=64):
    return ap8.unsqueeze(2).broadcast_to([128, 8, d])


def _hd(ap512, d=64):
    return ap512.rearrange("p (h d) -> p h d", d=d)


def phase_ffn_subA(B, lyr):
    nc, S = B.nc, B.sched
    with contextlib.ExitStack() as ph:
        sbt = lambda n, sh, dt=F32: ph.enter_context(nc.sbuf_tensor(f"{n}_l{lyr}", sh, dt))
        wqkv = sbt("wqkv", [128, 8 * 1536], BF16)
        qsr = sbt("qsr", [128, 64])
        ksr = sbt("ksr", [128, 64])
        xw = [sbt(f"xw{i}", [128, 8 * 128], BF16) for i in range(2)]
        xpt = [sbt(f"xpt{i}", [128, 256]) for i in range(2)]
        kf = sbt("kf", [128, 512])
        bufA = sbt("bufA", [128, 512]); bufB = sbt("bufB", [128, 512]); bufC = sbt("bufC", [128, 512])
        bufD = sbt("bufD", [128, 512]); bufE = sbt("bufE", [128, 512]); bufF = sbt("bufF", [128, 512])
        qhat = [sbt(f"qhat{i}", [128, 512], BF16) for i in range(2)]
        khat = [sbt(f"khat{i}", [128, 512], BF16) for i in range(2)]
        ss = sbt("ss", [128, 48])
        scrb = sbt("scrb", [128, 512], BF16)
        vring = sbt("vring", [128, 8 * 512], BF16)
        kring = [ph.enter_context(nc.sbuf_tensor(f"kring{h}_l{lyr}", [65, 8 * 128], BF16)) for h in range(8)]
        qpp = [ph.enter_context(nc.sbuf_tensor(f"qpp{h}_l{lyr}", [65, 2 * 128], BF16)) for h in range(8)]
        sjunk = sbt("sjunk", [128, 16])
        scp = sbt("scp", [128, 2 * 640])
        mjunk = sbt("mjunk", [128, 640])
        nm8 = sbt("nm8", [128, 8])
        nm3 = sbt("nm3", [128, 4])
        nm8n = sbt("nm8n", [128, 8], BF16)
        sc04 = sbt("sc04", [128, 512])
        pt = [sbt(f"pt{i}", [128, 640], BF16) for i in range(2)]
        rcp = sbt("rcp", [128, 8])
        osb = sbt("osb", [128, 512], BF16)

        qkv_ps = ph.enter_context(nc.psum_tensor(f"qkv_ps_l{lyr}", [128, 1536], F32))
        s_ps = ph.enter_context(nc.psum_tensor(f"s_ps_l{lyr}", [128, 1024], F32))
        st_ps = ph.enter_context(nc.psum_tensor(f"st_ps_l{lyr}", [128, 1024], F32))
        o_ps = ph.enter_context(nc.psum_tensor(f"o_ps_l{lyr}", [128, 512], F32))
        trs = s_ps[:, :].bitcast(BF16)
        rs_c, nmt_c, otr_c = 640, 648, 904
        otr = st_ps[:, :].bitcast(BF16)

        sqq = sbt("sqq", [128, 512])
        sqk = sbt("sqk", [128, 512])
        smask = sbt("smask", [128, 5 * 640], BF16)
        B.smask = smask
        mm = {}
        b = {k: S.val(getattr(B, k)) for k in
             ["dS", "dG", "dA", "tT", "xwS", "vF", "vQ", "vY", "vN", "vC", "vR", "vE", "vM",
              "aP", "aK", "aN", "aE", "gK", "aSq", "vSum", "e1"]}

        @B.block.gpsimd
        def _(e):
            for ci in range(8):
                S.inc(e.dma_start(out=wqkv[:, ci * 1536:(ci + 1) * 1536],
                                  in_=B.inp["L0_ffn_wqkv"][lyr, ci * 128:(ci + 1) * 128, :]), B.dG, 16)
            S.inc(e.dma_start(out=qsr[:, :], in_=B.inp["qs_rep"][lyr]), B.dG, 16)
            S.inc(e.dma_start(out=ksr[:, :], in_=B.inp["ks_rep"][lyr]), B.dG, 16)
            for itt in range(5):
                S.inc(e.dma_start(out=smask[:, itt * 640:(itt + 1) * 640],
                                  in_=B.inp["smask"][itt]), B.dG, 16)

        @B.block.vector
        def _(e):
            for h in range(8):
                e.memset(kring[h][64:65, :], 1.0)
            for ci in range(8):
                off = (HALO - 640) - (1 if ci >= 4 else 0)
                e.tensor_copy(xw[0][:, ci * 128:(ci + 1) * 128],
                              B.stream[:, ci * NT + off: ci * NT + off + 128])
            S.inc(e.engine_nop(), B.xwS, 1)

        @B.block.sync
        def _(e):
            for t in range(NKV):
                if t >= 2:
                    e.wait_ge(B.vQ, b["vQ"] + t - 1)
                    e.wait_ge(B.gK, b["gK"] + t - 1)
                S.inc(e.dma_start(out=xpt[t % 2][:, :],
                                  in_=B.inp["xpos"][HALO - 640 + 128 * t: HALO - 512 + 128 * t, :]), B.dS, 16)

        for t in range(NKV):
            it = t - 4
            prod = True

            @B.block.tensor
            def _(e, t=t, it=it, prod=prod):
                if prod:
                    e.wait_ge(B.xwS, b["xwS"] + t + 1)
                    if t == 0:
                        e.wait_ge(B.dG, b["dG"] + 15 * 16)
                    if t >= 1:
                        e.wait_ge(B.vF, b["vF"] + t)
                        e.wait_ge(B.aSq, b["aSq"] + t)
                    ins = None
                    for ci in range(8):
                        for g in range(3):
                            ins = e.matmul(qkv_ps[:, g * 512:(g + 1) * 512],
                                           xw[t % 2][:, ci * 128:(ci + 1) * 128],
                                           wqkv[:, ci * 1536 + g * 512: ci * 1536 + (g + 1) * 512],
                                           start=(ci == 0), stop=(ci == 7))
                    mm["qkv", t] = S.inc(ins, B.tT, 1)
                    e.wait_ge(B.vQ, b["vQ"] + t + 1)
                    e.wait_ge(B.gK, b["gK"] + t + 1)
                    if t >= 1:
                        e.wait_ge(B.aK, b["aK"] + t)
                        e.wait_ge(B.vY, b["vY"] + t)
                    if it >= 1:
                        e.wait_ge(B.vM, b["vM"] + it * 8)
                    for h in range(8):
                        e.transpose(trs[0:64, h * 128:(h + 1) * 128],
                                    khat[t % 2][:, h * 64:(h + 1) * 64], B.ident[:, :])
                    ins = None
                    for h in range(8):
                        ins = e.transpose(trs[0:64, 1024 + h * 128:1024 + (h + 1) * 128],
                                          qhat[t % 2][:, h * 64:(h + 1) * 64], B.ident[:, :])
                    mm["tr", t] = S.inc(ins, B.tT, 1)
                if it >= 0:
                    e.wait_ge(B.aK, b["aK"] + min(it + 4, NKV - 1) + 1)
                    e.wait_ge(B.vY, b["vY"] + min(it + 4, NKV - 1) + 1)
                    base = it % 8
                    runs = []
                    kk0 = 0
                    while kk0 < 5:
                        sl0 = (it + kk0) % 8
                        n_ring = min(5 - kk0, 8 - sl0)
                        n_bank = (4 - kk0) if kk0 < 4 else (5 - kk0)
                        nn = min(n_ring, n_bank)
                        runs.append((kk0, nn))
                        kk0 += nn
                    for h in range(8):
                        e.wait_ge(B.vM, b["vM"] + it * 8 + h)
                        ins = None
                        for kk0, nn in runs:
                            sl0 = (it + kk0) % 8
                            ins = e.matmul(s_ps[:, kk0 * 128:(kk0 + nn) * 128],
                                           qpp[h][0:64, (it % 2) * 128:(it % 2) * 128 + 128],
                                           kring[h][0:64, sl0 * 128:(sl0 + nn) * 128],
                                           start=True, stop=True)
                        mm["S", it, h] = S.inc(ins, B.tT, 1)
                    e.wait_ge(B.vN, b["vN"] + it + 1)
                    if it >= 1:
                        e.wait_ge(B.aE, b["aE"] + it * 8)
                        e.wait_ge(B.vR, b["vR"] + it)
                        e.wait_ge(B.vE, b["vE"] + it * 4)
                    for h in range(8):
                        if h >= 1:
                            e.wait_ge(B.aN, b["aN"] + it * 8 + h)
                        ins = e.matmul(st_ps[64:65, nmt_c + (h % 2) * 128: nmt_c + (h % 2) * 128 + 128],
                                       nm8n[:, h:h + 1], B.ident[:, :], start=True, stop=True,
                                       tile_position=(0, 64))
                        mm["nmt", it, h] = S.inc(ins, B.tT, 1)
                    for h in range(8):
                        e.wait_ge(B.aN, b["aN"] + it * 8 + h + 1)
                        if h >= 1:
                            e.wait_ge(B.aE, b["aE"] + it * 8 + h)
                        ins = None
                        for kk in range(5):
                            kvt = it + kk
                            ins = e.matmul(st_ps[:, kk * 128:(kk + 1) * 128],
                                           kring[h][:, (kvt % 8) * 128:(kvt % 8) * 128 + 128],
                                           qpp[h][:, (it % 2) * 128:(it % 2) * 128 + 128],
                                           start=True, stop=True)
                        mm["ST", it, h] = S.inc(ins, B.tT, 1)
                        e.wait_ge(B.aE, b["aE"] + it * 8 + h + 1)
                        ins = None
                        for kk in range(5):
                            kvt = it + kk
                            e.matmul(o_ps[:, h * 64:(h + 1) * 64],
                                     pt[h % 2][:, kk * 128:(kk + 1) * 128],
                                     vring[:, (kvt % 8) * 512 + h * 64:(kvt % 8) * 512 + (h + 1) * 64],
                                     start=(kk == 0), stop=(kk == 4))
                            ins = e.matmul(st_ps[:, rs_c + h:rs_c + h + 1],
                                           pt[h % 2][:, kk * 128:(kk + 1) * 128],
                                           B.kvvalid_bf[:, kvt:kvt + 1],
                                           start=(kk == 0), stop=(kk == 4))
                        mm["PV", it, h] = S.inc(ins, B.tT, 1)
                    e.wait_ge(B.vR, b["vR"] + it + 1)
                    for j in range(4):
                        if j >= 1:
                            e.wait_ge(B.vE, b["vE"] + it * 4 + j)
                        ins = e.transpose(otr[:, 1808:1936],
                                          osb[:, j * 128:(j + 1) * 128], B.ident[:, :])
                        mm["otr", it, j] = S.inc(ins, B.tT, 1)

            @B.block.scalar
            def _(e, t=t, it=it, prod=prod):
                if prod:
                    e.wait_ge(B.tT, mm["qkv", t])
                    if t >= 1:
                        e.wait_ge(B.vF, b["vF"] + t)
                        e.wait_ge(B.gK, b["gK"] + t)
                    e.copy(kf[:, 0:256], qkv_ps[:, 512:768])
                    e.activation(sqq[:, :], qkv_ps[:, 0:512], AF.Square)
                    S.raw(e, e.copy(kf[:, 256:512], qkv_ps[:, 768:1024]), B.aZ)
                    S.inc(e.activation(sqk[:, :], kf[:, :], AF.Square), B.aSq, 1)
                    e.wait_ge(B.vSum, b["vSum"] + t + 1)
                    S.inc(e.activation(ss[:, 16:32], ss[:, 32:48], AF.Sqrt), B.aP, 1)
                    if t >= 8:
                        e.wait_ge(B.tT, mm["PV", t - 8, 7])
                    e.wait_ge(B.tT, mm["tr", t])
                    ins = None
                    for h in range(8):
                        ins = e.copy(kring[h][0:64, (t % 8) * 128:(t % 8) * 128 + 128],
                                     trs[0:64, h * 128:(h + 1) * 128])
                    S.inc(ins, B.aK, 1)
                if it >= 0:
                    if it >= 2:
                        e.wait_ge(B.tT, mm["ST", it - 2, 7])
                    for h in range(8):
                        e.wait_ge(B.tT, mm["nmt", it, h])
                        ins = e.copy(qpp[h][64:65, (it % 2) * 128:(it % 2) * 128 + 128],
                                     st_ps[64:65, nmt_c + (h % 2) * 128: nmt_c + (h % 2) * 128 + 128])
                        S.inc(ins, B.aN, 1)
                    for h in range(8):
                        e.wait_ge(B.vC, b["vC"] + it * 8 + h + 1)
                        if h >= 2:
                            e.wait_ge(B.tT, mm["PV", it, h - 2])
                        elif it >= 1:
                            e.wait_ge(B.tT, mm["PV", it - 1, h + 6])
                        e.activation(pt[h % 2][:, 128:512], st_ps[:, 128:512], AF.Exp)
                        e.activation(pt[h % 2][:, 0:128], sc04[:, (h % 2) * 256:(h % 2) * 256 + 128], AF.Exp)
                        ins = e.activation(pt[h % 2][:, 512:640],
                                           sc04[:, (h % 2) * 256 + 128:(h % 2) * 256 + 256], AF.Exp)
                        S.inc(ins, B.aE, 1)

            @B.block.vector
            def _(e, t=t, it=it, prod=prod):
                if prod:
                    e.wait_ge(B.tT, mm["qkv", t])
                    if t >= 8:
                        e.wait_ge(B.tT, mm["PV", t - 8, 7])
                    e.tensor_copy(vring[:, (t % 8) * 512:(t % 8) * 512 + 512], qkv_ps[:, 1024:1536])
                    e.wait_ge(B.aSq, b["aSq"] + t + 1)
                    e.tensor_reduce(ss[:, 0:8].unsqueeze(2), _hd(sqq[:, :]), AX.X, ALU.add)
                    S.raw(e, e.tensor_reduce(ss[:, 8:16].unsqueeze(2), _hd(sqk[:, :]), AX.X, ALU.add), B.vZ)
                    S.inc(e.tensor_scalar(ss[:, 32:48], ss[:, 0:16], 1e-24, None, ALU.max), B.vSum, 1)
                    e.wait_ge(B.aP, b["aP"] + t + 1)
                    S.raw(e, e.reciprocal(ss[:, 32:48], ss[:, 16:32]), B.vZ)
                    S.raw(e, e.tensor_tensor(_hd(bufA[:, :]), _hd(qkv_ps[:, 0:512]),
                                             _bc_d(ss[:, 32:40]), ALU.mult), B.vZ)
                    S.inc(e.engine_nop(), B.vF, 1)
                    if t + 1 < NKV:
                        for ci in range(8):
                            off = (HALO - 640) + 128 * (t + 1) - (1 if ci >= 4 else 0)
                            e.tensor_copy(xw[(t + 1) % 2][:, ci * 128:(ci + 1) * 128],
                                          B.stream[:, ci * NT + off: ci * NT + off + 128])
                        S.inc(e.engine_nop(), B.xwS, 1)
                    S.raw(e, e.tensor_tensor(_hd(bufB[:, :]), _hd(bufA[:, :]), _bc_h(qsr[:, :]), ALU.mult), B.vZ)
                    e.tensor_tensor(_hd(bufA[:, :]), _hd(bufB[:, :]), _bc_h(xpt[t % 2][:, 0:64]), ALU.mult)
                    e.tensor_tensor(_hd(bufC[:, :], 64)[:, :, 0:32], _hd(bufB[:, :])[:, :, 32:64],
                                    xpt[t % 2][:, 64:96].unsqueeze(1).broadcast_to([128, 8, 32]), ALU.mult)
                    S.raw(e, e.tensor_tensor(_hd(bufC[:, :], 64)[:, :, 32:64], _hd(bufB[:, :])[:, :, 0:32],
                                             xpt[t % 2][:, 96:128].unsqueeze(1).broadcast_to([128, 8, 32]),
                                             ALU.mult), B.vZ)
                    if t >= 2:
                        e.wait_ge(B.tT, mm["tr", t - 1])
                    S.inc(e.tensor_add(qhat[t % 2][:, :], bufA[:, :], bufC[:, :]), B.vQ, 1)
                    e.wait_ge(B.tT, mm["tr", t])
                    if t >= 6:
                        e.wait_ge(B.tT, mm["ST", t - 6, 7])
                    if t >= 4:
                        ins = None
                        for h in range(8):
                            ins = e.tensor_copy(qpp[h][0:64, ((t - 4) % 2) * 128:((t - 4) % 2) * 128 + 128],
                                                trs[0:64, 1024 + h * 128:1024 + (h + 1) * 128])
                        S.inc(ins, B.vY, 1)
                    else:
                        S.inc(e.engine_nop(), B.vY, 1)
                if it >= 0:
                    for h in range(8):
                        e.wait_ge(B.tT, mm["S", it, h])
                        o_ = (h % 2) * 640
                        S.inc(e.tensor_copy(scp[:, o_:o_ + 640], s_ps[:, 0:640]), B.vM, 1)
                        if it <= 4:
                            S.raw(e, e.tensor_tensor(mjunk[:, 0:640], scp[:, o_:o_ + 640],
                                                     B.smask[:, it * 640:(it + 1) * 640], ALU.add), B.vZ)
                            e.tensor_reduce(nm8[:, h:h + 1], mjunk[:, 0:640], AX.X, ALU.max)
                        else:
                            e.tensor_tensor(mjunk[:, 0:128], scp[:, o_:o_ + 128], B.addm_lo[:, :], ALU.add)
                            S.raw(e, e.tensor_tensor(mjunk[:, 512:640], scp[:, o_ + 512:o_ + 640],
                                                     B.addm_up[:, :], ALU.add), B.vZ)
                            e.tensor_reduce(nm3[:, 0:1], mjunk[:, 0:128], AX.X, ALU.max)
                            e.tensor_reduce(nm3[:, 1:2], scp[:, o_ + 128:o_ + 512], AX.X, ALU.max)
                            S.raw(e, e.tensor_reduce(nm3[:, 2:3], mjunk[:, 512:640], AX.X, ALU.max), B.vZ)
                            e.tensor_reduce(nm8[:, h:h + 1], nm3[:, 0:3], AX.X, ALU.max)
                    S.raw(e, e.engine_nop(), B.vZ)
                    S.inc(e.tensor_scalar(nm8n[:, :], nm8[:, :], -1.0, None, ALU.mult), B.vN, 1)
                    for h in range(8):
                        e.wait_ge(B.tT, mm["ST", it, h])
                        e.tensor_tensor(sc04[:, (h % 2) * 256:(h % 2) * 256 + 128],
                                        st_ps[:, 0:128], B.addm_lo[:, :], ALU.add)
                        ins = e.tensor_tensor(sc04[:, (h % 2) * 256 + 128:(h % 2) * 256 + 256],
                                              st_ps[:, 512:640], B.addm_up[:, :], ALU.add)
                        S.inc(ins, B.vC, 1)
                    e.wait_ge(B.tT, mm["PV", it, 7])
                    S.raw(e, e.tensor_scalar(sjunk[:, 0:8], st_ps[:, rs_c:rs_c + 8], 1e-30, None, ALU.max), B.vZ)
                    S.raw(e, e.reciprocal(rcp[:, :], sjunk[:, 0:8]), B.vZ)
                    S.inc(e.tensor_tensor(_hd(osb[:, :]), _hd(o_ps[:, :]), _bc_d(rcp[:, :]),
                                          ALU.mult), B.vR, 1)
                    for j in range(4):
                        e.wait_ge(B.tT, mm["otr", it, j])
                        ins = e.tensor_copy(B.o0[:, j * NQ * 128 + it * 128: j * NQ * 128 + (it + 1) * 128],
                                            otr[:, 1808:1936])
                        S.inc(ins, B.vE, 1)

            @B.block.gpsimd
            def _(e, t=t, prod=prod):
                if not prod:
                    return
                e.wait_ge(B.aP, b["aP"] + t + 1)
                e.wait_ge(B.vF, b["vF"] + t + 1)
                e.tensor_tensor(_hd(bufD[:, :]), _hd(kf[:, :]), _bc_d(ss[:, 40:48]), ALU.mult)
                S.raw(e, e.engine_nop(), B.gZ)
                S.raw(e, e.tensor_tensor(_hd(bufE[:, :]), _hd(bufD[:, :]), _bc_h(ksr[:, :]), ALU.mult), B.gZ)
                e.tensor_tensor(_hd(bufD[:, :]), _hd(bufE[:, :]), _bc_h(xpt[t % 2][:, 128:192]), ALU.mult)
                e.tensor_tensor(_hd(bufF[:, :], 64)[:, :, 0:32], _hd(bufE[:, :])[:, :, 32:64],
                                xpt[t % 2][:, 192:224].unsqueeze(1).broadcast_to([128, 8, 32]), ALU.mult)
                S.raw(e, e.tensor_tensor(_hd(bufF[:, :], 64)[:, :, 32:64], _hd(bufE[:, :])[:, :, 0:32],
                                         xpt[t % 2][:, 224:256].unsqueeze(1).broadcast_to([128, 8, 32]),
                                         ALU.mult), B.gZ)
                if t >= 2:
                    e.wait_ge(B.tT, mm["tr", t - 1])
                S.inc(e.tensor_add(khat[t % 2][:, :], bufD[:, :], bufF[:, :]), B.gK, 1)


    _barrier(B)
    _tap_sbuf(B, f"o0_{lyr}", B.o0[:, 0:4 * NQ * 128], (128, 4 * NQ * 128), BF16)


def phase_ffn_rest(B, lyr, o0stack=None):
    nc, S = B.nc, B.sched
    with contextlib.ExitStack() as ph:
        sbt = lambda n, sh, dt=F32: ph.enter_context(nc.sbuf_tensor(f"{n}_r{lyr}", sh, dt))
        scr = sbt("scr", [128, 512])
        scr2 = sbt("scr2", [128, 512])
        phA = contextlib.ExitStack()
        wo = phA.enter_context(nc.sbuf_tensor(f"wo_r{lyr}", [128, 4 * D], BF16, side="right"))
        ps1 = ph.enter_context(nc.psum_tensor(f"ps1_r{lyr}", [128, 512], F32))
        ps2 = ph.enter_context(nc.psum_tensor(f"ps2_r{lyr}", [128, 512], F32))

        b = {k: S.val(getattr(B, k)) for k in ["dG", "tT", "vF"]}
        mm = {}

        @B.block.gpsimd
        def _(e):
            for ic in range(4):
                S.inc(e.dma_start(out=wo[:, ic * D:(ic + 1) * D],
                                  in_=B.inp["L0_ffn_wo"][lyr, ic * 128:(ic + 1) * 128, :]), B.dG, 16)

        widths = [512, 512, 512, 512, 128]
        offs = [HALO - 128 + i * 512 for i in range(5)]

        @B.block.tensor
        def _(e):
            e.wait_ge(B.dG, b["dG"] + 4 * 16)
            for wi, (w, off) in enumerate(zip(widths, offs)):
                for oc in range(8):
                    g = wi * 8 + oc
                    pp = [ps1, ps2][g % 2]
                    if g >= 2:
                        e.wait_ge(B.vF, b["vF"] + 2 * (g - 1))
                    ins = None
                    o0c = off - (HALO - 128)
                    for ic in range(4):
                        ins = e.matmul(pp[:, 0:w], wo[:, ic * D + oc * 128: ic * D + (oc + 1) * 128],
                                       B.o0[:, ic * NQ * 128 + o0c: ic * NQ * 128 + o0c + w],
                                       start=(ic == 0), stop=(ic == 3))
                    mm["wo", g] = S.inc(ins, B.tT, 1)

        @B.block.vector
        def _(e):
            for wi, (w, off) in enumerate(zip(widths, offs)):
                for oc in range(8):
                    g = wi * 8 + oc
                    pp = [ps1, ps2][g % 2]
                    e.wait_ge(B.tT, mm["wo", g])
                    S.raw(e, e.tensor_add(scr[:, 0:w], pp[:, 0:w],
                                          B.stream[:, oc * NT + off: oc * NT + off + w]), B.vZ)
                    S.inc(e.tensor_copy(B.stream[:, oc * NT + off: oc * NT + off + w], scr[:, 0:w]),
                          B.vF, 2)

        _barrier(B)
        _tap_sbuf(B, f"xmid{lyr}", B.stream[:, 0:8 * NT], (128, 8 * NT))
        phA.close()
        if o0stack is not None:
            o0stack.close()

        w2 = sbt("w2", [128, 22 * D], BF16)
        w1ab = [sbt(f"w1ab{i}", [128, 8 * 128], BF16) for i in range(3)]
        w1gb = [sbt(f"w1gb{i}", [128, 8 * 128], BF16) for i in range(3)]
        h2n = sbt("h2n", [128, 8 * 512], BF16)
        sqb = [sbt(f"sqb{i}", [128, 512]) for i in range(2)]
        glu = sbt("glu", [128, 22 * 512], BF16)
        gl = sbt("gl", [128, 512])
        stv = sbt("stv", [1, 3 * 512])
        r2b = sbt("r2b", [2, 512])
        rstdb = sbt("rstdb", [1, 512])
        lnr = sbt("lnr", [2, 8 * 128])
        b1ac = sbt("b1ac", [128, 22])
        b1gc = sbt("b1gc", [128, 22])
        b2c = sbt("b2c", [128, 8])
        psa = [ph.enter_context(nc.psum_tensor(f"psa{i}_r{lyr}", [128, 512], F32)) for i in range(2)]
        psg = [ph.enter_context(nc.psum_tensor(f"psg{i}_r{lyr}", [128, 512], F32)) for i in range(2)]
        psc1 = ph.enter_context(nc.psum_tensor(f"psc1_r{lyr}", [128, 512], F32))
        psc2 = ph.enter_context(nc.psum_tensor(f"psc2_r{lyr}", [128, 512], F32))

        @B.block.gpsimd
        def _(e):
            for fc in range(22):
                S.inc(e.dma_start(out=w2[:, fc * D:(fc + 1) * D],
                                  in_=B.inp["w2_blk"][lyr, fc]), B.dG, 16)
            S.inc(e.dma_start(out=b1ac[:, :], in_=B.inp["b1a_cols"][lyr]), B.dG, 16)
            S.inc(e.dma_start(out=b1gc[:, :], in_=B.inp["b1g_cols"][lyr]), B.dG, 16)
            S.inc(e.dma_start(out=b2c[:, :], in_=B.inp["b2_cols"][lyr]), B.dG, 16)
            S.inc(e.dma_start(out=lnr[0:2, :], in_=B.inp["lnvec"][lyr]), B.dG, 16)

        @B.block.tensor
        def _(e):
            e.wait_ge(B.dG, S.val(B.dG))

        global H2N_GLU
        H2N_GLU = (h2n, glu)

        for tt in (3, 2, 1, 0):
            toff = HALO + tt * 512
            b2_ = {k: S.val(getattr(B, k)) for k in ["tT", "vF", "vQ", "vN", "vC", "aP", "aK", "dG"]}

            @B.block.scalar
            def _(e, toff=toff, b2_=b2_):
                e.wait_ge(B.vF, b2_["vF"])
                for ci in range(8):
                    off2 = toff - (1 if ci >= 4 else 0)
                    if ci >= 2:
                        e.wait_ge(B.tT, b2_["tT"] + ci - 1)
                    S.inc(e.activation(sqb[ci % 2][:, :],
                                       B.stream[:, ci * NT + off2: ci * NT + off2 + 512],
                                       AF.Square), B.aP, 1)

            @B.block.tensor
            def _(e, tt=tt, toff=toff, b2_=b2_):
                e.wait_ge(B.vF, b2_["vF"])
                for ci in range(8):
                    off2 = toff - (1 if ci >= 4 else 0)
                    e.wait_ge(B.aP, b2_["aP"] + ci + 1)
                    e.matmul(psc1[0:1, 0:512], B.onescolf[:, :],
                             B.stream[:, ci * NT + off2: ci * NT + off2 + 512],
                             start=(ci == 0), stop=(ci == 7))
                    ins = e.matmul(psc2[0:1, 0:512], B.onescolf[:, :], sqb[ci % 2][:, :],
                                   start=(ci == 0), stop=(ci == 7))
                    S.inc(ins, B.tT, 1)
                mm["lnsum", tt] = S.val(B.tT)

            @B.block.vector
            def _(e, tt=tt, b2_=b2_):
                e.wait_ge(B.tT, mm["lnsum", tt])
                e.tensor_scalar(stv[0:1, 0:512], psc1[0:1, 0:512], 1.0 / D, None, ALU.mult)
                S.raw(e, e.tensor_scalar(stv[0:1, 512:1024], psc2[0:1, 0:512], 1.0 / D, None, ALU.mult), B.vZ)
                S.raw(e, e.tensor_tensor(gl[0:1, 0:512], stv[0:1, 0:512], stv[0:1, 0:512], ALU.mult), B.vZ)
                S.inc(e.tensor_tensor(stv[0:1, 1024:1536], stv[0:1, 512:1024], gl[0:1, 0:512],
                                      ALU.subtract), B.vN, 1)

            @B.block.scalar
            def _(e, b2_=b2_):
                e.wait_ge(B.vN, b2_["vN"] + 1)
                S.inc(e.activation(stv[0:1, 512:1024], stv[0:1, 1024:1536], AF.Sqrt, bias=B.cst[0:1, 0:1]), B.aK, 1)

            @B.block.vector
            def _(e, b2_=b2_):
                e.wait_ge(B.aK, b2_["aK"] + 1)
                S.raw(e, e.reciprocal(stv[0:1, 1024:1536], stv[0:1, 512:1024]), B.vZ)
                e.tensor_copy(rstdb[0:1, :], stv[0:1, 1024:1536])
                S.raw(e, e.tensor_tensor(stv[0:1, 512:1024], stv[0:1, 0:512], stv[0:1, 1024:1536], ALU.mult), B.vZ)
                S.raw(e, e.memset(r2b[0:2, :], -1.0), B.vZ)
                S.inc(e.tensor_copy(r2b[0:1, :], stv[0:1, 512:1024]), B.vC, 1)

            @B.block.tensor
            def _(e, tt=tt, b2_=b2_):
                e.wait_ge(B.vC, b2_["vC"] + 1)
                for ci in range(8):
                    if ci >= 1:
                        e.wait_ge(B.vF, b2_["vF"] + 2 * ci)
                    e.matmul(psc1[:, :], lnr[0:1, ci * 128:(ci + 1) * 128], rstdb[0:1, :],
                             start=True, stop=True)
                    ins = e.matmul(psc2[:, :], lnr[0:2, ci * 128:(ci + 1) * 128], r2b[0:2, :],
                                   start=True, stop=True)
                    mm["c12", tt, ci] = S.inc(ins, B.tT, 1)

            @B.block.vector
            def _(e, tt=tt, toff=toff, b2_=b2_):
                for ci in range(8):
                    off2 = toff - (1 if ci >= 4 else 0)
                    e.wait_ge(B.tT, mm["c12", tt, ci])
                    S.raw(e, e.tensor_tensor(scr[:, 0:512],
                                             B.stream[:, ci * NT + off2: ci * NT + off2 + 512],
                                             psc1[:, :], ALU.mult), B.vZ)
                    S.raw(e, e.tensor_tensor(h2n[:, ci * 512:(ci + 1) * 512], scr[:, 0:512],
                                             psc2[:, :], ALU.subtract), B.vZ)
                    S.inc(e.engine_nop(), B.vF, 2)

            @B.block.tensor
            def _(e, tt=tt, b2_=b2_):
                for ocp in range(22):
                    e.wait_ge(B.dG, b2_["dG"] + (ocp + 1) * 32)
                    if ocp >= 2:
                        e.wait_ge(B.vC, b2_["vC"] + 1 + ocp - 1)
                        e.wait_ge(B.aK, b2_["aK"] + 1 + ocp - 1)
                    else:
                        e.wait_ge(B.vC, b2_["vC"])
                        e.wait_ge(B.aK, b2_["aK"])
                    if ocp == 0:
                        e.wait_ge(B.vF, b2_["vF"] + 16)
                    ins = None
                    for ci in range(8):
                        e.matmul(psa[ocp % 2][:, :], w1ab[ocp % 3][:, ci * 128:(ci + 1) * 128],
                                 h2n[:, ci * 512:(ci + 1) * 512], start=(ci == 0), stop=(ci == 7))
                        ins = e.matmul(psg[ocp % 2][:, :], w1gb[ocp % 3][:, ci * 128:(ci + 1) * 128],
                                       h2n[:, ci * 512:(ci + 1) * 512], start=(ci == 0), stop=(ci == 7))
                    mm["w1", tt, ocp] = S.inc(ins, B.tT, 1)

            @B.block.gpsimd
            def _(e, tt=tt, b2_=b2_):
                for ocp in range(22):
                    if ocp >= 3:
                        e.wait_ge(B.tT, mm["w1", tt, ocp - 3])
                    else:
                        e.wait_ge(B.tT, b2_["tT"])
                    for half, dst in ((0, w1ab), (1, w1gb)):
                        base = ((lyr * 2 + half) * 22 + ocp) * 1024 * 128
                        S.inc(e.dma_start(out=dst[ocp % 3][:, :],
                                          in_=AP(B.inp["w1_blk"], base,
                                                 [[128, 128], [16384, 8], [1, 128]])),
                              B.dG, 16)

            @B.block.scalar
            def _(e, tt=tt, b2_=b2_):
                for ocp in range(22):
                    e.wait_ge(B.tT, mm["w1", tt, ocp])
                    if ocp >= 1:
                        e.wait_ge(B.vC, b2_["vC"] + 1 + ocp)
                    S.inc(e.activation(gl[:, :], psg[ocp % 2][:, :], AF.Gelu, bias=b1gc[:, ocp:ocp + 1]), B.aK, 1)

            @B.block.vector
            def _(e, tt=tt, b2_=b2_):
                for ocp in range(22):
                    e.wait_ge(B.aK, b2_["aK"] + 1 + ocp + 1)
                    ins = e.scalar_tensor_tensor(glu[:, ocp * 512:(ocp + 1) * 512], psa[ocp % 2][:, :],
                                                 b1ac[:, ocp:ocp + 1], gl[:, :], ALU.add, ALU.mult)
                    S.inc(ins, B.vC, 1)

            @B.block.tensor
            def _(e, tt=tt, b2_=b2_):
                for oc in range(8):
                    pp = [ps1, ps2][oc % 2]
                    if oc >= 2:
                        e.wait_ge(B.vF, b2_["vF"] + 16 + 2 * (oc - 1))
                    e.wait_ge(B.vC, b2_["vC"] + 1 + 22)
                    ins = None
                    for fc in range(22):
                        ins = e.matmul(pp[:, :], w2[:, fc * D + oc * 128: fc * D + (oc + 1) * 128],
                                       glu[:, fc * 512:(fc + 1) * 512], start=(fc == 0), stop=(fc == 21))
                    mm["w2", tt, oc] = S.inc(ins, B.tT, 1)

            @B.block.vector
            def _(e, tt=tt, toff=toff, b2_=b2_):
                for oc in range(8):
                    pp = [ps1, ps2][oc % 2]
                    e.wait_ge(B.tT, mm["w2", tt, oc])
                    S.raw(e, e.scalar_tensor_tensor(scr2[:, 0:512], pp[:, :], b2c[:, oc:oc + 1],
                                                    B.stream[:, oc * NT + toff: oc * NT + toff + 512],
                                                    ALU.add, ALU.add), B.vZ)
                    S.inc(e.tensor_copy(B.stream[:, oc * NT + toff: oc * NT + toff + 512],
                                        scr2[:, 0:512]), B.vF, 2)

    _barrier(B)
    _tap_sbuf(B, f"h2n{lyr}", H2N_GLU[0][:, :], (128, 8 * 512), BF16)
    _tap_sbuf(B, f"glu{lyr}", H2N_GLU[1][:, :], (128, 22 * 512), BF16)
    _tap_sbuf(B, f"x{lyr + 1}", B.stream[:, 0:8 * NT], (128, 8 * NT))

    if lyr == 0:
        phase_halo_exchange(B)
        _barrier(B)


def phase_halo_exchange(B):
    nc, S = B.nc, B.sched
    with contextlib.ExitStack() as ph:
        hsc = ph.enter_context(nc.sbuf_tensor("hsc", [128, 8 * HALO], F32))
        pid = ph.enter_context(nc.sbuf_tensor("pid_sb", [1, 2], mybir.dt.uint32))

        @B.block.sync
        def _(e):
            src = B.stream[:, :].rearrange("p (c n) -> p c n", n=NT)[:, :, NT - HALO:NT]
            S.inc(e.dma_start(out=B.ag2_in[:, :], in_=src), B.dS, 16)

        @B.block.gpsimd
        def _(e):
            S.inc(e.dma_start(out=pid[0:1, 0:1], in_=B.nc.partition_id_tensor[0:1, 0:1]), B.dG, 16)
            e.wait_ge(B.dS, S.val(B.dS))
            S.inc(e.collective_compute("AllGather", ALU.bypass, replica_groups=RG,
                                       ins=[B.ag2_in.ap()], outs=[B.ag2_out.ap()]), B.cc, 1)
            e.wait_ge(B.cc, S.val(B.cc))
            e.wait_ge(B.dG, S.val(B.dG))
            with e.register("hx") as gr:
                e.reg_load(gr, pid[0:1, 0:1])
                e.reg_mod(gr, gr, 4)
                e.reg_add(gr, gr, 3)
                e.reg_mod(gr, gr, 4)
                e.reg_mul(gr, gr, 128 * 8 * HALO)
                S.inc(e.dma_start(out=hsc[:, :],
                                  in_=AP(B.ag2_out, gr, [[8 * HALO, 128], [1, 8 * HALO]])), B.dG, 16)

        @B.block.vector
        def _(e):
            e.wait_ge(B.dG, S.val(B.dG))
            for ci in range(8):
                S.inc(e.tensor_scalar(B.stream[:, ci * NT: ci * NT + HALO],
                                      hsc[:, ci * HALO:(ci + 1) * HALO],
                                      B.hgate[:, 0:1], None, ALU.mult), B.vF, 1)


def phase_final(B):
    nc, S = B.nc, B.sched
    with contextlib.ExitStack() as ph:
        sbt = lambda n, sh, dt=F32: ph.enter_context(nc.sbuf_tensor(f"{n}_fin", sh, dt))
        wob = sbt("wob", [128, 8 * 1024], BF16)
        ogs = [sbt(f"ogs{i}", [128, 8 * 512], BF16) for i in range(2)]
        fs = sbt("fs", [128, 8 * 512])
        x0t = [sbt(f"x0t{i}", [128, 1024]) for i in range(2)]
        ot = [sbt(f"ot{i}", [128, 1024]) for i in range(2)]
        pa = [ph.enter_context(nc.psum_tensor(f"pa{i}_fin", [128, 512], F32)) for i in range(2)]
        tp = [ph.enter_context(nc.psum_tensor(f"tp{i}_fin", [128, 512], F32)) for i in range(2)]
        mm = {}
        b = {k: S.val(getattr(B, k)) for k in ["dS", "dG", "dA", "tT", "vF", "vQ", "aP"]}

        @B.block.gpsimd
        def _(e):
            for dc in range(8):
                S.inc(e.dma_start(out=wob[:, dc * 1024:(dc + 1) * 1024],
                                  in_=B.inp["L0_attn_Wo"][dc * 128:(dc + 1) * 128, :]), B.dG, 16)
            mm["wo"] = S.val(B.dG)

        for t4 in range(4):
            @B.block.scalar
            def _(e, t4=t4):
                if t4 >= 2:
                    e.wait_ge(B.tT, mm["att", t4 - 2, 7])
                src = B.og_d.ap().rearrange("p (dc t) -> p dc t", t=P)[:, :, t4 * 512:(t4 + 1) * 512]
                S.inc(e.dma_start(out=ogs[t4 % 2][:, :].rearrange("p (dc t) -> p dc t", t=512),
                                  in_=src), B.dA, 16)
                mm["ogl", t4] = S.val(B.dA)

            for oc in range(8):
                @B.block.tensor
                def _(e, t4=t4, oc=oc):
                    if oc == 0:
                        e.wait_ge(B.dA, mm["ogl", t4])
                    if t4 == 0 and oc == 0:
                        e.wait_ge(B.dG, mm["wo"])
                    pp = pa[oc % 2]
                    if oc >= 2:
                        e.wait_ge(B.vF, mm["fse", t4, oc - 2])
                    elif t4 > 0:
                        e.wait_ge(B.vF, mm["fse", t4 - 1, 6 + oc])
                    ins = None
                    for dc in range(8):
                        ins = e.matmul(pp[:, :], wob[:, dc * 1024 + oc * 128: dc * 1024 + (oc + 1) * 128],
                                       ogs[t4 % 2][:, dc * 512:(dc + 1) * 512],
                                       start=(dc == 0), stop=(dc == 7))
                    mm["att", t4, oc] = S.inc(ins, B.tT, 1)

                @B.block.vector
                def _(e, t4=t4, oc=oc):
                    e.wait_ge(B.tT, mm["att", t4, oc])
                    mm["fse", t4, oc] = S.inc(
                        e.tensor_add(fs[:, oc * 512:(oc + 1) * 512], pa[oc % 2][:, :],
                                     B.stream[:, oc * NT + HALO + t4 * 512: oc * NT + HALO + t4 * 512 + 512]),
                        B.vF, 1)

            for tc in range(4):
                gt = t4 * 4 + tc

                @B.block.sync
                def _(e, t4=t4, tc=tc, gt=gt):
                    if gt >= 2:
                        e.wait_ge(B.vQ, mm["oadd", gt - 2])
                    r0 = HALO + t4 * 512 + tc * 128
                    S.inc(e.dma_start(out=x0t[gt % 2][:, :], in_=B.inp["x0"][r0:r0 + 128, :]), B.dS, 16)
                    mm["x0l", gt] = S.val(B.dS)

                for oc in range(8):
                    gg = gt * 8 + oc

                    @B.block.tensor
                    def _(e, t4=t4, tc=tc, gt=gt, oc=oc, gg=gg):
                        e.wait_ge(B.vF, mm["fse", t4, oc])
                        if gg >= 2:
                            e.wait_ge(B.vQ, mm["oev", gg - 2])
                        ins = e.transpose(tp[gg % 2][:, 0:128],
                                          fs[:, oc * 512 + tc * 128: oc * 512 + (tc + 1) * 128],
                                          B.identf[:, :])
                        mm["ftr", gt, oc] = S.inc(ins, B.tT, 1)

                    @B.block.vector
                    def _(e, t4=t4, tc=tc, gt=gt, oc=oc, gg=gg):
                        if oc == 0:
                            e.wait_ge(B.dS, mm["x0l", gt])
                            if gt >= 2:
                                e.wait_ge(B.dS, mm["osave", gt - 2])
                        e.wait_ge(B.tT, mm["ftr", gt, oc])
                        mm["oev", gg] = S.inc(
                            e.tensor_add(ot[gt % 2][:, oc * 128:(oc + 1) * 128], tp[gg % 2][:, 0:128],
                                         x0t[gt % 2][:, oc * 128:(oc + 1) * 128]), B.vQ, 1)
                        if oc == 7:
                            mm["oadd", gt] = S.val(B.vQ)

                @B.block.sync
                def _(e, gt=gt):
                    e.wait_ge(B.vQ, mm["oadd", gt])
                    S.inc(e.dma_start(out=B.out[gt * 128:(gt + 1) * 128, :], in_=ot[gt % 2][:, :]),
                          B.dS, 16)
                    mm["osave", gt] = S.val(B.dS)

        @B.block.sync
        def _(e):
            e.wait_ge(B.dS, mm["osave", 15])



def kernel(**inputs):
    consts = _make_consts()
    in_maps = [_stage_core(c, inputs, consts) for c in range(8)]
    B = build_program(DEBUG_TAPS)
    res = run_bass_kernel_spmd(B.nc, in_maps, core_ids=list(range(8)))
    out = np.zeros((2, 8192, 1024), np.float32)
    for c in range(8):
        b, p = c // 4, c % 4
        out[b, p * P:(p + 1) * P] = res.results[c]["out"]
    if DEBUG_TAPS:
        kernel.last_taps = [{k.replace("tap_", ""): v for k, v in r.items()
                             if k.startswith("tap_") or k == "og_d"} for r in res.results]
    return out
